# version 20
# baseline (speedup 1.0000x reference)
import numpy as np
from contextlib import ExitStack

import concourse.bass as bass
import concourse.mybir as mybir
from concourse.bass_utils import run_bass_kernel_spmd

F32 = mybir.dt.float32
BF16 = mybir.dt.bfloat16
I32 = mybir.dt.int32
AF = mybir.ActivationFunctionType
ALU = mybir.AluOpType
AX = mybir.AxisListType.X

NCORES = 8
D = 1024
LSEQ = 8192
NTOK = 16384
TPC = 2048
CPC = 128
FILT = 64
NFFT = 16384
NEXP = 8
DEBUG_DUMPS = False

ENGINES = ("tensor", "vector", "scalar", "gpsimd", "sync")


class W:
    def __init__(self, nc, ctx):
        self.nc = nc
        self.tape = {e: [] for e in ENGINES}
        self.sems = {}
        self.cnt = {}
        self.ctx = ctx

    def sem(self, name):
        if name not in self.sems:
            self.sems[name] = self.ctx.enter_context(self.nc.semaphore(name))
            self.cnt[name] = 0
        return name

    def op(self, eng, fn, waits=(), inc=None, amt=1):
        if inc is not None:
            self.sem(inc)
            self.cnt[inc] += amt
        self.tape[eng].append((tuple(waits), fn, inc, amt))
        return self.cnt[inc] if inc is not None else None

    def wait(self, eng, sem, val):
        self.tape[eng].append((((sem, val),), None, None, 0))

    def emit(self, block):
        for eng in ENGINES:
            deco = getattr(block, eng)
            tape = self.tape[eng]
            sems = self.sems

            def make(tape_):
                def body(b):
                    for waits, fn, inc, amt in tape_:
                        for s, v in waits:
                            if v > 0:
                                b.wait_ge(sems[s], v)
                        if fn is not None:
                            inst = fn(b)
                            if inc is not None:
                                inst.then_inc(sems[inc], amt)
                return body

            deco(make(tape))


def build_program():
    nc = bass.Bass("TRN2", target_bir_lowering=False)

    def din(name, shape, dtype=F32):
        return nc.dram_tensor(name, shape, dtype, kind="ExternalInput").ap()

    x_sl = din("x_sl", [TPC, D])
    inw_sl = din("inw_sl", [D, 384])
    inb_sl = din("inb_sl", [128, 3])
    shw_sl = din("shw_sl", [128, 9])
    shb_sl = din("shb_sl", [128, 3])
    w1_i = din("w1", [4, FILT])
    b1_i = din("b1", [FILT, 1])
    w2_i = din("w2", [FILT, FILT])
    b2_i = din("b2", [FILT, 1])
    w3_i = din("w3", [FILT, FILT])
    b3_i = din("b3", [FILT, 1])
    freq_i = din("freq_c", [FILT, 1])
    wo_i = din("wo_sl", [FILT, 128])
    fb_i = din("fb_sl", [128, 1])
    outw_i = din("outw_sl", [128, D])
    outb_i = din("outb", [128, 8])
    wr_i = din("wr_t", [D, 8])
    rb_i = din("rb", [8, 1])
    sel_i = din("sel8", [8, 1024])
    wfq_i = din("wfq", [9, 4, 128, 12288])
    chan_i = din("chan_row", [1, 128])
    id_i = din("ident32", [128, 128])

    out_fm = nc.dram_tensor("out_fm", [D, TPC], F32, kind="ExternalOutput").ap()

    xn_bnc = nc.dram_tensor("xn_bnc", [D, TPC], BF16).ap()
    g_xn = nc.dram_tensor("g_xn", [NCORES * D, TPC], BF16).ap()
    vx_loc = nc.dram_tensor("vx_loc", [CPC, NTOK], F32).ap()
    x0_loc = nc.dram_tensor("x0_loc", [CPC, NTOK], F32).ap()
    y_loc = nc.dram_tensor("y_loc", [CPC, NTOK], F32).ap()
    k_loc = nc.dram_tensor("k_loc", [CPC, LSEQ], F32).ap()
    p_buf = nc.dram_tensor("p_buf", [NCORES * D, TPC], F32).ap()
    attn_rs = nc.dram_tensor("attn_rs", [D, TPC], F32).ap()

    ctx = ExitStack()
    E = ctx.enter_context
    w = W(nc, ctx)

    xn_fm = E(nc.sbuf_tensor("xn_fm", [128, 8 * TPC], BF16))
    acc = E(nc.sbuf_tensor("acc", [128, 8 * TPC], F32))
    wq = [E(nc.sbuf_tensor(f"wq{i}", [128, 3 * 4096], BF16)) for i in range(2)]
    consts = E(nc.sbuf_tensor("consts", [128, 24], F32))
    ones_f32 = E(nc.sbuf_tensor("ones_f32", [1, 128], F32))
    ident_f = E(nc.sbuf_tensor("ident_f", [128, 128], F32))
    inb_sb = E(nc.sbuf_tensor("inb_sb", [128, 3], F32))
    shw_sb = E(nc.sbuf_tensor("shw_sb", [128, 9], F32))
    shb_sb = E(nc.sbuf_tensor("shb_sb", [128, 3], F32))
    fb_sb = E(nc.sbuf_tensor("fb_sb", [128, 1], F32))
    outb_sb = E(nc.sbuf_tensor("outb_sb", [128, 8], F32))
    wr_sb = E(nc.sbuf_tensor("wr_sb", [128, 64], F32))
    rb_sb = E(nc.sbuf_tensor("rb_sb", [8, 1], F32))
    sel_sb = E(nc.sbuf_tensor("sel_sb", [8, 1024], BF16))
    w1_sb = E(nc.sbuf_tensor("w1_sb", [4, FILT], F32))
    w2_sb = E(nc.sbuf_tensor("w2_sb", [FILT, FILT], F32))
    w3_sb = E(nc.sbuf_tensor("w3_sb", [FILT, FILT], F32))
    b1_sb = E(nc.sbuf_tensor("b1_sb", [FILT, 1], F32))
    b2_sb = E(nc.sbuf_tensor("b2_sb", [FILT, 1], F32))
    b3_sb = E(nc.sbuf_tensor("b3_sb", [FILT, 1], F32))
    freq_sb = E(nc.sbuf_tensor("freq_sb", [FILT, 1], F32))
    fbias_sb = [E(nc.sbuf_tensor(f"fbias{i}", [FILT, 1], F32)) for i in range(3)]
    wo_sb = E(nc.sbuf_tensor("wo_sb", [FILT, 128], F32))
    chan_sb = E(nc.sbuf_tensor("chan_sb", [1, 128], F32))
    cw_t = E(nc.sbuf_tensor("cw_t", [8, TPC], BF16))
    f1re = E(nc.sbuf_tensor("f1re", [128, 128], BF16))
    f1im = E(nc.sbuf_tensor("f1im", [128, 128], BF16))
    nf1im = E(nc.sbuf_tensor("nf1im", [128, 128], BF16))
    stk_re = E(nc.sbuf_tensor("stk_re", [128, 128], BF16))
    stk_im = E(nc.sbuf_tensor("stk_im", [128, 128], BF16))
    fi_re = E(nc.sbuf_tensor("fi_re", [128, 64], BF16))
    fi_ip = E(nc.sbuf_tensor("fi_ip", [128, 64], BF16))
    fi_in = E(nc.sbuf_tensor("fi_in", [128, 64], BF16))
    t_re = E(nc.sbuf_tensor("t_re", [128, 128], F32))
    t_im = E(nc.sbuf_tensor("t_im", [128, 128], F32))
    tw_s1 = E(nc.sbuf_tensor("tw_s1", [128, 2 * 512], F32))
    cwb_sb = E(nc.sbuf_tensor("cwb_sb", [128, 2048], F32))
    gs_sb = E(nc.sbuf_tensor("gs_sb", [128, 512], F32))
    us_sb = E(nc.sbuf_tensor("us_sb", [128, 512], F32))
    d_sb = [E(nc.sbuf_tensor(f"d_sb{i}", [128, 4 * 512], BF16)) for i in range(2)]

    bank = [E(nc.psum_tensor(f"bank{i}", [128, 512], F32)) for i in range(8)]
    A, Bk, C, Dk, Ebk, Fbk, G, H = bank

    CONSTS = {}

    def cdef(name, val):
        CONSTS[name] = (len(CONSTS), float(val))

    TWO_PI = 6.283185307179586
    C_HI = float(np.float32(TWO_PI))
    C_LO = float(np.float64(TWO_PI) - np.float64(np.float32(TWO_PI)))
    cdef("inv1024", 1.0 / 1024.0)
    cdef("eps", 1e-5)
    cdef("one", 1.0)
    cdef("neg1", -1.0)
    cdef("half", 0.5)
    cdef("neghalf", -0.5)
    cdef("inv2pi", 1.0 / TWO_PI)
    cdef("nchi", -C_HI)
    cdef("nclo", -C_LO)
    cdef("pihalf", float(np.pi / 2))
    cdef("inv128", 1.0 / 128.0)
    cdef("invN", 1.0 / NFFT)
    cdef("zero", 0.0)
    cdef("invL1", 1.0 / (LSEQ - 1.0))
    MIND = float(np.log(0.01) / 1.5)
    MAXD = float(np.log(0.01) / 0.3)
    cdef("mind", MIND)
    cdef("dslope", (MAXD - MIND) / (D - 1.0))
    POSC = TWO_PI * 1e-4 / LSEQ

    def cc(name, p=128):
        j = CONSTS[name][0]
        return consts[0:p, j:j + 1]

    for s in ("ld", "cst", "st", "wld", "pst", "ag", "rs", "pe", "se", "ve", "po"):
        w.sem(s)

    def dma_sync(dst, src, waits=()):
        return w.op("sync", lambda b, d=dst, s=src: b.dma_start(out=d, in_=s),
                    waits=waits, inc="ld", amt=16)

    def dma_st(dst, src, waits=()):
        return w.op("sync", lambda b, d=dst, s=src: b.dma_start(out=d, in_=s),
                    waits=waits, inc="st", amt=16)

    def dma_cast(dst, src, waits=()):
        return w.op("gpsimd", lambda b, d=dst, s=src: b.dma_start(out=d, in_=s),
                    waits=waits, inc="cst", amt=16)

    def dma_wt(dst, src, waits=()):
        return w.op("gpsimd", lambda b, d=dst, s=src: b.dma_start(out=d, in_=s),
                    waits=waits, inc="wld", amt=16)

    for name, (idx, val) in CONSTS.items():
        w.op("gpsimd", lambda b, i=idx, v=val: b.memset(consts[:, i:i + 1], v), inc="po")
    w.op("gpsimd", lambda b: b.memset(ones_f32[:, :], 1.0), inc="po")
    po_setup = w.cnt["po"]

    for dst, src in [
        (inb_sb[:, :], inb_sl), (shw_sb[:, :], shw_sl), (shb_sb[:, :], shb_sl),
        (fb_sb[:, :], fb_i), (outb_sb[:, :], outb_i), (rb_sb[:, :], rb_i),
        (w1_sb[:, :], w1_i), (w2_sb[:, :], w2_i), (w3_sb[:, :], w3_i),
        (b1_sb[:, :], b1_i), (b2_sb[:, :], b2_i), (b3_sb[:, :], b3_i),
        (freq_sb[:, :], freq_i), (wo_sb[:, :], wo_i), (chan_sb[:, :], chan_i),
        (ident_f[:, :], id_i),
        (wr_sb[:, :].rearrange("p (hc e) -> p hc e", e=8),
         wr_i.rearrange("(hc p) e -> p hc e", p=128)),
    ]:
        dma_sync(dst, src)
    ld_small = w.cnt["ld"]
    dma_cast(sel_sb[:, :], sel_i)
    cst_small = w.cnt["cst"]
    pref = {}
    for qq in range(2):
        dma_wt(wq[qq][:, :], wfq_i[0, qq])
        pref[(0, qq)] = w.cnt["wld"]

    ctxF = ExitStack()
    EF = ctxF.enter_context
    gscr = EF(nc.sbuf_tensor("gscr", [128, 128], F32))
    gscr2 = EF(nc.sbuf_tensor("gscr2", [128, 128], F32))
    gscri = EF(nc.sbuf_tensor("gscri", [128, 128], I32))
    irow = EF(nc.sbuf_tensor("irow", [1, 512], F32))
    zt = EF(nc.sbuf_tensor("zt", [4, 512], F32))
    zr = [EF(nc.sbuf_tensor("zr0", [1, 512], F32))]
    h_a = EF(nc.sbuf_tensor("h_a", [FILT, 512], F32))
    h_b = EF(nc.sbuf_tensor("h_b", [FILT, 512], F32))
    rrs = EF(nc.sbuf_tensor("rrs", [FILT, 4 * 512], F32))
    rri = EF(nc.sbuf_tensor("rri", [1, 512], I32))
    kt_sb = EF(nc.sbuf_tensor("kt_sb", [128, 512], F32))
    drow = EF(nc.sbuf_tensor("drow", [1, 128], F32))

    w.op("gpsimd", lambda b: b.iota(gscri[0:1, 0:128], [[1, 128]], base=0, channel_multiplier=0), inc="po")
    w.op("vector", lambda b: b.tensor_copy(gscr[0:1, 0:128], gscri[0:1, 0:128]),
         waits=[("po", w.cnt["po"])], inc="ve")
    ve_iota = w.cnt["ve"]
    pe_jk = w.op("tensor", lambda b: b.matmul(A[:, 0:128], gscr[0:1, 0:128],
                                              gscr[0:1, 0:128], start=True, stop=True),
                 waits=[("ve", ve_iota), ("ld", ld_small)], inc="pe")

    def gen_trig(div_const, cos_dst, sin_dst, extra=None, ncols=128, neg_tile=None):
        w.op("vector", lambda b: b.tensor_single_scalar(gscr[:, 0:ncols], A[:, 0:ncols],
                                                        cc(div_const), ALU.mult),
             waits=[("pe", pe_jk), ("se", w.cnt["se"])], inc="ve")
        w.op("vector", lambda b: b.tensor_copy(gscri[:, 0:ncols], gscr[:, 0:ncols]),
             waits=[("ve", w.cnt["ve"])], inc="ve")
        w.op("vector", lambda b: b.tensor_copy(gscr2[:, 0:ncols], gscri[:, 0:ncols]),
             waits=[("ve", w.cnt["ve"])], inc="ve")
        w.op("vector", lambda b: b.tensor_sub(gscr2[:, 0:ncols], gscr[:, 0:ncols],
                                              gscr2[:, 0:ncols]),
             waits=[("ve", w.cnt["ve"])], inc="ve")
        w.op("vector", lambda b: b.tensor_single_scalar(gscr[:, 0:ncols], gscr2[:, 0:ncols],
                                                        cc("half"), ALU.is_gt),
             waits=[("ve", w.cnt["ve"])], inc="ve")
        w.op("vector", lambda b: b.tensor_sub(gscr2[:, 0:ncols], gscr2[:, 0:ncols],
                                              gscr[:, 0:ncols]),
             waits=[("ve", w.cnt["ve"])], inc="ve")
        w.op("vector", lambda b: b.tensor_single_scalar(gscr[:, 0:ncols], gscr2[:, 0:ncols],
                                                        cc("neghalf"), ALU.is_lt),
             waits=[("ve", w.cnt["ve"])], inc="ve")
        w.op("vector", lambda b: b.tensor_add(gscr2[:, 0:ncols], gscr2[:, 0:ncols],
                                              gscr[:, 0:ncols]),
             waits=[("ve", w.cnt["ve"])], inc="ve")
        vf = w.cnt["ve"]
        sin_tmp = gscr[:, 0:ncols]
        w.op("scalar", lambda b: b.activation(sin_tmp, gscr2[:, 0:ncols], AF.Sin,
                                              bias=cc("zero"), scale=-TWO_PI),
             waits=[("ve", vf)], inc="se")
        se_s = w.cnt["se"]
        if extra is None:
            w.op("vector", lambda b: b.tensor_copy(sin_dst, sin_tmp),
                 waits=[("se", se_s)], inc="ve")
        else:
            w.op("vector", lambda b: b.tensor_single_scalar(sin_dst, sin_tmp,
                                                            cc(extra), ALU.mult),
                 waits=[("se", se_s)], inc="ve")
        if neg_tile is not None:
            w.op("vector", lambda b: b.tensor_single_scalar(neg_tile, sin_dst,
                                                            cc("neg1"), ALU.mult),
                 waits=[("ve", w.cnt["ve"])], inc="ve")
        ve_sin = w.cnt["ve"]
        w.op("vector", lambda b: b.tensor_single_scalar(gscr[:, 0:ncols], gscr2[:, 0:ncols],
                                                        cc("neg1"), ALU.mult),
             waits=[("ve", ve_sin)], inc="ve")
        w.op("vector", lambda b: b.tensor_max(gscr[:, 0:ncols], gscr[:, 0:ncols],
                                              gscr2[:, 0:ncols]),
             waits=[("ve", w.cnt["ve"])], inc="ve")
        va = w.cnt["ve"]
        cos_tmp = gscr2[:, 0:ncols]
        w.op("scalar", lambda b: b.activation(cos_tmp, gscr[:, 0:ncols], AF.Sin,
                                              bias=cc("pihalf"), scale=-TWO_PI),
             waits=[("ve", va)], inc="se")
        se_c = w.cnt["se"]
        if extra is None:
            w.op("vector", lambda b: b.tensor_copy(cos_dst, cos_tmp),
                 waits=[("se", se_c)], inc="ve")
        else:
            w.op("vector", lambda b: b.tensor_single_scalar(cos_dst, cos_tmp,
                                                            cc(extra), ALU.mult),
                 waits=[("se", se_c)], inc="ve")

    gen_trig("inv128", f1re[:, :], f1im[:, :], neg_tile=nf1im[:, :])
    gen_trig("invN", t_re[:, :], t_im[:, :])
    gen_trig("inv128", fi_re[:, :], fi_ip[:, :], extra="invN", ncols=64,
             neg_tile=fi_in[:, :])
    ve_t4 = w.cnt["ve"]
    dma_st(stk_re[0:64, :], f1re[0:64, :], waits=[("ve", ve_t4)])
    dma_st(stk_re[64:128, :], nf1im[0:64, :])
    dma_st(stk_im[0:64, :], f1im[0:64, :])
    dma_st(stk_im[64:128, :], f1re[0:64, :])
    st_stk = w.cnt["st"]

    w.op("vector", lambda b: b.tensor_scalar(drow[0:1, :], chan_sb[0:1, :],
                                             cc("dslope", 1), cc("mind", 1),
                                             ALU.mult, ALU.add),
         waits=[("ld", ld_small), ("ve", ve_t4)], inc="ve")
    for bsb, fdst in ((b1_sb, fbias_sb[0]), (b2_sb, fbias_sb[1]), (b3_sb, fbias_sb[2])):
        w.op("vector", lambda b, s=bsb, d_=fdst: b.tensor_mul(d_[0:FILT, :],
                                                              freq_sb[0:FILT, :],
                                                              s[0:FILT, :]),
             waits=[("ld", ld_small)], inc="ve")
    ve_fb = w.cnt["ve"]

    def rrsin(dst, src, scale_ap, bias_ap, waits):
        s0 = rrs[0:FILT, 0:512]
        s1 = rrs[0:FILT, 512:1024]
        s2 = rrs[0:FILT, 1024:1536]
        s3 = rrs[0:FILT, 1536:2048]
        i0 = rrs[0:FILT, :].bitcast(I32)[0:FILT, 1536:2048]
        w.op("vector", lambda b: b.tensor_scalar(s0, src, scale_ap, bias_ap,
                                                 ALU.mult, ALU.add), waits=waits, inc="ve")
        w.op("vector", lambda b: b.tensor_single_scalar(s1, s0, cc("inv2pi", FILT),
                                                        ALU.mult),
             waits=[("ve", w.cnt["ve"])], inc="ve")
        w.op("vector", lambda b: b.tensor_copy(i0, s1), waits=[("ve", w.cnt["ve"])], inc="ve")
        w.op("vector", lambda b: b.tensor_copy(s2, i0), waits=[("ve", w.cnt["ve"])], inc="ve")
        w.op("vector", lambda b: b.tensor_sub(s3, s1, s2), waits=[("ve", w.cnt["ve"])],
             inc="ve")
        w.op("vector", lambda b: b.tensor_single_scalar(s1, s3, cc("half", FILT), ALU.is_gt),
             waits=[("ve", w.cnt["ve"])], inc="ve")
        w.op("vector", lambda b: b.tensor_add(s2, s2, s1), waits=[("ve", w.cnt["ve"])],
             inc="ve")
        w.op("vector", lambda b: b.tensor_single_scalar(s1, s3, cc("neghalf", FILT),
                                                        ALU.is_lt),
             waits=[("ve", w.cnt["ve"])], inc="ve")
        w.op("vector", lambda b: b.tensor_sub(s2, s2, s1), waits=[("ve", w.cnt["ve"])],
             inc="ve")
        w.op("vector", lambda b: b.scalar_tensor_tensor(s1, s2, cc("nchi", FILT), s0,
                                                        ALU.mult, ALU.add),
             waits=[("ve", w.cnt["ve"])], inc="ve")
        w.op("vector", lambda b: b.scalar_tensor_tensor(s1, s2, cc("nclo", FILT), s1,
                                                        ALU.mult, ALU.add),
             waits=[("ve", w.cnt["ve"])], inc="ve")
        vm = w.cnt["ve"]
        w.op("scalar", lambda b: b.activation(dst, s1, AF.Sin, bias=cc("zero", FILT),
                                              scale=1.0),
             waits=[("ve", vm)], inc="se")
        return w.cnt["se"]

    flt = {"st": 0, "pe": 0, "se": 0}
    for t in range(16):
        base = 512 * t
        w.op("gpsimd", lambda b, bb=base: b.iota(rri[0:1, :], [[1, 512]], base=bb, channel_multiplier=0),
             waits=[("ve", flt.get("tile_ve", 0))], inc="po")
        po_i = w.cnt["po"]
        w.op("vector", lambda b: b.tensor_copy(irow[0:1, :], rri[0:1, :]),
             waits=[("po", po_i), ("pe", flt["pe"]), ("st", flt["st"]),
                    ("se", flt["se"])], inc="ve")
        w.op("vector", lambda b: b.tensor_single_scalar(zt[0:1, :], irow[0:1, :],
                                                        cc("invL1", 1), ALU.mult),
             waits=[("ve", w.cnt["ve"])], inc="ve")
        ve_tr = w.cnt["ve"]
        w.op("scalar", lambda b: b.activation(zr[0][0:1, :], irow[0:1, :], AF.Sin,
                                              bias=cc("pihalf", 1), scale=POSC),
             waits=[("ve", ve_tr)], inc="se")
        w.op("scalar", lambda b: b.activation(irow[0:1, :], irow[0:1, :], AF.Sin,
                                              bias=cc("zero", 1), scale=-POSC),
             waits=[("ve", ve_tr), ("se", w.cnt["se"])], inc="se")
        se_z = w.cnt["se"]
        dma_st(zt[1:2, :], zr[0][0:1, :], waits=[("se", se_z)])
        dma_st(zt[2:3, :], irow[0:1, :], waits=[("se", se_z)])
        st_z = w.cnt["st"]
        pe1 = w.op("tensor", lambda b: b.matmul(Ebk[0:FILT, :], w1_sb[0:3, :], zt[0:3, :],
                                                start=True, stop=True),
                   waits=[("st", st_z), ("ld", ld_small), ("ve", w.cnt["ve"]),
                          ("se", w.cnt["se"])], inc="pe")
        se1 = rrsin(h_a[0:FILT, :], Ebk[0:FILT, 0:512], freq_sb[0:FILT, :],
                    fbias_sb[0][0:FILT, :], [("pe", pe1), ("ve", ve_fb)])
        pe2 = w.op("tensor", lambda b: b.matmul(Fbk[0:FILT, :], w2_sb[:, :], h_a[0:FILT, :],
                                                start=True, stop=True),
                   waits=[("se", se1)], inc="pe")
        se2 = rrsin(h_b[0:FILT, :], Fbk[0:FILT, 0:512], freq_sb[0:FILT, :],
                    fbias_sb[1][0:FILT, :], [("pe", pe2)])
        pe3 = w.op("tensor", lambda b: b.matmul(Ebk[0:FILT, :], w3_sb[:, :], h_b[0:FILT, :],
                                                start=True, stop=True),
                   waits=[("se", se2)], inc="pe")
        se3 = rrsin(h_a[0:FILT, :], Ebk[0:FILT, 0:512], freq_sb[0:FILT, :],
                    fbias_sb[2][0:FILT, :], [("pe", pe3)])
        pe4 = w.op("tensor", lambda b: b.matmul(G[:, :], wo_sb[0:FILT, :], h_a[0:FILT, :],
                                                start=True, stop=True),
                   waits=[("se", se3)], inc="pe")
        pe5 = w.op("tensor", lambda b: b.matmul(H[:, :], drow[0:1, :], zt[0:1, :],
                                                start=True, stop=True),
                   waits=[("ve", ve_tr)], inc="pe")
        w.op("scalar", lambda b: b.activation(kt_sb[:, :], H[:, :], AF.Exp,
                                              bias=cc("zero"), scale=1.0),
             waits=[("pe", pe5), ("ve", flt.get("kt_ve", 0)), ("st", flt["st"])],
             inc="se")
        se_dec = w.cnt["se"]
        w.op("vector", lambda b: b.tensor_mul(kt_sb[:, :], G[:, :], kt_sb[:, :]),
             waits=[("pe", pe4), ("se", se_dec)], inc="ve")
        flt["kt_ve"] = w.cnt["ve"]
        dma_st(k_loc[:, base:base + 512], kt_sb[:, :], waits=[("ve", flt["kt_ve"])])
        flt["st"] = w.cnt["st"]
        flt["pe"] = w.cnt["pe"]
        flt["se"] = w.cnt["se"]
        flt["tile_ve"] = w.cnt["ve"]
    st_filter = w.cnt["st"]
    ctxF.close()

    ffn = {}

    def ffn_entry():
        for kk, ss in (("gu_se", "se"), ("u_ve", "ve"), ("d_po", "po"),
                       ("dn_ve0", "ve"), ("dn_ve1", "ve"), ("dn_pe", "pe"),
                       ("acc_se", "se"), ("cwb_ve", "ve"), ("cwb_se", "se")):
            ffn[kk] = max(ffn.get(kk, 0), w.cnt[ss])

    def emit_down(wt, tok_, po_d_, dp, wpar_, acc_w=None):
        for oc in range(8):
            bk = C if oc % 2 == 0 else Dk
            key = f"dn_ve{oc % 2}"
            pe_d = None
            for ic in range(4):
                pe_d = w.op("tensor", lambda b, i_=ic, o=oc, wt_=wt, bb=bk, d_=dp:
                            b.matmul(bb[:, 0:512],
                                     wt_[:, 8192 + 1024 * i_ + 128 * o:8192 + 1024 * i_ + 128 * o + 128],
                                     d_sb[d_][:, 512 * i_:512 * i_ + 512],
                                     start=(i_ == 0), stop=(i_ == 3)),
                            waits=([("po", po_d_), ("ve", ffn[key])]
                                   + ([("se", ffn["cwb_se"])] if oc == 1 else [])
                                   if ic == 0 else []), inc="pe")
            asl = acc[:, 2048 * oc + tok_:2048 * oc + tok_ + 512]
            w.op("vector", lambda b, a_=asl, bb=bk: b.tensor_add(
                a_, bb[:, 0:512], a_),
                waits=[("pe", pe_d),
                       acc_w if acc_w else ("se", ffn["acc_se"])], inc="ve")
            ffn[key] = w.cnt["ve"]
        ffn["dn_pe"] = w.cnt["pe"]
        ffn[f"wq_pe{wpar_}"] = w.cnt["pe"]

    def emit_load(k):
        ei_, q_ = divmod(k, 4)
        par_ = k % 2
        prev = ffn.get(f"wq_pe{par_}", 0)
        dma_wt(wq[par_][:, :], wfq_i[ei_, q_],
               waits=[("pe", prev)] if prev else [])
        pref[(ei_, q_)] = w.cnt["wld"]

    def ffn_expert_units(e, gates=None):
        if gates is None:
            ffn_entry()
        if e >= 0:
            for t4 in range(4):
                pe_cb = w.op("tensor", lambda b, ee=e, tt=512 * t4:
                             b.matmul(Dk[:, 0:512],
                                      sel_sb[0:8, 128 * ee:128 * ee + 128],
                                      cw_t[0:8, tt:tt + 512], start=True, stop=True),
                             waits=[("ve", ffn["dn_ve1"]), ("se", ffn["cwb_se"])],
                             inc="pe")
                w.op("scalar", lambda b, tt=512 * t4: b.copy(cwb_sb[:, tt:tt + 512],
                                                             H[:, 0:512] if False else Dk[:, 0:512]),
                     waits=[("pe", pe_cb)], inc="se")
                ffn["cwb_se"] = w.cnt["se"]
        ei = e + 1
        for q in range(4):
            k = 4 * ei + q
            par = k % 2
            wqt = wq[par]
            if (ei, q) not in pref:
                emit_load(k)
            cst_w = pref.pop((ei, q))
            for t in range(4):
                tok = 512 * t
                gwts = [("wld", cst_w), ("se", ffn["gu_se"])]
                if gates is not None:
                    gwts.append(("ve", gates[0][4 * t + 3]))
                for ic in range(4):
                    pe_g = None
                    for hc in range(8):
                        pe_g = w.op("tensor", lambda b, i_=ic, h=hc, tt=tok, wt=wqt:
                                    b.matmul(A[:, 0:512],
                                             wt[:, 512 * h + 128 * i_:512 * h + 128 * i_ + 128],
                                             xn_fm[:, 2048 * h + tt:2048 * h + tt + 512],
                                             start=(h == 0), stop=(h == 7)),
                                    waits=(gwts if hc == 0 and ic == 0 else
                                           [("se", ffn["gu_se"])] if hc == 0 else []),
                                    inc="pe")
                    pe_u = None
                    for hc in range(8):
                        pe_u = w.op("tensor", lambda b, i_=ic, h=hc, tt=tok, wt=wqt:
                                    b.matmul(Bk[:, 0:512],
                                             wt[:, 4096 + 512 * h + 128 * i_:4096 + 512 * h + 128 * i_ + 128],
                                             xn_fm[:, 2048 * h + tt:2048 * h + tt + 512],
                                             start=(h == 0), stop=(h == 7)),
                                    waits=([("ve", ffn["u_ve"])] if hc == 0 else []),
                                    inc="pe")
                    w.op("scalar", lambda b: b.activation(gs_sb[:, :], A[:, 0:512],
                                                          AF.Silu, bias=cc("zero"),
                                                          scale=1.0),
                         waits=[("pe", pe_g), ("po", ffn["d_po"])], inc="se")
                    ffn["gu_se"] = w.cnt["se"]
                    if e >= 0:
                        w.op("vector", lambda b, tt=tok:
                             b.tensor_mul(us_sb[:, :], Bk[:, 0:512],
                                          cwb_sb[:, tt:tt + 512]),
                             waits=[("pe", pe_u), ("se", ffn["cwb_se"]),
                                    ("po", ffn["d_po"])], inc="ve")
                    else:
                        w.op("vector", lambda b: b.tensor_copy(us_sb[:, :], Bk[:, 0:512]),
                             waits=[("pe", pe_u), ("po", ffn["d_po"])], inc="ve")
                    ffn["u_ve"] = w.cnt["ve"]
                    w.op("gpsimd", lambda b, i_=ic, dp=t % 2:
                         b.tensor_mul(d_sb[dp][:, 512 * i_:512 * i_ + 512], gs_sb[:, :],
                                      us_sb[:, :]),
                         waits=[("se", ffn["gu_se"]), ("ve", ffn["u_ve"]),
                                ("pe", ffn["dn_pe"])], inc="po")
                    ffn["d_po"] = w.cnt["po"]
                po_d = w.cnt["po"]
                if "prev_pend" in ffn:
                    emit_down(*ffn.pop("prev_pend"))
                if t == 0 and k + 1 <= 35:
                    ei2, q2 = divmod(k + 1, 4)
                    if (ei2, q2) not in pref:
                        emit_load(k + 1)
                accw = ("se", gates[1][4 * t + 3]) if gates is not None else None
                ffn["prev_pend"] = (wqt, tok, po_d, t % 2, par, accw)
                ffn["cwb_ve"] = ffn["u_ve"]
                yield

    ffn_entry()
    ctx0 = ExitStack()
    E0 = ctx0.enter_context
    x_tm = [E0(nc.sbuf_tensor(f"x_tm{i}", [128, D], F32)) for i in range(2)]
    xn_tm = [E0(nc.sbuf_tensor(f"xn_tm{i}", [128, D], F32)) for i in range(2)]
    ln_s = [E0(nc.sbuf_tensor(f"ln_s{i}", [128, 8], F32)) for i in range(2)]
    xnf_t = E0(nc.sbuf_tensor("xnf_t", [128, 256], F32))
    top_sb = E0(nc.sbuf_tensor("top_sb", [128, 48], F32))
    pr_sb = E0(nc.sbuf_tensor("pr_sb", [8, 128], F32))

    ph0 = {"sig_se": w.cnt["se"], "cw_ve": w.cnt["ve"],
           "pe0": w.cnt["pe"], "pe1": w.cnt["pe"],
           "se0": w.cnt["se"], "se1": w.cnt["se"],
           "ve0": w.cnt["ve"], "ve1": w.cnt["ve"]}
    ph0_tile_ve = []
    ph0_tile_se = []

    def ph0_tile(i):
        bb2 = i % 2
        xt, xnt, lns = x_tm[bb2], xn_tm[bb2], ln_s[bb2]
        ldx = dma_sync(xt[:, :], x_sl[128 * i:128 * (i + 1), :],
                       waits=([("st", st_filter)] if i < 2 else []) +
                       [("pe", ph0[f"pe{bb2}"]), ("se", ph0[f"se{bb2}"]),
                        ("ve", ph0[f"ve{bb2}"])])
        w.op("vector", lambda b, xt=xt, lns=lns:
             b.reduce_sum(lns[:, 0:1], xt[:, :], axis=AX),
             waits=[("ld", ldx)], inc="ve")
        w.op("scalar", lambda b, xt=xt, xnt=xnt: b.square(xnt[:, :], xt[:, :]),
             waits=[("ld", ldx), ("pe", ph0[f"pe{bb2}"]), ("ve", w.cnt["ve"])],
             inc="se")
        se_sq = w.cnt["se"]
        w.op("vector", lambda b, xnt=xnt, lns=lns:
             b.reduce_sum(lns[:, 1:2], xnt[:, :], axis=AX),
             waits=[("se", se_sq), ("ve", w.cnt["ve"])], inc="ve")
        w.op("vector", lambda b, lns=lns: b.tensor_single_scalar(
            lns[:, 0:1], lns[:, 0:1], cc("inv1024"), ALU.mult),
             waits=[("ve", w.cnt["ve"])], inc="ve")
        w.op("vector", lambda b, lns=lns: b.tensor_mul(lns[:, 2:3], lns[:, 0:1],
                                                       lns[:, 0:1]),
             waits=[("ve", w.cnt["ve"])], inc="ve")
        w.op("vector", lambda b, lns=lns: b.scalar_tensor_tensor(
            lns[:, 2:3], lns[:, 1:2], cc("inv1024"), lns[:, 2:3],
            ALU.mult, ALU.subtract),
             waits=[("ve", w.cnt["ve"])], inc="ve")
        w.op("scalar", lambda b, lns=lns: b.activation(lns[:, 3:4], lns[:, 2:3],
                                                       AF.Sqrt, bias=cc("eps"),
                                                       scale=1.0),
             waits=[("ve", w.cnt["ve"])], inc="se")
        w.op("vector", lambda b, lns=lns: b.reciprocal(lns[:, 4:5], lns[:, 3:4]),
             waits=[("se", w.cnt["se"])], inc="ve")
        w.op("vector", lambda b, lns=lns: b.tensor_mul(lns[:, 5:6], lns[:, 0:1],
                                                       lns[:, 4:5]),
             waits=[("ve", w.cnt["ve"])], inc="ve")
        w.op("vector", lambda b, xt=xt, xnt=xnt, lns=lns: b.tensor_scalar(
            xnt[:, :], xt[:, :], lns[:, 4:5], lns[:, 5:6], ALU.mult, ALU.subtract),
             waits=[("ve", w.cnt["ve"]), ("se", se_sq)], inc="ve")
        ve_xn = w.cnt["ve"]
        ph0[f"ve{bb2}"] = ve_xn
        for c8 in range(8):
            bx = G
            by = H
            xs = 128 * c8
            pe_t1 = w.op("tensor", lambda b, bb=bx, s=xs, xnt=xnt:
                         b.transpose(bb[:, 0:128], xnt[:, s:s + 128], ident_f[:, :]),
                         waits=[("ve", ve_xn), ("se", w.cnt["se"])], inc="pe")
            pe_t2 = w.op("tensor", lambda b, bb=by, s=xs, xt=xt:
                         b.transpose(bb[:, 0:128], xt[:, s:s + 128], ident_f[:, :]),
                         waits=[("se", w.cnt["se"])], inc="pe")
            o = (c8 % 2) * 128
            w.op("scalar", lambda b, bb=bx, oo=o: b.copy(xnf_t[:, oo:oo + 128],
                                                         bb[:, 0:128]),
                 waits=[("pe", pe_t1), ("ve", w.cnt["ve"])], inc="se")
            se_e1 = w.cnt["se"]
            w.op("scalar", lambda b, bb=by, c=c8, ti=i:
                 b.copy(acc[:, 2048 * c + 128 * ti:2048 * c + 128 * ti + 128],
                        bb[:, 0:128]),
                 waits=[("pe", pe_t2)], inc="se")
            w.op("vector", lambda b, c=c8, ti=i, oo=o:
                 b.tensor_copy(xn_fm[:, 2048 * c + 128 * ti:2048 * c + 128 * ti + 128],
                               xnf_t[:, oo:oo + 128]),
                 waits=[("se", se_e1)], inc="ve")
            w.op("tensor", lambda b, c=c8, oo=o:
                 b.matmul(Ebk[0:8, 0:128], wr_sb[:, 8 * c:8 * c + 8],
                          xnf_t[:, oo:oo + 128], start=(c == 0), stop=(c == 7)),
                 waits=([("se", se_e1), ("ve", ve_xn)] if c8 > 0 else
                        [("se", se_e1), ("ve", ve_xn), ("se", ph0["sig_se"])]),
                 inc="pe")
        pe_r = w.cnt["pe"]
        w.op("scalar", lambda b: b.activation(pr_sb[0:8, :], Ebk[0:8, 0:128], AF.Sigmoid,
                                              bias=rb_sb[0:8, :], scale=1.0),
             waits=[("pe", pe_r)], inc="se")
        ph0["sig_se"] = w.cnt["se"]
        pe_tp = w.op("tensor", lambda b: b.transpose(Fbk[0:128, 0:8], pr_sb[0:8, :],
                                                     ident_f[0:8, 0:8]),
                     waits=[("se", ph0["sig_se"]), ("ve", ph0["cw_ve"])], inc="pe")
        w.op("vector", lambda b: b.tensor_copy(top_sb[:, 0:8], Fbk[0:128, 0:8]),
             waits=[("pe", pe_tp)], inc="ve")
        w.op("vector", lambda b: b.reduce_max(top_sb[:, 8:9], top_sb[:, 0:8], axis=AX),
             waits=[("ve", w.cnt["ve"])], inc="ve")
        w.op("vector", lambda b: b.tensor_single_scalar(top_sb[:, 16:24], top_sb[:, 0:8],
                                                        top_sb[:, 8:9], ALU.is_ge),
             waits=[("ve", w.cnt["ve"])], inc="ve")
        w.op("vector", lambda b: b.tensor_scalar(top_sb[:, 24:32], top_sb[:, 16:24],
                                                 cc("neg1"), cc("one"), ALU.mult, ALU.add),
             waits=[("ve", w.cnt["ve"])], inc="ve")
        w.op("vector", lambda b: b.tensor_mul(top_sb[:, 24:32], top_sb[:, 0:8],
                                              top_sb[:, 24:32]),
             waits=[("ve", w.cnt["ve"])], inc="ve")
        w.op("vector", lambda b: b.reduce_max(top_sb[:, 9:10], top_sb[:, 24:32], axis=AX),
             waits=[("ve", w.cnt["ve"])], inc="ve")
        w.op("vector", lambda b: b.tensor_single_scalar(top_sb[:, 32:40], top_sb[:, 24:32],
                                                        top_sb[:, 9:10], ALU.is_ge),
             waits=[("ve", w.cnt["ve"])], inc="ve")
        w.op("vector", lambda b: b.tensor_add(top_sb[:, 16:24], top_sb[:, 16:24],
                                              top_sb[:, 32:40]),
             waits=[("ve", w.cnt["ve"])], inc="ve")
        w.op("vector", lambda b: b.tensor_mul(top_sb[:, 16:24], top_sb[:, 0:8],
                                              top_sb[:, 16:24]),
             waits=[("ve", w.cnt["ve"])], inc="ve")
        ve_cw = w.cnt["ve"]
        pe_tb = w.op("tensor", lambda b: b.transpose(Fbk[0:8, 0:128], top_sb[0:128, 16:24],
                                                     ident_f[:, :]),
                     waits=[("ve", ve_cw)], inc="pe")
        w.op("vector", lambda b, ti=i: b.tensor_copy(cw_t[0:8, 128 * ti:128 * ti + 128],
                                                     Fbk[0:8, 0:128]),
             waits=[("pe", pe_tb)], inc="ve")
        ph0["cw_ve"] = w.cnt["ve"]
        ph0[f"pe{bb2}"] = w.cnt["pe"]
        ph0[f"se{bb2}"] = w.cnt["se"]
        ph0_tile_ve.append(w.cnt["ve"])
        ph0_tile_se.append(w.cnt["se"])

    gen_sh = ffn_expert_units(-1, gates=(ph0_tile_ve, ph0_tile_se))
    for i in range(16):
        ph0_tile(i)
        if i % 4 == 3 and i < 15:
            next(gen_sh)
    ve_ph0 = w.cnt["ve"]

    st_xn = dma_st(xn_bnc.rearrange("(hc p) t -> p hc t", p=128),
                   xn_fm[:, :].rearrange("p (hc t) -> p hc t", t=TPC),
                   waits=[("ve", ve_ph0)])
    for _ in gen_sh:
        pass
    w.op("gpsimd", lambda b: b.collective_compute(
        "AllGather", ALU.bypass, replica_groups=[list(range(NCORES))],
        ins=[xn_bnc.opt()], outs=[g_xn.opt()]),
        waits=[("st", st_xn)], inc="ag")
    ctx0.close()

    ctxA = ExitStack()
    EA = ctxA.enter_context
    inw_sb = EA(nc.sbuf_tensor("inw_sb", [128, 3072], BF16))
    xgh = [EA(nc.sbuf_tensor(f"xgh{i}", [128, 4 * 512], BF16)) for i in range(2)]
    up_sb = EA(nc.sbuf_tensor("up_sb", [128, 3 * 514], F32))
    uc2 = EA(nc.sbuf_tensor("uc2", [128, 2 * 512], F32))
    cscr = EA(nc.sbuf_tensor("cscr", [128, 512], F32))

    dma_cast(inw_sb[:, :].rearrange("p (hc i) -> p hc i", i=384),
             inw_sl.rearrange("(hc p) i -> p hc i", p=128),
             waits=[("st", st_filter), ("ag", 1)])
    cst_inw = w.cnt["cst"]
    pa = {"spill_st": w.cnt["st"],
          "xh_pe0": w.cnt["pe"], "xh_pe1": w.cnt["pe"]}
    for oc in range(3):
        pa[f"conv_{oc}"] = ("po" if oc == 1 else "ve", w.cnt["po" if oc == 1 else "ve"])
        pa[f"up_se{oc}"] = w.cnt["se"]
        w.op("gpsimd", lambda b, o=oc: b.memset(up_sb[:, 514 * o:514 * o + 2], 0.0),
             waits=[("st", st_filter), ("ag", 1)], inc="po")
    pa["po_hist"] = w.cnt["po"]

    def phaseA_tile(t):
        r = t // 4
        colc = 512 * (t % 4)
        ldh = []
        for half in range(2):
            ldh.append(w.op("sync", lambda b, hf=half, rr=r, cl=colc: b.dma_start(
                out=xgh[hf][:, :].rearrange("p (hc n) -> p hc n", n=512),
                in_=g_xn[1024 * rr + 512 * hf:1024 * rr + 512 * hf + 512,
                         cl:cl + 512].rearrange("(hc p) n -> p hc n", p=128)),
                waits=[("ag", 1), ("pe", pa[f"xh_pe{half}"])], inc="ld", amt=16))
        pe_last = None
        for hc in range(8):
            half = hc // 4
            xsl = 512 * (hc % 4)
            for oc in range(3):
                bk = (Ebk, Fbk, G)[oc]
                wts = []
                if hc == 0 and oc == 0:
                    wts = [("ld", ldh[0]), ("cst", cst_inw), ("po", pa["po_hist"]),
                           ("se", max(pa["up_se0"], pa["up_se1"], pa["up_se2"])),
                           pa["conv_0"], pa["conv_1"], pa["conv_2"]]
                elif hc == 4 and oc == 0:
                    wts = [("ld", ldh[1])]
                pe_last = w.op("tensor", lambda b, h=hc, o=oc, hf=half, xs=xsl, bb=bk:
                               b.matmul(bb[:, 0:512],
                                        inw_sb[:, 384 * h + 128 * o:384 * h + 128 * o + 128],
                                        xgh[hf][:, xs:xs + 512],
                                        start=(h == 0), stop=(h == 7)),
                               waits=wts, inc="pe")
            if hc == 3:
                pa["xh_pe0"] = w.cnt["pe"]
        pa["xh_pe1"] = w.cnt["pe"]
        if t == 16:
            pass
        for oc in range(3):
            bk = (Ebk, Fbk, G)[oc]
            if t == 16 and oc == 0:
                for o3 in range(3):
                    w.op("gpsimd", lambda b, o=o3:
                         b.memset(up_sb[:, 514 * o:514 * o + 2], 0.0),
                         waits=[pa[f"conv_{o3}"]], inc="po")
                pa["po_hist"] = w.cnt["po"]
            w.op("scalar", lambda b, o=oc, bb=bk:
                 b.activation(up_sb[:, 514 * o + 2:514 * o + 514], bb[:, 0:512],
                              AF.Identity, bias=inb_sb[:, o:o + 1], scale=1.0),
                 waits=[("pe", pe_last), ("po", pa["po_hist"]), pa[f"conv_{oc}"]],
                 inc="se")
            pa[f"up_se{oc}"] = w.cnt["se"]
            u0 = up_sb[:, 514 * oc + 0:514 * oc + 512]
            u1 = up_sb[:, 514 * oc + 1:514 * oc + 513]
            u2 = up_sb[:, 514 * oc + 2:514 * oc + 514]
            eng = "vector"
            sem = "ve"
            stage1 = cscr[:, :] if oc < 2 else tw_s1[:, 0:512]
            dst = uc2[:, 512 * oc:512 * oc + 512] if oc < 2 else cscr[:, :]
            w.op(eng, lambda b, a_=u0, o=oc, s_=stage1:
                 b.tensor_scalar(s_, a_, shw_sb[:, 3 * o:3 * o + 1], shb_sb[:, o:o + 1],
                                 ALU.mult, ALU.add),
                 waits=[("se", pa[f"up_se{oc}"]), ("st", pa["spill_st"])], inc=sem)
            w.op(eng, lambda b, a_=u1, o=oc, dd=dst, s_=stage1:
                 b.scalar_tensor_tensor(dd, a_, shw_sb[:, 3 * o + 1:3 * o + 2], s_,
                                        ALU.mult, ALU.add), inc=sem)
            w.op(eng, lambda b, a_=u2, o=oc, dd=dst:
                 b.scalar_tensor_tensor(dd, a_, shw_sb[:, 3 * o + 2:3 * o + 3], dd,
                                        ALU.mult, ALU.add), inc=sem)
            w.op(eng, lambda b, o=oc:
                 b.tensor_copy(up_sb[:, 514 * o:514 * o + 2],
                               up_sb[:, 514 * o + 512:514 * o + 514]), inc=sem)
            pa[f"conv_{oc}"] = (sem, w.cnt[sem])
        w.op("vector", lambda b: b.tensor_mul(uc2[:, 512:1024], cscr[:, :],
                                              uc2[:, 512:1024]),
             waits=[("ve", w.cnt["ve"]), pa["conv_1"]], inc="ve")
        ve_vx = w.cnt["ve"]
        dma_st(vx_loc[:, 512 * t:512 * t + 512], uc2[:, 512:1024], waits=[("ve", ve_vx)])
        dma_st(x0_loc[:, 512 * t:512 * t + 512], uc2[:, 0:512], waits=[("ve", ve_vx)])
        pa["spill_st"] = w.cnt["st"]
    def run_expert(e, fillers, per_unit):
        for _ in ffn_expert_units(e):
            n = per_unit
            while n > 0 and fillers:
                fillers.pop(0)()
                n -= 1

    def flush(fillers):
        while fillers:
            fillers.pop(0)()

    fillA = [(lambda tt: (lambda: phaseA_tile(tt)))(t) for t in range(32)]
    run_expert(0, fillA, 2)
    flush(fillA)
    st_phA = w.cnt["st"]
    ctxA.close()

    ctxB = ExitStack()
    EB = ctxB.enter_context
    vx_st = [EB(nc.sbuf_tensor(f"vx_st{i}", [128, 512], BF16)) for i in range(2)]
    k_st = [EB(nc.sbuf_tensor(f"k_st{i}", [128, 512], BF16)) for i in range(2)]
    ap_re = EB(nc.sbuf_tensor("ap_re", [128, 512], BF16))
    ap_im = EB(nc.sbuf_tensor("ap_im", [128, 512], BF16))
    xk_re = EB(nc.sbuf_tensor("xk_re", [128, 512], F32))
    xk_im = EB(nc.sbuf_tensor("xk_im", [128, 512], F32))
    w_re = EB(nc.sbuf_tensor("w_re", [128, 512], BF16))
    w_im = EB(nc.sbuf_tensor("w_im", [128, 512], BF16))
    gp_re = EB(nc.sbuf_tensor("gp_re", [128, 512], BF16))
    gp_im = EB(nc.sbuf_tensor("gp_im", [128, 512], BF16))
    y_sb = EB(nc.sbuf_tensor("y_sb", [64, 2 * 512], F32))

    fft = {
        "s1_ve0": w.cnt["ve"], "s1_ve1": w.cnt["ve"],
        "x_ve0": w.cnt["ve"], "x_ve1": w.cnt["ve"], "s1_ve": w.cnt["ve"],
        "x_ve": w.cnt["ve"], "y_se": w.cnt["se"],
        "ap_pe": w.cnt["pe"], "gp_pe": w.cnt["pe"], "w_pe": w.cnt["pe"],
        "st_pe0": w.cnt["pe"], "st_pe1": w.cnt["pe"],
        "prod_ve": w.cnt["ve"], "y_st": w.cnt["st"], "xk_ve": w.cnt["ve"],
        "ent_se": w.cnt["se"], "y_se0": w.cnt["se"], "y_se1": w.cnt["se"],
    }
    w.op("gpsimd", lambda b: b.memset(k_st[0][64:128, :], 0.0),
         waits=[("st", w.cnt["st"]), ("ve", w.cnt["ve"])], inc="po")
    w.op("gpsimd", lambda b: b.memset(k_st[1][64:128, :], 0.0), inc="po")
    po_kz = w.cnt["po"]

    tre_b = t_re[:, :].unsqueeze(1).broadcast_to((128, 4, 128))
    tim_b = t_im[:, :].unsqueeze(1).broadcast_to((128, 4, 128))

    def v3(ap):
        return ap.rearrange("p (c jf) -> p c jf", jf=128)

    def twiddle(dst_re, dst_im, src_re, src_im, conj, wait_pe, extra=()):
        sc1 = tw_s1[:, 0:512]
        sc2 = tw_s1[:, 512:1024]
        w.op("vector", lambda b: b.tensor_mul(v3(sc1), v3(src_re), tre_b),
             waits=[("pe", wait_pe)] + list(extra), inc="ve")
        w.op("vector", lambda b: b.tensor_mul(v3(sc2), v3(src_im), tim_b),
             waits=[("ve", w.cnt["ve"])], inc="ve")
        if conj:
            w.op("vector", lambda b: b.tensor_add(dst_re, sc1, sc2),
                 waits=[("ve", w.cnt["ve"])], inc="ve")
        else:
            w.op("vector", lambda b: b.tensor_sub(dst_re, sc1, sc2),
                 waits=[("ve", w.cnt["ve"])], inc="ve")
        w.op("vector", lambda b: b.tensor_mul(v3(sc1), v3(src_re), tim_b),
             waits=[("ve", w.cnt["ve"])], inc="ve")
        w.op("vector", lambda b: b.tensor_mul(v3(sc2), v3(src_im), tre_b),
             waits=[("ve", w.cnt["ve"])], inc="ve")
        if conj:
            w.op("vector", lambda b: b.tensor_sub(dst_im, sc2, sc1),
                 waits=[("ve", w.cnt["ve"])], inc="ve")
        else:
            w.op("vector", lambda b: b.tensor_add(dst_im, sc1, sc2),
                 waits=[("ve", w.cnt["ve"])], inc="ve")
        return w.cnt["ve"]

    def fft_group(g):
        c0 = 4 * g
        par = g % 2
        s1r, s1i = Ebk, Fbk
        sXr, sXi = G, H
        vst, kst = vx_st[par], k_st[par]
        dma_cast(vst[0:64, :].rearrange("js (c jf) -> js c jf", jf=128),
                 vx_loc[c0:c0 + 4, 0:LSEQ].rearrange("c (js jf) -> js c jf", jf=128),
                 waits=[("st", fft["phA_st"]), ("pe", fft[f"st_pe{par}"])])
        dma_cast(vst[64:128, :].rearrange("js (c jf) -> js c jf", jf=128),
                 vx_loc[c0:c0 + 4, LSEQ:NTOK].rearrange("c (js jf) -> js c jf", jf=128))
        dma_cast(kst[0:64, :].rearrange("js (c jf) -> js c jf", jf=128),
                 k_loc[c0:c0 + 4, :].rearrange("c (js jf) -> js c jf", jf=128),
                 waits=[("st", st_filter), ("po", po_kz)])
        cst_g = w.cnt["cst"]
        pe_s1 = None
        for c in range(4):
            pe_s1 = w.op("tensor", lambda b, c_=c, bb=s1r, v=vst:
                         b.matmul(bb[:, 128 * c_:128 * c_ + 128],
                                  v[:, 128 * c_:128 * c_ + 128], stk_re[:, :],
                                  start=True, stop=True),
                         waits=([("cst", cst_g), ("st", st_stk),
                                 ("ve", fft["s1_ve"]), ("se", fft["ent_se"])]
                                if c == 0 else []), inc="pe")
            pe_s1 = w.op("tensor", lambda b, c_=c, bb=s1i, v=vst:
                         b.matmul(bb[:, 128 * c_:128 * c_ + 128],
                                  v[:, 128 * c_:128 * c_ + 128], stk_im[:, :],
                                  start=True, stop=True), inc="pe")
        ve_tw1 = twiddle(ap_re[:, :], ap_im[:, :], s1r[:, 0:512], s1i[:, 0:512],
                         conj=False, wait_pe=pe_s1, extra=[("pe", fft["ap_pe"])])
        fft["s1_ve"] = ve_tw1
        w.op("tensor", lambda b, bb=sXr: b.matmul(bb[:, 0:512], f1re[:, :], ap_re[:, :],
                                                  start=True, stop=False),
             waits=[("ve", ve_tw1), ("ve", fft["x_ve"]),
                    ("se", fft["y_se"])], inc="pe")
        w.op("tensor", lambda b, bb=sXr: b.matmul(bb[:, 0:512], nf1im[:, :], ap_im[:, :],
                                                  start=False, stop=True), inc="pe")
        w.op("tensor", lambda b, bb=sXi: b.matmul(bb[:, 0:512], f1im[:, :], ap_re[:, :],
                                                  start=True, stop=False), inc="pe")
        pe_x = w.op("tensor", lambda b, bb=sXi: b.matmul(bb[:, 0:512], f1re[:, :],
                                                         ap_im[:, :],
                                                         start=False, stop=True), inc="pe")
        fft["ap_pe"] = w.cnt["pe"]
        pe_k1 = None
        for c in range(4):
            pe_k1 = w.op("tensor", lambda b, c_=c, bb=s1r, v=kst:
                         b.matmul(bb[:, 128 * c_:128 * c_ + 128],
                                  v[:, 128 * c_:128 * c_ + 128], stk_re[:, :],
                                  start=True, stop=True),
                         waits=([("ve", ve_tw1)] if c == 0 else []), inc="pe")
            pe_k1 = w.op("tensor", lambda b, c_=c, bb=s1i, v=kst:
                         b.matmul(bb[:, 128 * c_:128 * c_ + 128],
                                  v[:, 128 * c_:128 * c_ + 128], stk_im[:, :],
                                  start=True, stop=True), inc="pe")
        ve_tw2 = twiddle(ap_re[:, :], ap_im[:, :], s1r[:, 0:512], s1i[:, 0:512],
                         conj=False, wait_pe=pe_k1, extra=[("pe", fft["ap_pe"])])
        fft["s1_ve"] = ve_tw2
        w.op("tensor", lambda b, bb=s1r: b.matmul(bb[:, 0:512], f1re[:, :], ap_re[:, :],
                                                  start=True, stop=False),
             waits=[("ve", ve_tw2)], inc="pe")
        w.op("tensor", lambda b, bb=s1r: b.matmul(bb[:, 0:512], nf1im[:, :], ap_im[:, :],
                                                  start=False, stop=True), inc="pe")
        w.op("tensor", lambda b, bb=s1i: b.matmul(bb[:, 0:512], f1im[:, :], ap_re[:, :],
                                                  start=True, stop=False), inc="pe")
        pe_k2 = w.op("tensor", lambda b, bb=s1i: b.matmul(bb[:, 0:512], f1re[:, :],
                                                          ap_im[:, :],
                                                          start=False, stop=True),
                     inc="pe")
        fft["ap_pe"] = w.cnt["pe"]
        w.op("scalar", lambda b, bb=s1r: b.copy(xk_re[:, :], bb[:, 0:512]),
             waits=[("pe", pe_k2), ("ve", fft["xk_ve"])], inc="se")
        w.op("scalar", lambda b, bb=s1i: b.copy(xk_im[:, :], bb[:, 0:512]),
             waits=[("pe", pe_k2)], inc="se")
        se_xk = w.cnt["se"]
        sc1 = tw_s1[:, 0:512]
        sc2 = tw_s1[:, 512:1024]
        w.op("vector", lambda b, bb=sXr: b.tensor_mul(sc1, bb[:, 0:512], xk_re[:, :]),
             waits=[("pe", pe_x), ("se", se_xk), ("pe", fft["w_pe"])], inc="ve")
        w.op("vector", lambda b, bb=sXi: b.tensor_mul(sc2, bb[:, 0:512], xk_im[:, :]),
             waits=[("ve", w.cnt["ve"])], inc="ve")
        w.op("vector", lambda b: b.tensor_sub(w_re[:, :], sc1, sc2),
             waits=[("ve", w.cnt["ve"])], inc="ve")
        w.op("vector", lambda b, bb=sXr: b.tensor_mul(sc1, bb[:, 0:512], xk_im[:, :]),
             waits=[("ve", w.cnt["ve"])], inc="ve")
        w.op("vector", lambda b, bb=sXi: b.tensor_mul(sc2, bb[:, 0:512], xk_re[:, :]),
             waits=[("ve", w.cnt["ve"])], inc="ve")
        w.op("vector", lambda b: b.tensor_add(w_im[:, :], sc1, sc2),
             waits=[("ve", w.cnt["ve"])], inc="ve")
        ve_prod = w.cnt["ve"]
        fft["prod_ve"] = ve_prod
        fft["xk_ve"] = ve_prod
        fft["x_ve"] = ve_prod
        pe_i1 = None
        for c in range(4):
            pe_i1 = w.op("tensor", lambda b, c_=c, bb=s1r:
                         b.matmul(bb[:, 128 * c_:128 * c_ + 128],
                                  w_re[:, 128 * c_:128 * c_ + 128], f1re[:, :],
                                  start=True, stop=False),
                         waits=([("ve", ve_prod), ("se", se_xk)] if c == 0 else []),
                         inc="pe")
            pe_i1 = w.op("tensor", lambda b, c_=c, bb=s1r:
                         b.matmul(bb[:, 128 * c_:128 * c_ + 128],
                                  w_im[:, 128 * c_:128 * c_ + 128], f1im[:, :],
                                  start=False, stop=True), inc="pe")
            pe_i1 = w.op("tensor", lambda b, c_=c, bb=s1i:
                         b.matmul(bb[:, 128 * c_:128 * c_ + 128],
                                  w_re[:, 128 * c_:128 * c_ + 128], nf1im[:, :],
                                  start=True, stop=False), inc="pe")
            pe_i1 = w.op("tensor", lambda b, c_=c, bb=s1i:
                         b.matmul(bb[:, 128 * c_:128 * c_ + 128],
                                  w_im[:, 128 * c_:128 * c_ + 128], f1re[:, :],
                                  start=False, stop=True), inc="pe")
        fft["w_pe"] = w.cnt["pe"]
        ve_tw3 = twiddle(gp_re[:, :], gp_im[:, :], s1r[:, 0:512], s1i[:, 0:512],
                         conj=True, wait_pe=pe_i1, extra=[("pe", fft["gp_pe"])])
        fft["s1_ve"] = ve_tw3
        w.op("tensor", lambda b, bb=sXr: b.matmul(bb[0:64, 0:512], fi_re[:, :],
                                                  gp_re[:, :], start=True, stop=False),
             waits=[("ve", ve_tw3)], inc="pe")
        w.op("tensor", lambda b, bb=sXr: b.matmul(bb[0:64, 0:512], fi_ip[:, :],
                                                  gp_im[:, :], start=False, stop=True),
             inc="pe")
        w.op("tensor", lambda b, bb=sXi: b.matmul(bb[0:64, 0:512], fi_re[:, :],
                                                  gp_im[:, :], start=True, stop=False),
             inc="pe")
        pe_y = w.op("tensor", lambda b, bb=sXi: b.matmul(bb[0:64, 0:512], fi_in[:, :],
                                                         gp_re[:, :],
                                                         start=False, stop=True), inc="pe")
        fft["gp_pe"] = w.cnt["pe"]
        w.op("scalar", lambda b, bb=sXr: b.copy(y_sb[:, 0:512], bb[0:64, 0:512]),
             waits=[("pe", pe_y), ("st", fft["y_st"])], inc="se")
        w.op("scalar", lambda b, bb=sXi: b.copy(y_sb[:, 512:1024], bb[0:64, 0:512]),
             waits=[("pe", pe_y)], inc="se")
        se_y = w.cnt["se"]
        fft["y_se"] = se_y
        dma_st(y_loc[c0:c0 + 4, 0:LSEQ].rearrange("c (js jf) -> js c jf", jf=128),
               y_sb[:, 0:512].rearrange("js (c jf) -> js c jf", jf=128),
               waits=[("se", se_y)])
        dma_st(y_loc[c0:c0 + 4, LSEQ:NTOK].rearrange("c (js jf) -> js c jf", jf=128),
               y_sb[:, 512:1024].rearrange("js (c jf) -> js c jf", jf=128))
        fft["y_st"] = w.cnt["st"]
        fft[f"st_pe{par}"] = w.cnt["pe"]
    fft["phA_st"] = st_phA
    fillB = [(lambda gg: (lambda: fft_group(gg)))(g) for g in range(32)]
    run_expert(1, fillB, 1)
    run_expert(2, fillB, 1)
    flush(fillB)
    st_fft = w.cnt["st"]
    ctxB.close()

    ctxC = ExitStack()
    EC = ctxC.enter_context
    outw_sb = EC(nc.sbuf_tensor("outw_sb", [128, D], BF16))
    yc_sb = [EC(nc.sbuf_tensor(f"yc{i}", [128, 512], F32)) for i in range(2)]
    vxc_sb = [EC(nc.sbuf_tensor(f"vxc{i}", [128, 512], F32)) for i in range(2)]
    x0c_sb = [EC(nc.sbuf_tensor(f"x0c{i}", [128, 512], F32)) for i in range(2)]
    yv_sb = EC(nc.sbuf_tensor("yv_sb", [128, 512], BF16))
    pc_sb = [EC(nc.sbuf_tensor(f"pc{i}", [128, 512], F32)) for i in range(2)]
    at_sb = [EC(nc.sbuf_tensor(f"at_sb{i}", [128, 512], F32)) for i in range(2)]
    fo_sb = [EC(nc.sbuf_tensor(f"fo_sb{i}", [128, 512], F32)) for i in range(2)]

    dma_cast(outw_sb[:, :], outw_i, waits=[("st", w.cnt["st"])])
    cst_outw = w.cnt["cst"]
    pc = {"po_se0": w.cnt["se"], "po_se1": w.cnt["se"], "yv_pe": w.cnt["pe"],
          "v_ve": w.cnt["ve"]}
    def pc_tile(t):
        par = t % 2
        yc, vxc, x0c = yc_sb[par], vxc_sb[par], x0c_sb[par]
        dma_sync(yc[:, :], y_loc[:, 512 * t:512 * t + 512],
                 waits=[("st", pc["fft_st"])] + ([("ve", pc.get(f"ld_ve{par}", 0))]
                                           if t >= 2 else []))
        dma_sync(vxc[:, :], vx_loc[:, 512 * t:512 * t + 512])
        dma_sync(x0c[:, :], x0_loc[:, 512 * t:512 * t + 512])
        ld_c = w.cnt["ld"]
        w.op("vector", lambda b, a_=vxc, c_=yc: b.scalar_tensor_tensor(
            tw_s1[:, 0:512], a_[:, :], fb_sb[:, 0:1], c_[:, :], ALU.mult, ALU.add),
            waits=[("ld", ld_c), ("ve", pc["v_ve"])], inc="ve")
        w.op("vector", lambda b, a_=x0c: b.tensor_mul(yv_sb[:, :], tw_s1[:, 0:512],
                                                      a_[:, :]),
             waits=[("ve", w.cnt["ve"]), ("pe", pc["yv_pe"])], inc="ve")
        ve_yv = w.cnt["ve"]
        pc[f"ld_ve{par}"] = ve_yv
        pc["v_ve"] = ve_yv
        blk = t // 4
        coln = 512 * (t % 4)
        for oc in range(8):
            bk = Ebk if oc % 2 == 0 else Fbk
            pcs = pc_sb[oc % 2]
            pe_o = w.op("tensor", lambda b, o=oc, bb=bk:
                        b.matmul(bb[:, 0:512], outw_sb[:, 128 * o:128 * o + 128],
                                 yv_sb[:, :], start=True, stop=True),
                        waits=[("ve", ve_yv), ("se", pc[f"po_se{oc % 2}"]),
                               ("cst", cst_outw)], inc="pe")
            w.op("scalar", lambda b, bb=bk, p_=pcs: b.copy(p_[:, :], bb[:, 0:512]),
                 waits=[("pe", pe_o), ("pst", pc.get(f"p_st{oc % 2}", 0))], inc="se")
            pc[f"po_se{oc % 2}"] = w.cnt["se"]
            w.op("gpsimd", lambda b, bl=blk, o=oc, cl=coln, p_=pcs: b.dma_start(
                out=p_buf[1024 * bl + 128 * o:1024 * bl + 128 * o + 128, cl:cl + 512],
                in_=p_[:, :]),
                waits=[("se", pc[f"po_se{oc % 2}"])], inc="pst", amt=16)
            pc[f"p_st{oc % 2}"] = w.cnt["pst"]
        pc["yv_pe"] = w.cnt["pe"]
    pc["fft_st"] = st_fft
    fillC = [(lambda tt: (lambda: pc_tile(tt)))(t) for t in range(32)]
    run_expert(3, fillC, 1)
    run_expert(4, fillC, 1)
    flush(fillC)
    st_pc = w.cnt["pst"]

    w.op("gpsimd", lambda b: b.collective_compute(
        "ReduceScatter", ALU.add, replica_groups=[list(range(NCORES))],
        ins=[p_buf.opt()], outs=[attn_rs.opt()]),
        waits=[("pst", st_pc)], inc="rs")

    for e in range(5, NEXP):
        run_expert(e, [], 0)
    if "prev_pend" in ffn:
        emit_down(*ffn.pop("prev_pend"))

    fin = {"st0": 0, "st1": 0, "ve0": w.cnt["ve"], "ve1": w.cnt["ve"]}
    for i4 in range(32):
        hc, s4 = divmod(i4, 4)
        par = i4 % 2
        col = 512 * s4
        lda = dma_sync(at_sb[par][:, :],
                       attn_rs[128 * hc:128 * (hc + 1), col:col + 512],
                       waits=[("rs", 1), ("ve", fin[f"ve{par}"])])
        w.op("vector", lambda b, c_=hc, cl=col, p_=par: b.scalar_tensor_tensor(
            fo_sb[p_][:, :], at_sb[p_][:, :], outb_sb[:, c_:c_ + 1],
            acc[:, 2048 * c_ + cl:2048 * c_ + cl + 512], ALU.add, ALU.add),
            waits=[("ld", lda), ("st", fin[f"st{par}"])], inc="ve")
        fin[f"ve{par}"] = w.cnt["ve"]
        dma_st(out_fm[128 * hc:128 * (hc + 1), col:col + 512], fo_sb[par][:, :],
               waits=[("ve", fin[f"ve{par}"])])
        fin[f"st{par}"] = w.cnt["st"]
    if DEBUG_DUMPS:
        dbg_acc = nc.dram_tensor("dbg_acc", [128, 8 * TPC], F32, kind="ExternalOutput").ap()
        dma_st(dbg_acc, acc[:, :], waits=[("st", w.cnt["st"])])
        for nm, src_ap in (("dbg_vx", vx_loc), ("dbg_x0", x0_loc), ("dbg_y", y_loc),
                           ("dbg_k", k_loc), ("dbg_at", attn_rs)):
            dout = nc.dram_tensor(nm, list(src_ap.shape), F32, kind="ExternalOutput").ap()
            dma_st(dout, src_ap, waits=[("st", w.cnt["st"])])
    w.wait("sync", "st", w.cnt["st"])
    w.wait("gpsimd", "st", w.cnt["st"])
    ctxC.close()

    with nc.Block() as block:
        w.emit(block)
    ctx.close()
    return nc



def _prep_inputs(inp):
    f32 = np.float32
    x = np.asarray(inp["L0_attn_u"], f32).reshape(NTOK, D)
    in_w = np.asarray(inp["L0_attn_in_w"], f32)
    in_b = np.asarray(inp["L0_attn_in_b"], f32)
    short_w = np.asarray(inp["L0_attn_short_w"], f32)
    short_b = np.asarray(inp["L0_attn_short_b"], f32)
    mlp_w1 = np.asarray(inp["L0_attn_mlp_w1"], f32)
    mlp_b1 = np.asarray(inp["L0_attn_mlp_b1"], f32)
    mlp_w2 = np.asarray(inp["L0_attn_mlp_w2"], f32)
    mlp_b2 = np.asarray(inp["L0_attn_mlp_b2"], f32)
    mlp_w3 = np.asarray(inp["L0_attn_mlp_w3"], f32)
    mlp_b3 = np.asarray(inp["L0_attn_mlp_b3"], f32)
    mlp_wo = np.asarray(inp["L0_attn_mlp_wo"], f32)
    freq = np.asarray(inp["L0_attn_freq"], f32)
    filt_bias = np.asarray(inp["L0_attn_filt_bias"], f32)
    out_w = np.asarray(inp["L0_attn_out_w"], f32)
    out_b = np.asarray(inp["L0_attn_out_b"], f32)
    wg_s = np.asarray(inp["L0_ffn_Wg_s"], f32)
    wu_s = np.asarray(inp["L0_ffn_Wu_s"], f32)
    wd_s = np.asarray(inp["L0_ffn_Wd_s"], f32)
    wg = np.asarray(inp["L0_ffn_Wg"], f32)
    wu = np.asarray(inp["L0_ffn_Wu"], f32)
    wd = np.asarray(inp["L0_ffn_Wd"], f32)
    wr = np.asarray(inp["L0_ffn_Wr"], f32)
    rb = np.asarray(inp["L0_ffn_rb"], f32)

    A = np.ascontiguousarray
    w1p = np.zeros((4, FILT), f32)
    w1p[0:3] = mlp_w1
    sel = np.zeros((8, 1024), f32)
    for e in range(8):
        sel[e, 128 * e:128 * (e + 1)] = 1.0
    shared = {
        "w1": w1p, "b1": A(mlp_b1.reshape(FILT, 1)),
        "w2": A(mlp_w2), "b2": A(mlp_b2.reshape(FILT, 1)),
        "w3": A(mlp_w3), "b3": A(mlp_b3.reshape(FILT, 1)),
        "freq_c": A(freq.reshape(1, FILT).T),
        "outb": A(out_b.reshape(8, 128).T),
        "wr_t": A(wr.T), "rb": A(rb.reshape(8, 1)), "sel8": sel,
        "ident32": np.eye(128, dtype=f32),
    }
    g9 = np.concatenate([wg_s[None], wg], 0)
    u9 = np.concatenate([wu_s[None], wu], 0)
    d9 = np.concatenate([wd_s[None], wd], 0)
    gq = g9.reshape(9, 4, 4, 128, 8, 128).transpose(0, 1, 5, 4, 2, 3).reshape(9, 4, 128, 4096)
    uq = u9.reshape(9, 4, 4, 128, 8, 128).transpose(0, 1, 5, 4, 2, 3).reshape(9, 4, 128, 4096)
    dq = d9.reshape(9, 1024, 4, 4, 128).transpose(0, 2, 4, 3, 1).reshape(9, 4, 128, 4096)
    shared["wfq"] = A(np.concatenate([gq, uq, dq], axis=3))
    maps = []
    for p in range(NCORES):
        sl = slice(128 * p, 128 * (p + 1))
        cols = np.r_[np.arange(128 * p, 128 * (p + 1)),
                     1024 + np.arange(128 * p, 128 * (p + 1)),
                     2048 + np.arange(128 * p, 128 * (p + 1))]
        m = dict(shared)
        m["x_sl"] = A(x[TPC * p:TPC * (p + 1)])
        m["inw_sl"] = A(in_w[:, cols])
        m["inb_sl"] = A(in_b[cols].reshape(3, 128).T)
        m["shw_sl"] = A(short_w[cols].reshape(3, 128, 3).transpose(1, 0, 2).reshape(128, 9))
        m["shb_sl"] = A(short_b[cols].reshape(3, 128).T)
        m["wo_sl"] = A(mlp_wo[:, sl])
        m["fb_sl"] = A(filt_bias[sl].reshape(128, 1))
        m["outw_sl"] = A(out_w[sl, :])
        m["chan_row"] = A(np.arange(128 * p, 128 * (p + 1), dtype=f32).reshape(1, 128))
        maps.append(m)
    return maps


def kernel(**inputs):
    nc = build_program()
    in_maps = _prep_inputs(inputs)
    res = run_bass_kernel_spmd(nc, in_maps, core_ids=list(range(NCORES)))
    parts = [np.asarray(r["out_fm"], dtype=np.float32).T for r in res.results]
    out = np.concatenate(parts, axis=0)
    return np.ascontiguousarray(out.reshape(2, LSEQ, D))


# version 21
# speedup vs baseline: 1.0207x; 1.0207x over previous
import numpy as np
from contextlib import ExitStack

import concourse.bass as bass
import concourse.mybir as mybir
from concourse.bass_utils import run_bass_kernel_spmd

F32 = mybir.dt.float32
BF16 = mybir.dt.bfloat16
I32 = mybir.dt.int32
AF = mybir.ActivationFunctionType
ALU = mybir.AluOpType
AX = mybir.AxisListType.X

NCORES = 8
D = 1024
LSEQ = 8192
NTOK = 16384
TPC = 2048
CPC = 128
FILT = 64
NFFT = 16384
NEXP = 8
DEBUG_DUMPS = False

ENGINES = ("tensor", "vector", "scalar", "gpsimd", "sync")


class W:
    def __init__(self, nc, ctx):
        self.nc = nc
        self.tape = {e: [] for e in ENGINES}
        self.sems = {}
        self.cnt = {}
        self.ctx = ctx

    def sem(self, name):
        if name not in self.sems:
            self.sems[name] = self.ctx.enter_context(self.nc.semaphore(name))
            self.cnt[name] = 0
        return name

    def op(self, eng, fn, waits=(), inc=None, amt=1):
        if inc is not None:
            self.sem(inc)
            self.cnt[inc] += amt
        self.tape[eng].append((tuple(waits), fn, inc, amt))
        return self.cnt[inc] if inc is not None else None

    def wait(self, eng, sem, val):
        self.tape[eng].append((((sem, val),), None, None, 0))

    def emit(self, block):
        for eng in ENGINES:
            deco = getattr(block, eng)
            tape = self.tape[eng]
            sems = self.sems

            def make(tape_):
                def body(b):
                    for waits, fn, inc, amt in tape_:
                        for s, v in waits:
                            if v > 0:
                                b.wait_ge(sems[s], v)
                        if fn is not None:
                            inst = fn(b)
                            if inc is not None:
                                inst.then_inc(sems[inc], amt)
                return body

            deco(make(tape))


def build_program():
    nc = bass.Bass("TRN2", target_bir_lowering=False)

    def din(name, shape, dtype=F32):
        return nc.dram_tensor(name, shape, dtype, kind="ExternalInput").ap()

    x_sl = din("x_sl", [TPC, D])
    inw_sl = din("inw_sl", [D, 384])
    inb_sl = din("inb_sl", [128, 3])
    shw_sl = din("shw_sl", [128, 9])
    shb_sl = din("shb_sl", [128, 3])
    w1_i = din("w1", [4, FILT])
    b1_i = din("b1", [FILT, 1])
    w2_i = din("w2", [FILT, FILT])
    b2_i = din("b2", [FILT, 1])
    w3_i = din("w3", [FILT, FILT])
    b3_i = din("b3", [FILT, 1])
    freq_i = din("freq_c", [FILT, 1])
    wo_i = din("wo_sl", [FILT, 128])
    fb_i = din("fb_sl", [128, 1])
    outw_i = din("outw_sl", [128, D])
    outb_i = din("outb", [128, 8])
    wr_i = din("wr_t", [D, 8])
    rb_i = din("rb", [8, 1])
    sel_i = din("sel8", [8, 1024])
    wfq_i = din("wfq", [9, 4, 128, 12288])
    chan_i = din("chan_row", [1, 128])
    id_i = din("ident32", [128, 128])

    out_fm = nc.dram_tensor("out_fm", [D, TPC], F32, kind="ExternalOutput").ap()

    xn_bnc = nc.dram_tensor("xn_bnc", [D, TPC], BF16).ap()
    g_xn = nc.dram_tensor("g_xn", [NCORES * D, TPC], BF16).ap()
    vx_loc = nc.dram_tensor("vx_loc", [CPC, NTOK], F32).ap()
    x0_loc = nc.dram_tensor("x0_loc", [CPC, NTOK], F32).ap()
    y_loc = nc.dram_tensor("y_loc", [CPC, NTOK], F32).ap()
    k_loc = nc.dram_tensor("k_loc", [CPC, LSEQ], F32).ap()
    p_buf = nc.dram_tensor("p_buf", [NCORES * D, TPC], F32).ap()
    attn_rs = nc.dram_tensor("attn_rs", [D, TPC], F32).ap()

    ctx = ExitStack()
    E = ctx.enter_context
    w = W(nc, ctx)

    xn_fm = E(nc.sbuf_tensor("xn_fm", [128, 8 * TPC], BF16))
    acc = E(nc.sbuf_tensor("acc", [128, 8 * TPC], F32))
    wq = [E(nc.sbuf_tensor(f"wq{i}", [128, 3 * 4096], BF16)) for i in range(2)]
    consts = E(nc.sbuf_tensor("consts", [128, 24], F32))
    ones_f32 = E(nc.sbuf_tensor("ones_f32", [1, 128], F32))
    ident_f = E(nc.sbuf_tensor("ident_f", [128, 128], F32))
    inb_sb = E(nc.sbuf_tensor("inb_sb", [128, 3], F32))
    shw_sb = E(nc.sbuf_tensor("shw_sb", [128, 9], F32))
    shb_sb = E(nc.sbuf_tensor("shb_sb", [128, 3], F32))
    fb_sb = E(nc.sbuf_tensor("fb_sb", [128, 1], F32))
    outb_sb = E(nc.sbuf_tensor("outb_sb", [128, 8], F32))
    wr_sb = E(nc.sbuf_tensor("wr_sb", [128, 64], F32))
    rb_sb = E(nc.sbuf_tensor("rb_sb", [8, 1], F32))
    sel_sb = E(nc.sbuf_tensor("sel_sb", [8, 1024], BF16))
    w1_sb = E(nc.sbuf_tensor("w1_sb", [4, FILT], F32))
    w2_sb = E(nc.sbuf_tensor("w2_sb", [FILT, FILT], F32))
    w3_sb = E(nc.sbuf_tensor("w3_sb", [FILT, FILT], F32))
    b1_sb = E(nc.sbuf_tensor("b1_sb", [FILT, 1], F32))
    b2_sb = E(nc.sbuf_tensor("b2_sb", [FILT, 1], F32))
    b3_sb = E(nc.sbuf_tensor("b3_sb", [FILT, 1], F32))
    freq_sb = E(nc.sbuf_tensor("freq_sb", [FILT, 1], F32))
    fbias_sb = [E(nc.sbuf_tensor(f"fbias{i}", [FILT, 1], F32)) for i in range(3)]
    wo_sb = E(nc.sbuf_tensor("wo_sb", [FILT, 128], F32))
    chan_sb = E(nc.sbuf_tensor("chan_sb", [1, 128], F32))
    cw_t = E(nc.sbuf_tensor("cw_t", [8, TPC], BF16))
    f1re = E(nc.sbuf_tensor("f1re", [128, 128], BF16))
    f1im = E(nc.sbuf_tensor("f1im", [128, 128], BF16))
    nf1im = E(nc.sbuf_tensor("nf1im", [128, 128], BF16))
    stk_re = E(nc.sbuf_tensor("stk_re", [128, 128], BF16))
    stk_im = E(nc.sbuf_tensor("stk_im", [128, 128], BF16))
    fi_re = E(nc.sbuf_tensor("fi_re", [128, 64], BF16))
    fi_ip = E(nc.sbuf_tensor("fi_ip", [128, 64], BF16))
    fi_in = E(nc.sbuf_tensor("fi_in", [128, 64], BF16))
    t_re = E(nc.sbuf_tensor("t_re", [128, 128], F32))
    t_im = E(nc.sbuf_tensor("t_im", [128, 128], F32))
    tw_s1 = E(nc.sbuf_tensor("tw_s1", [128, 2 * 512], F32))
    cwb_sb = E(nc.sbuf_tensor("cwb_sb", [128, 2048], F32))
    gs_sb = E(nc.sbuf_tensor("gs_sb", [128, 512], F32))
    us_sb = E(nc.sbuf_tensor("us_sb", [128, 512], F32))
    d_sb = [E(nc.sbuf_tensor(f"d_sb{i}", [128, 4 * 512], BF16)) for i in range(2)]

    bank = [E(nc.psum_tensor(f"bank{i}", [128, 512], F32)) for i in range(8)]
    A, Bk, C, Dk, Ebk, Fbk, G, H = bank

    CONSTS = {}

    def cdef(name, val):
        CONSTS[name] = (len(CONSTS), float(val))

    TWO_PI = 6.283185307179586
    C_HI = float(np.float32(TWO_PI))
    C_LO = float(np.float64(TWO_PI) - np.float64(np.float32(TWO_PI)))
    cdef("inv1024", 1.0 / 1024.0)
    cdef("eps", 1e-5)
    cdef("one", 1.0)
    cdef("neg1", -1.0)
    cdef("half", 0.5)
    cdef("neghalf", -0.5)
    cdef("inv2pi", 1.0 / TWO_PI)
    cdef("nchi", -C_HI)
    cdef("nclo", -C_LO)
    cdef("pihalf", float(np.pi / 2))
    cdef("inv128", 1.0 / 128.0)
    cdef("invN", 1.0 / NFFT)
    cdef("zero", 0.0)
    cdef("invL1", 1.0 / (LSEQ - 1.0))
    MIND = float(np.log(0.01) / 1.5)
    MAXD = float(np.log(0.01) / 0.3)
    cdef("mind", MIND)
    cdef("dslope", (MAXD - MIND) / (D - 1.0))
    POSC = TWO_PI * 1e-4 / LSEQ

    def cc(name, p=128):
        j = CONSTS[name][0]
        return consts[0:p, j:j + 1]

    for s in ("ld", "cst", "st", "wld", "pst", "ag", "rs", "pe", "se", "ve", "po"):
        w.sem(s)

    def dma_sync(dst, src, waits=()):
        return w.op("sync", lambda b, d=dst, s=src: b.dma_start(out=d, in_=s),
                    waits=waits, inc="ld", amt=16)

    def dma_st(dst, src, waits=()):
        return w.op("sync", lambda b, d=dst, s=src: b.dma_start(out=d, in_=s),
                    waits=waits, inc="st", amt=16)

    def dma_cast(dst, src, waits=()):
        return w.op("gpsimd", lambda b, d=dst, s=src: b.dma_start(out=d, in_=s),
                    waits=waits, inc="cst", amt=16)

    def dma_wt(dst, src, waits=()):
        return w.op("gpsimd", lambda b, d=dst, s=src: b.dma_start(out=d, in_=s),
                    waits=waits, inc="wld", amt=16)

    for name, (idx, val) in CONSTS.items():
        w.op("gpsimd", lambda b, i=idx, v=val: b.memset(consts[:, i:i + 1], v), inc="po")
    w.op("gpsimd", lambda b: b.memset(ones_f32[:, :], 1.0), inc="po")
    po_setup = w.cnt["po"]

    for dst, src in [
        (inb_sb[:, :], inb_sl), (shw_sb[:, :], shw_sl), (shb_sb[:, :], shb_sl),
        (fb_sb[:, :], fb_i), (outb_sb[:, :], outb_i), (rb_sb[:, :], rb_i),
        (w1_sb[:, :], w1_i), (w2_sb[:, :], w2_i), (w3_sb[:, :], w3_i),
        (b1_sb[:, :], b1_i), (b2_sb[:, :], b2_i), (b3_sb[:, :], b3_i),
        (freq_sb[:, :], freq_i), (wo_sb[:, :], wo_i), (chan_sb[:, :], chan_i),
        (ident_f[:, :], id_i),
        (wr_sb[:, :].rearrange("p (hc e) -> p hc e", e=8),
         wr_i.rearrange("(hc p) e -> p hc e", p=128)),
    ]:
        dma_sync(dst, src)
    ld_small = w.cnt["ld"]
    dma_cast(sel_sb[:, :], sel_i)
    cst_small = w.cnt["cst"]
    pref = {}
    for qq in range(2):
        dma_wt(wq[qq][:, :], wfq_i[0, qq])
        pref[(0, qq)] = w.cnt["wld"]

    ctxF = ExitStack()
    EF = ctxF.enter_context
    gscr = EF(nc.sbuf_tensor("gscr", [128, 128], F32))
    gscr2 = EF(nc.sbuf_tensor("gscr2", [128, 128], F32))
    gscri = EF(nc.sbuf_tensor("gscri", [128, 128], I32))
    irow = EF(nc.sbuf_tensor("irow", [1, 512], F32))
    zt = EF(nc.sbuf_tensor("zt", [4, 512], F32))
    zr = [EF(nc.sbuf_tensor("zr0", [1, 512], F32))]
    h_a = EF(nc.sbuf_tensor("h_a", [FILT, 512], F32))
    h_b = EF(nc.sbuf_tensor("h_b", [FILT, 512], F32))
    rrs = EF(nc.sbuf_tensor("rrs", [FILT, 4 * 512], F32))
    rri = EF(nc.sbuf_tensor("rri", [1, 512], I32))
    kt_sb = EF(nc.sbuf_tensor("kt_sb", [128, 512], F32))
    drow = EF(nc.sbuf_tensor("drow", [1, 128], F32))

    w.op("gpsimd", lambda b: b.iota(gscri[0:1, 0:128], [[1, 128]], base=0, channel_multiplier=0), inc="po")
    w.op("vector", lambda b: b.tensor_copy(gscr[0:1, 0:128], gscri[0:1, 0:128]),
         waits=[("po", w.cnt["po"])], inc="ve")
    ve_iota = w.cnt["ve"]
    pe_jk = w.op("tensor", lambda b: b.matmul(A[:, 0:128], gscr[0:1, 0:128],
                                              gscr[0:1, 0:128], start=True, stop=True),
                 waits=[("ve", ve_iota), ("ld", ld_small)], inc="pe")

    def gen_trig(div_const, cos_dst, sin_dst, extra=None, ncols=128, neg_tile=None):
        w.op("vector", lambda b: b.tensor_single_scalar(gscr[:, 0:ncols], A[:, 0:ncols],
                                                        cc(div_const), ALU.mult),
             waits=[("pe", pe_jk), ("se", w.cnt["se"])], inc="ve")
        w.op("vector", lambda b: b.tensor_copy(gscri[:, 0:ncols], gscr[:, 0:ncols]),
             waits=[("ve", w.cnt["ve"])], inc="ve")
        w.op("vector", lambda b: b.tensor_copy(gscr2[:, 0:ncols], gscri[:, 0:ncols]),
             waits=[("ve", w.cnt["ve"])], inc="ve")
        w.op("vector", lambda b: b.tensor_sub(gscr2[:, 0:ncols], gscr[:, 0:ncols],
                                              gscr2[:, 0:ncols]),
             waits=[("ve", w.cnt["ve"])], inc="ve")
        w.op("vector", lambda b: b.tensor_single_scalar(gscr[:, 0:ncols], gscr2[:, 0:ncols],
                                                        cc("half"), ALU.is_gt),
             waits=[("ve", w.cnt["ve"])], inc="ve")
        w.op("vector", lambda b: b.tensor_sub(gscr2[:, 0:ncols], gscr2[:, 0:ncols],
                                              gscr[:, 0:ncols]),
             waits=[("ve", w.cnt["ve"])], inc="ve")
        w.op("vector", lambda b: b.tensor_single_scalar(gscr[:, 0:ncols], gscr2[:, 0:ncols],
                                                        cc("neghalf"), ALU.is_lt),
             waits=[("ve", w.cnt["ve"])], inc="ve")
        w.op("vector", lambda b: b.tensor_add(gscr2[:, 0:ncols], gscr2[:, 0:ncols],
                                              gscr[:, 0:ncols]),
             waits=[("ve", w.cnt["ve"])], inc="ve")
        vf = w.cnt["ve"]
        sin_tmp = gscr[:, 0:ncols]
        w.op("scalar", lambda b: b.activation(sin_tmp, gscr2[:, 0:ncols], AF.Sin,
                                              bias=cc("zero"), scale=-TWO_PI),
             waits=[("ve", vf)], inc="se")
        se_s = w.cnt["se"]
        if extra is None:
            w.op("vector", lambda b: b.tensor_copy(sin_dst, sin_tmp),
                 waits=[("se", se_s)], inc="ve")
        else:
            w.op("vector", lambda b: b.tensor_single_scalar(sin_dst, sin_tmp,
                                                            cc(extra), ALU.mult),
                 waits=[("se", se_s)], inc="ve")
        if neg_tile is not None:
            w.op("vector", lambda b: b.tensor_single_scalar(neg_tile, sin_dst,
                                                            cc("neg1"), ALU.mult),
                 waits=[("ve", w.cnt["ve"])], inc="ve")
        ve_sin = w.cnt["ve"]
        w.op("vector", lambda b: b.tensor_single_scalar(gscr[:, 0:ncols], gscr2[:, 0:ncols],
                                                        cc("neg1"), ALU.mult),
             waits=[("ve", ve_sin)], inc="ve")
        w.op("vector", lambda b: b.tensor_max(gscr[:, 0:ncols], gscr[:, 0:ncols],
                                              gscr2[:, 0:ncols]),
             waits=[("ve", w.cnt["ve"])], inc="ve")
        va = w.cnt["ve"]
        cos_tmp = gscr2[:, 0:ncols]
        w.op("scalar", lambda b: b.activation(cos_tmp, gscr[:, 0:ncols], AF.Sin,
                                              bias=cc("pihalf"), scale=-TWO_PI),
             waits=[("ve", va)], inc="se")
        se_c = w.cnt["se"]
        if extra is None:
            w.op("vector", lambda b: b.tensor_copy(cos_dst, cos_tmp),
                 waits=[("se", se_c)], inc="ve")
        else:
            w.op("vector", lambda b: b.tensor_single_scalar(cos_dst, cos_tmp,
                                                            cc(extra), ALU.mult),
                 waits=[("se", se_c)], inc="ve")

    gen_trig("inv128", f1re[:, :], f1im[:, :], neg_tile=nf1im[:, :])
    gen_trig("invN", t_re[:, :], t_im[:, :])
    gen_trig("inv128", fi_re[:, :], fi_ip[:, :], extra="invN", ncols=64,
             neg_tile=fi_in[:, :])
    ve_t4 = w.cnt["ve"]
    dma_st(stk_re[0:64, :], f1re[0:64, :], waits=[("ve", ve_t4)])
    dma_st(stk_re[64:128, :], nf1im[0:64, :])
    dma_st(stk_im[0:64, :], f1im[0:64, :])
    dma_st(stk_im[64:128, :], f1re[0:64, :])
    st_stk = w.cnt["st"]

    w.op("vector", lambda b: b.tensor_scalar(drow[0:1, :], chan_sb[0:1, :],
                                             cc("dslope", 1), cc("mind", 1),
                                             ALU.mult, ALU.add),
         waits=[("ld", ld_small), ("ve", ve_t4)], inc="ve")
    for bsb, fdst in ((b1_sb, fbias_sb[0]), (b2_sb, fbias_sb[1]), (b3_sb, fbias_sb[2])):
        w.op("vector", lambda b, s=bsb, d_=fdst: b.tensor_mul(d_[0:FILT, :],
                                                              freq_sb[0:FILT, :],
                                                              s[0:FILT, :]),
             waits=[("ld", ld_small)], inc="ve")
    ve_fb = w.cnt["ve"]

    def rrsin(dst, src, scale_ap, bias_ap, waits):
        s0 = rrs[0:FILT, 0:512]
        s1 = rrs[0:FILT, 512:1024]
        s2 = rrs[0:FILT, 1024:1536]
        s3 = rrs[0:FILT, 1536:2048]
        i0 = rrs[0:FILT, :].bitcast(I32)[0:FILT, 1536:2048]
        w.op("vector", lambda b: b.tensor_scalar(s0, src, scale_ap, bias_ap,
                                                 ALU.mult, ALU.add), waits=waits, inc="ve")
        w.op("vector", lambda b: b.tensor_single_scalar(s1, s0, cc("inv2pi", FILT),
                                                        ALU.mult),
             waits=[("ve", w.cnt["ve"])], inc="ve")
        w.op("vector", lambda b: b.tensor_copy(i0, s1), waits=[("ve", w.cnt["ve"])], inc="ve")
        w.op("vector", lambda b: b.tensor_copy(s2, i0), waits=[("ve", w.cnt["ve"])], inc="ve")
        w.op("vector", lambda b: b.tensor_sub(s3, s1, s2), waits=[("ve", w.cnt["ve"])],
             inc="ve")
        w.op("vector", lambda b: b.tensor_single_scalar(s1, s3, cc("half", FILT), ALU.is_gt),
             waits=[("ve", w.cnt["ve"])], inc="ve")
        w.op("vector", lambda b: b.tensor_add(s2, s2, s1), waits=[("ve", w.cnt["ve"])],
             inc="ve")
        w.op("vector", lambda b: b.tensor_single_scalar(s1, s3, cc("neghalf", FILT),
                                                        ALU.is_lt),
             waits=[("ve", w.cnt["ve"])], inc="ve")
        w.op("vector", lambda b: b.tensor_sub(s2, s2, s1), waits=[("ve", w.cnt["ve"])],
             inc="ve")
        w.op("vector", lambda b: b.scalar_tensor_tensor(s1, s2, cc("nchi", FILT), s0,
                                                        ALU.mult, ALU.add),
             waits=[("ve", w.cnt["ve"])], inc="ve")
        w.op("vector", lambda b: b.scalar_tensor_tensor(s1, s2, cc("nclo", FILT), s1,
                                                        ALU.mult, ALU.add),
             waits=[("ve", w.cnt["ve"])], inc="ve")
        vm = w.cnt["ve"]
        w.op("scalar", lambda b: b.activation(dst, s1, AF.Sin, bias=cc("zero", FILT),
                                              scale=1.0),
             waits=[("ve", vm)], inc="se")
        return w.cnt["se"]

    flt = {"st": 0, "pe": 0, "se": 0}
    for t in range(16):
        base = 512 * t
        w.op("gpsimd", lambda b, bb=base: b.iota(rri[0:1, :], [[1, 512]], base=bb, channel_multiplier=0),
             waits=[("ve", flt.get("tile_ve", 0))], inc="po")
        po_i = w.cnt["po"]
        w.op("vector", lambda b: b.tensor_copy(irow[0:1, :], rri[0:1, :]),
             waits=[("po", po_i), ("pe", flt["pe"]), ("st", flt["st"]),
                    ("se", flt["se"])], inc="ve")
        w.op("vector", lambda b: b.tensor_single_scalar(zt[0:1, :], irow[0:1, :],
                                                        cc("invL1", 1), ALU.mult),
             waits=[("ve", w.cnt["ve"])], inc="ve")
        ve_tr = w.cnt["ve"]
        w.op("scalar", lambda b: b.activation(zr[0][0:1, :], irow[0:1, :], AF.Sin,
                                              bias=cc("pihalf", 1), scale=POSC),
             waits=[("ve", ve_tr)], inc="se")
        w.op("scalar", lambda b: b.activation(irow[0:1, :], irow[0:1, :], AF.Sin,
                                              bias=cc("zero", 1), scale=-POSC),
             waits=[("ve", ve_tr), ("se", w.cnt["se"])], inc="se")
        se_z = w.cnt["se"]
        dma_st(zt[1:2, :], zr[0][0:1, :], waits=[("se", se_z)])
        dma_st(zt[2:3, :], irow[0:1, :], waits=[("se", se_z)])
        st_z = w.cnt["st"]
        pe1 = w.op("tensor", lambda b: b.matmul(Ebk[0:FILT, :], w1_sb[0:3, :], zt[0:3, :],
                                                start=True, stop=True),
                   waits=[("st", st_z), ("ld", ld_small), ("ve", w.cnt["ve"]),
                          ("se", w.cnt["se"])], inc="pe")
        se1 = rrsin(h_a[0:FILT, :], Ebk[0:FILT, 0:512], freq_sb[0:FILT, :],
                    fbias_sb[0][0:FILT, :], [("pe", pe1), ("ve", ve_fb)])
        pe2 = w.op("tensor", lambda b: b.matmul(Fbk[0:FILT, :], w2_sb[:, :], h_a[0:FILT, :],
                                                start=True, stop=True),
                   waits=[("se", se1)], inc="pe")
        se2 = rrsin(h_b[0:FILT, :], Fbk[0:FILT, 0:512], freq_sb[0:FILT, :],
                    fbias_sb[1][0:FILT, :], [("pe", pe2)])
        pe3 = w.op("tensor", lambda b: b.matmul(Ebk[0:FILT, :], w3_sb[:, :], h_b[0:FILT, :],
                                                start=True, stop=True),
                   waits=[("se", se2)], inc="pe")
        se3 = rrsin(h_a[0:FILT, :], Ebk[0:FILT, 0:512], freq_sb[0:FILT, :],
                    fbias_sb[2][0:FILT, :], [("pe", pe3)])
        pe4 = w.op("tensor", lambda b: b.matmul(G[:, :], wo_sb[0:FILT, :], h_a[0:FILT, :],
                                                start=True, stop=True),
                   waits=[("se", se3)], inc="pe")
        pe5 = w.op("tensor", lambda b: b.matmul(H[:, :], drow[0:1, :], zt[0:1, :],
                                                start=True, stop=True),
                   waits=[("ve", ve_tr)], inc="pe")
        w.op("scalar", lambda b: b.activation(kt_sb[:, :], H[:, :], AF.Exp,
                                              bias=cc("zero"), scale=1.0),
             waits=[("pe", pe5), ("ve", flt.get("kt_ve", 0)), ("st", flt["st"])],
             inc="se")
        se_dec = w.cnt["se"]
        w.op("vector", lambda b: b.tensor_mul(kt_sb[:, :], G[:, :], kt_sb[:, :]),
             waits=[("pe", pe4), ("se", se_dec)], inc="ve")
        flt["kt_ve"] = w.cnt["ve"]
        dma_st(k_loc[:, base:base + 512], kt_sb[:, :], waits=[("ve", flt["kt_ve"])])
        flt["st"] = w.cnt["st"]
        flt["pe"] = w.cnt["pe"]
        flt["se"] = w.cnt["se"]
        flt["tile_ve"] = w.cnt["ve"]
    st_filter = w.cnt["st"]
    ctxF.close()

    ffn = {}

    def ffn_entry():
        for kk, ss in (("gu_se", "se"), ("u_ve", "ve"), ("d_po", "po"),
                       ("dn_ve0", "ve"), ("dn_ve1", "ve"), ("dn_pe", "pe"),
                       ("acc_se", "se"), ("cwb_ve", "ve"), ("cwb_se", "se")):
            ffn[kk] = max(ffn.get(kk, 0), w.cnt[ss])

    def emit_down(wt, tok_, po_d_, dp, wpar_, acc_w=None):
        for oc in range(8):
            bk = C if oc % 2 == 0 else Dk
            key = f"dn_ve{oc % 2}"
            pe_d = None
            for ic in range(4):
                pe_d = w.op("tensor", lambda b, i_=ic, o=oc, wt_=wt, bb=bk, d_=dp:
                            b.matmul(bb[:, 0:512],
                                     wt_[:, 8192 + 1024 * i_ + 128 * o:8192 + 1024 * i_ + 128 * o + 128],
                                     d_sb[d_][:, 512 * i_:512 * i_ + 512],
                                     start=(i_ == 0), stop=(i_ == 3)),
                            waits=([("po", po_d_), ("ve", ffn[key])]
                                   + ([("se", ffn["cwb_se"])] if oc == 1 else [])
                                   if ic == 0 else []), inc="pe")
            asl = acc[:, 2048 * oc + tok_:2048 * oc + tok_ + 512]
            w.op("vector", lambda b, a_=asl, bb=bk: b.tensor_add(
                a_, bb[:, 0:512], a_),
                waits=[("pe", pe_d),
                       acc_w if acc_w else ("se", ffn["acc_se"])], inc="ve")
            ffn[key] = w.cnt["ve"]
        ffn["dn_pe"] = w.cnt["pe"]
        ffn[f"wq_pe{wpar_}"] = w.cnt["pe"]

    def emit_load(k):
        ei_, q_ = divmod(k, 4)
        par_ = k % 2
        prev = ffn.get(f"wq_pe{par_}", 0)
        dma_wt(wq[par_][:, :], wfq_i[ei_, q_],
               waits=[("pe", prev)] if prev else [])
        pref[(ei_, q_)] = w.cnt["wld"]

    def ffn_expert_units(e, gates=None):
        if gates is None:
            ffn_entry()
        if e >= 0:
            for t4 in range(4):
                pe_cb = w.op("tensor", lambda b, ee=e, tt=512 * t4:
                             b.matmul(Dk[:, 0:512],
                                      sel_sb[0:8, 128 * ee:128 * ee + 128],
                                      cw_t[0:8, tt:tt + 512], start=True, stop=True),
                             waits=[("ve", ffn["dn_ve1"]), ("se", ffn["cwb_se"])],
                             inc="pe")
                w.op("scalar", lambda b, tt=512 * t4: b.copy(cwb_sb[:, tt:tt + 512],
                                                             H[:, 0:512] if False else Dk[:, 0:512]),
                     waits=[("pe", pe_cb)], inc="se")
                ffn["cwb_se"] = w.cnt["se"]
        ei = e + 1
        for q in range(4):
            k = 4 * ei + q
            par = k % 2
            wqt = wq[par]
            if (ei, q) not in pref:
                emit_load(k)
            cst_w = pref.pop((ei, q))
            for t in range(4):
                tok = 512 * t
                gwts = [("wld", cst_w), ("se", ffn["gu_se"])]
                if gates is not None:
                    gwts.append(("ve", gates[0][4 * t + 3]))
                for ic in range(4):
                    pe_g = None
                    for hc in range(8):
                        pe_g = w.op("tensor", lambda b, i_=ic, h=hc, tt=tok, wt=wqt:
                                    b.matmul(A[:, 0:512],
                                             wt[:, 512 * h + 128 * i_:512 * h + 128 * i_ + 128],
                                             xn_fm[:, 2048 * h + tt:2048 * h + tt + 512],
                                             start=(h == 0), stop=(h == 7)),
                                    waits=(gwts if hc == 0 and ic == 0 else
                                           [("se", ffn["gu_se"])] if hc == 0 else []),
                                    inc="pe")
                    pe_u = None
                    for hc in range(8):
                        pe_u = w.op("tensor", lambda b, i_=ic, h=hc, tt=tok, wt=wqt:
                                    b.matmul(Bk[:, 0:512],
                                             wt[:, 4096 + 512 * h + 128 * i_:4096 + 512 * h + 128 * i_ + 128],
                                             xn_fm[:, 2048 * h + tt:2048 * h + tt + 512],
                                             start=(h == 0), stop=(h == 7)),
                                    waits=([("ve", ffn["u_ve"])] if hc == 0 else []),
                                    inc="pe")
                    w.op("scalar", lambda b: b.activation(gs_sb[:, :], A[:, 0:512],
                                                          AF.Silu, bias=cc("zero"),
                                                          scale=1.0),
                         waits=[("pe", pe_g), ("po", ffn["d_po"])], inc="se")
                    ffn["gu_se"] = w.cnt["se"]
                    if e >= 0:
                        w.op("vector", lambda b, tt=tok:
                             b.tensor_mul(us_sb[:, :], Bk[:, 0:512],
                                          cwb_sb[:, tt:tt + 512]),
                             waits=[("pe", pe_u), ("se", ffn["cwb_se"]),
                                    ("po", ffn["d_po"])], inc="ve")
                    else:
                        w.op("vector", lambda b: b.tensor_copy(us_sb[:, :], Bk[:, 0:512]),
                             waits=[("pe", pe_u), ("po", ffn["d_po"])], inc="ve")
                    ffn["u_ve"] = w.cnt["ve"]
                    w.op("gpsimd", lambda b, i_=ic, dp=t % 2:
                         b.tensor_mul(d_sb[dp][:, 512 * i_:512 * i_ + 512], gs_sb[:, :],
                                      us_sb[:, :]),
                         waits=[("se", ffn["gu_se"]), ("ve", ffn["u_ve"]),
                                ("pe", ffn["dn_pe"])], inc="po")
                    ffn["d_po"] = w.cnt["po"]
                po_d = w.cnt["po"]
                if "prev_pend" in ffn:
                    emit_down(*ffn.pop("prev_pend"))
                if t == 0 and k + 1 <= 35:
                    ei2, q2 = divmod(k + 1, 4)
                    if (ei2, q2) not in pref:
                        emit_load(k + 1)
                accw = ("se", gates[1][4 * t + 3]) if gates is not None else None
                ffn["prev_pend"] = (wqt, tok, po_d, t % 2, par, accw)
                ffn["cwb_ve"] = ffn["u_ve"]
                yield

    ffn_entry()
    ctx0 = ExitStack()
    E0 = ctx0.enter_context
    x_tm = [E0(nc.sbuf_tensor(f"x_tm{i}", [128, D], F32)) for i in range(2)]
    xn_tm = [E0(nc.sbuf_tensor(f"xn_tm{i}", [128, D], F32)) for i in range(2)]
    ln_s = [E0(nc.sbuf_tensor(f"ln_s{i}", [128, 8], F32)) for i in range(2)]
    xnf_t = E0(nc.sbuf_tensor("xnf_t", [128, 256], F32))
    top_sb = E0(nc.sbuf_tensor("top_sb", [128, 48], F32))
    pr_sb = E0(nc.sbuf_tensor("pr_sb", [8, 128], F32))

    ph0 = {"sig_se": w.cnt["se"], "cw_ve": w.cnt["ve"],
           "pe0": w.cnt["pe"], "pe1": w.cnt["pe"],
           "se0": w.cnt["se"], "se1": w.cnt["se"],
           "ve0": w.cnt["ve"], "ve1": w.cnt["ve"]}
    ph0_tile_ve = []
    ph0_tile_se = []

    def ph0_tile(i):
        bb2 = i % 2
        xt, xnt, lns = x_tm[bb2], xn_tm[bb2], ln_s[bb2]
        ldx = dma_sync(xt[:, :], x_sl[128 * i:128 * (i + 1), :],
                       waits=([("st", st_filter)] if i < 2 else []) +
                       [("pe", ph0[f"pe{bb2}"]), ("se", ph0[f"se{bb2}"]),
                        ("ve", ph0[f"ve{bb2}"])])
        w.op("vector", lambda b, xt=xt, lns=lns:
             b.reduce_sum(lns[:, 0:1], xt[:, :], axis=AX),
             waits=[("ld", ldx)], inc="ve")
        w.op("scalar", lambda b, xt=xt, xnt=xnt: b.square(xnt[:, :], xt[:, :]),
             waits=[("ld", ldx), ("pe", ph0[f"pe{bb2}"]), ("ve", w.cnt["ve"])],
             inc="se")
        se_sq = w.cnt["se"]
        w.op("vector", lambda b, xnt=xnt, lns=lns:
             b.reduce_sum(lns[:, 1:2], xnt[:, :], axis=AX),
             waits=[("se", se_sq), ("ve", w.cnt["ve"])], inc="ve")
        w.op("vector", lambda b, lns=lns: b.tensor_single_scalar(
            lns[:, 0:1], lns[:, 0:1], cc("inv1024"), ALU.mult),
             waits=[("ve", w.cnt["ve"])], inc="ve")
        w.op("vector", lambda b, lns=lns: b.tensor_mul(lns[:, 2:3], lns[:, 0:1],
                                                       lns[:, 0:1]),
             waits=[("ve", w.cnt["ve"])], inc="ve")
        w.op("vector", lambda b, lns=lns: b.scalar_tensor_tensor(
            lns[:, 2:3], lns[:, 1:2], cc("inv1024"), lns[:, 2:3],
            ALU.mult, ALU.subtract),
             waits=[("ve", w.cnt["ve"])], inc="ve")
        w.op("scalar", lambda b, lns=lns: b.activation(lns[:, 3:4], lns[:, 2:3],
                                                       AF.Sqrt, bias=cc("eps"),
                                                       scale=1.0),
             waits=[("ve", w.cnt["ve"])], inc="se")
        w.op("vector", lambda b, lns=lns: b.reciprocal(lns[:, 4:5], lns[:, 3:4]),
             waits=[("se", w.cnt["se"])], inc="ve")
        w.op("vector", lambda b, lns=lns: b.tensor_mul(lns[:, 5:6], lns[:, 0:1],
                                                       lns[:, 4:5]),
             waits=[("ve", w.cnt["ve"])], inc="ve")
        w.op("vector", lambda b, xt=xt, xnt=xnt, lns=lns: b.tensor_scalar(
            xnt[:, :], xt[:, :], lns[:, 4:5], lns[:, 5:6], ALU.mult, ALU.subtract),
             waits=[("ve", w.cnt["ve"]), ("se", se_sq)], inc="ve")
        ve_xn = w.cnt["ve"]
        ph0[f"ve{bb2}"] = ve_xn
        for c8 in range(8):
            bx = G
            by = H
            xs = 128 * c8
            pe_t1 = w.op("tensor", lambda b, bb=bx, s=xs, xnt=xnt:
                         b.transpose(bb[:, 0:128], xnt[:, s:s + 128], ident_f[:, :]),
                         waits=[("ve", ve_xn), ("se", w.cnt["se"])], inc="pe")
            pe_t2 = w.op("tensor", lambda b, bb=by, s=xs, xt=xt:
                         b.transpose(bb[:, 0:128], xt[:, s:s + 128], ident_f[:, :]),
                         waits=[("se", w.cnt["se"])], inc="pe")
            o = (c8 % 2) * 128
            w.op("scalar", lambda b, bb=bx, oo=o: b.copy(xnf_t[:, oo:oo + 128],
                                                         bb[:, 0:128]),
                 waits=[("pe", pe_t1), ("ve", w.cnt["ve"])], inc="se")
            se_e1 = w.cnt["se"]
            w.op("scalar", lambda b, bb=by, c=c8, ti=i:
                 b.copy(acc[:, 2048 * c + 128 * ti:2048 * c + 128 * ti + 128],
                        bb[:, 0:128]),
                 waits=[("pe", pe_t2)], inc="se")
            w.op("vector", lambda b, c=c8, ti=i, oo=o:
                 b.tensor_copy(xn_fm[:, 2048 * c + 128 * ti:2048 * c + 128 * ti + 128],
                               xnf_t[:, oo:oo + 128]),
                 waits=[("se", se_e1)], inc="ve")
            w.op("tensor", lambda b, c=c8, oo=o:
                 b.matmul(Ebk[0:8, 0:128], wr_sb[:, 8 * c:8 * c + 8],
                          xnf_t[:, oo:oo + 128], start=(c == 0), stop=(c == 7)),
                 waits=([("se", se_e1), ("ve", ve_xn)] if c8 > 0 else
                        [("se", se_e1), ("ve", ve_xn), ("se", ph0["sig_se"])]),
                 inc="pe")
        pe_r = w.cnt["pe"]
        w.op("scalar", lambda b: b.activation(pr_sb[0:8, :], Ebk[0:8, 0:128], AF.Sigmoid,
                                              bias=rb_sb[0:8, :], scale=1.0),
             waits=[("pe", pe_r)], inc="se")
        ph0["sig_se"] = w.cnt["se"]
        pe_tp = w.op("tensor", lambda b: b.transpose(Fbk[0:128, 0:8], pr_sb[0:8, :],
                                                     ident_f[0:8, 0:8]),
                     waits=[("se", ph0["sig_se"]), ("ve", ph0["cw_ve"])], inc="pe")
        w.op("vector", lambda b: b.tensor_copy(top_sb[:, 0:8], Fbk[0:128, 0:8]),
             waits=[("pe", pe_tp)], inc="ve")
        w.op("vector", lambda b: b.reduce_max(top_sb[:, 8:9], top_sb[:, 0:8], axis=AX),
             waits=[("ve", w.cnt["ve"])], inc="ve")
        w.op("vector", lambda b: b.tensor_single_scalar(top_sb[:, 16:24], top_sb[:, 0:8],
                                                        top_sb[:, 8:9], ALU.is_ge),
             waits=[("ve", w.cnt["ve"])], inc="ve")
        w.op("vector", lambda b: b.tensor_scalar(top_sb[:, 24:32], top_sb[:, 16:24],
                                                 cc("neg1"), cc("one"), ALU.mult, ALU.add),
             waits=[("ve", w.cnt["ve"])], inc="ve")
        w.op("vector", lambda b: b.tensor_mul(top_sb[:, 24:32], top_sb[:, 0:8],
                                              top_sb[:, 24:32]),
             waits=[("ve", w.cnt["ve"])], inc="ve")
        w.op("vector", lambda b: b.reduce_max(top_sb[:, 9:10], top_sb[:, 24:32], axis=AX),
             waits=[("ve", w.cnt["ve"])], inc="ve")
        w.op("vector", lambda b: b.tensor_single_scalar(top_sb[:, 32:40], top_sb[:, 24:32],
                                                        top_sb[:, 9:10], ALU.is_ge),
             waits=[("ve", w.cnt["ve"])], inc="ve")
        w.op("vector", lambda b: b.tensor_add(top_sb[:, 16:24], top_sb[:, 16:24],
                                              top_sb[:, 32:40]),
             waits=[("ve", w.cnt["ve"])], inc="ve")
        w.op("vector", lambda b: b.tensor_mul(top_sb[:, 16:24], top_sb[:, 0:8],
                                              top_sb[:, 16:24]),
             waits=[("ve", w.cnt["ve"])], inc="ve")
        ve_cw = w.cnt["ve"]
        pe_tb = w.op("tensor", lambda b: b.transpose(Fbk[0:8, 0:128], top_sb[0:128, 16:24],
                                                     ident_f[:, :]),
                     waits=[("ve", ve_cw)], inc="pe")
        w.op("vector", lambda b, ti=i: b.tensor_copy(cw_t[0:8, 128 * ti:128 * ti + 128],
                                                     Fbk[0:8, 0:128]),
             waits=[("pe", pe_tb)], inc="ve")
        ph0["cw_ve"] = w.cnt["ve"]
        ph0[f"pe{bb2}"] = w.cnt["pe"]
        ph0[f"se{bb2}"] = w.cnt["se"]
        ph0_tile_ve.append(w.cnt["ve"])
        ph0_tile_se.append(w.cnt["se"])

    gen_sh = ffn_expert_units(-1, gates=(ph0_tile_ve, ph0_tile_se))
    for i in range(16):
        ph0_tile(i)
        if i % 4 == 3 and i < 15:
            next(gen_sh)
    ve_ph0 = w.cnt["ve"]

    st_xn = dma_st(xn_bnc.rearrange("(hc p) t -> p hc t", p=128),
                   xn_fm[:, :].rearrange("p (hc t) -> p hc t", t=TPC),
                   waits=[("ve", ve_ph0)])
    w.op("gpsimd", lambda b: b.collective_compute(
        "AllGather", ALU.bypass, replica_groups=[list(range(NCORES))],
        ins=[xn_bnc.opt()], outs=[g_xn.opt()]),
        waits=[("st", st_xn)], inc="ag")
    for _ in gen_sh:
        pass
    ctx0.close()

    ctxA = ExitStack()
    EA = ctxA.enter_context
    inw_sb = EA(nc.sbuf_tensor("inw_sb", [128, 3072], BF16))
    xgh = [EA(nc.sbuf_tensor(f"xgh{i}", [128, 4 * 512], BF16)) for i in range(2)]
    up_sb = EA(nc.sbuf_tensor("up_sb", [128, 3 * 514], F32))
    uc2 = EA(nc.sbuf_tensor("uc2", [128, 2 * 512], F32))
    cscr = EA(nc.sbuf_tensor("cscr", [128, 512], F32))

    dma_cast(inw_sb[:, :].rearrange("p (hc i) -> p hc i", i=384),
             inw_sl.rearrange("(hc p) i -> p hc i", p=128),
             waits=[("st", st_filter), ("ag", 1)])
    cst_inw = w.cnt["cst"]
    pa = {"spill_st": w.cnt["st"],
          "xh_pe0": w.cnt["pe"], "xh_pe1": w.cnt["pe"]}
    for oc in range(3):
        pa[f"conv_{oc}"] = ("po" if oc == 1 else "ve", w.cnt["po" if oc == 1 else "ve"])
        pa[f"up_se{oc}"] = w.cnt["se"]
        w.op("gpsimd", lambda b, o=oc: b.memset(up_sb[:, 514 * o:514 * o + 2], 0.0),
             waits=[("st", st_filter), ("ag", 1)], inc="po")
    pa["po_hist"] = w.cnt["po"]

    def phaseA_tile(t):
        r = t // 4
        colc = 512 * (t % 4)
        ldh = []
        for half in range(2):
            ldh.append(w.op("sync", lambda b, hf=half, rr=r, cl=colc: b.dma_start(
                out=xgh[hf][:, :].rearrange("p (hc n) -> p hc n", n=512),
                in_=g_xn[1024 * rr + 512 * hf:1024 * rr + 512 * hf + 512,
                         cl:cl + 512].rearrange("(hc p) n -> p hc n", p=128)),
                waits=[("ag", 1), ("pe", pa[f"xh_pe{half}"])], inc="ld", amt=16))
        pe_last = None
        for hc in range(8):
            half = hc // 4
            xsl = 512 * (hc % 4)
            for oc in range(3):
                bk = (Ebk, Fbk, G)[oc]
                wts = []
                if hc == 0 and oc == 0:
                    wts = [("ld", ldh[0]), ("cst", cst_inw), ("po", pa["po_hist"]),
                           ("se", max(pa["up_se0"], pa["up_se1"], pa["up_se2"])),
                           pa["conv_0"], pa["conv_1"], pa["conv_2"]]
                elif hc == 4 and oc == 0:
                    wts = [("ld", ldh[1])]
                pe_last = w.op("tensor", lambda b, h=hc, o=oc, hf=half, xs=xsl, bb=bk:
                               b.matmul(bb[:, 0:512],
                                        inw_sb[:, 384 * h + 128 * o:384 * h + 128 * o + 128],
                                        xgh[hf][:, xs:xs + 512],
                                        start=(h == 0), stop=(h == 7)),
                               waits=wts, inc="pe")
            if hc == 3:
                pa["xh_pe0"] = w.cnt["pe"]
        pa["xh_pe1"] = w.cnt["pe"]
        if t == 16:
            pass
        for oc in range(3):
            bk = (Ebk, Fbk, G)[oc]
            if t == 16 and oc == 0:
                for o3 in range(3):
                    w.op("gpsimd", lambda b, o=o3:
                         b.memset(up_sb[:, 514 * o:514 * o + 2], 0.0),
                         waits=[pa[f"conv_{o3}"]], inc="po")
                pa["po_hist"] = w.cnt["po"]
            w.op("scalar", lambda b, o=oc, bb=bk:
                 b.activation(up_sb[:, 514 * o + 2:514 * o + 514], bb[:, 0:512],
                              AF.Identity, bias=inb_sb[:, o:o + 1], scale=1.0),
                 waits=[("pe", pe_last), ("po", pa["po_hist"]), pa[f"conv_{oc}"]],
                 inc="se")
            pa[f"up_se{oc}"] = w.cnt["se"]
            u0 = up_sb[:, 514 * oc + 0:514 * oc + 512]
            u1 = up_sb[:, 514 * oc + 1:514 * oc + 513]
            u2 = up_sb[:, 514 * oc + 2:514 * oc + 514]
            eng = "vector"
            sem = "ve"
            stage1 = cscr[:, :] if oc < 2 else tw_s1[:, 0:512]
            dst = uc2[:, 512 * oc:512 * oc + 512] if oc < 2 else cscr[:, :]
            w.op(eng, lambda b, a_=u0, o=oc, s_=stage1:
                 b.tensor_scalar(s_, a_, shw_sb[:, 3 * o:3 * o + 1], shb_sb[:, o:o + 1],
                                 ALU.mult, ALU.add),
                 waits=[("se", pa[f"up_se{oc}"]), ("st", pa["spill_st"])], inc=sem)
            w.op(eng, lambda b, a_=u1, o=oc, dd=dst, s_=stage1:
                 b.scalar_tensor_tensor(dd, a_, shw_sb[:, 3 * o + 1:3 * o + 2], s_,
                                        ALU.mult, ALU.add), inc=sem)
            w.op(eng, lambda b, a_=u2, o=oc, dd=dst:
                 b.scalar_tensor_tensor(dd, a_, shw_sb[:, 3 * o + 2:3 * o + 3], dd,
                                        ALU.mult, ALU.add), inc=sem)
            w.op(eng, lambda b, o=oc:
                 b.tensor_copy(up_sb[:, 514 * o:514 * o + 2],
                               up_sb[:, 514 * o + 512:514 * o + 514]), inc=sem)
            pa[f"conv_{oc}"] = (sem, w.cnt[sem])
        w.op("vector", lambda b: b.tensor_mul(uc2[:, 512:1024], cscr[:, :],
                                              uc2[:, 512:1024]),
             waits=[("ve", w.cnt["ve"]), pa["conv_1"]], inc="ve")
        ve_vx = w.cnt["ve"]
        dma_st(vx_loc[:, 512 * t:512 * t + 512], uc2[:, 512:1024], waits=[("ve", ve_vx)])
        dma_st(x0_loc[:, 512 * t:512 * t + 512], uc2[:, 0:512], waits=[("ve", ve_vx)])
        pa["spill_st"] = w.cnt["st"]
    def run_expert(e, fillers, per_unit):
        for _ in ffn_expert_units(e):
            n = per_unit
            while n > 0 and fillers:
                fillers.pop(0)()
                n -= 1

    def flush(fillers):
        while fillers:
            fillers.pop(0)()

    fillA = [(lambda tt: (lambda: phaseA_tile(tt)))(t) for t in range(32)]
    run_expert(0, fillA, 2)
    flush(fillA)
    st_phA = w.cnt["st"]
    ctxA.close()

    ctxB = ExitStack()
    EB = ctxB.enter_context
    vx_st = [EB(nc.sbuf_tensor(f"vx_st{i}", [128, 512], BF16)) for i in range(2)]
    k_st = [EB(nc.sbuf_tensor(f"k_st{i}", [128, 512], BF16)) for i in range(2)]
    ap_re = EB(nc.sbuf_tensor("ap_re", [128, 512], BF16))
    ap_im = EB(nc.sbuf_tensor("ap_im", [128, 512], BF16))
    xk_re = EB(nc.sbuf_tensor("xk_re", [128, 512], F32))
    xk_im = EB(nc.sbuf_tensor("xk_im", [128, 512], F32))
    w_re = EB(nc.sbuf_tensor("w_re", [128, 512], BF16))
    w_im = EB(nc.sbuf_tensor("w_im", [128, 512], BF16))
    gp_re = EB(nc.sbuf_tensor("gp_re", [128, 512], BF16))
    gp_im = EB(nc.sbuf_tensor("gp_im", [128, 512], BF16))
    y_sb = EB(nc.sbuf_tensor("y_sb", [64, 2 * 512], F32))

    fft = {
        "s1_ve0": w.cnt["ve"], "s1_ve1": w.cnt["ve"],
        "x_ve0": w.cnt["ve"], "x_ve1": w.cnt["ve"], "s1_ve": w.cnt["ve"],
        "x_ve": w.cnt["ve"], "y_se": w.cnt["se"],
        "ap_pe": w.cnt["pe"], "gp_pe": w.cnt["pe"], "w_pe": w.cnt["pe"],
        "st_pe0": w.cnt["pe"], "st_pe1": w.cnt["pe"],
        "prod_ve": w.cnt["ve"], "y_st": w.cnt["st"], "xk_ve": w.cnt["ve"],
        "ent_se": w.cnt["se"], "y_se0": w.cnt["se"], "y_se1": w.cnt["se"],
    }
    w.op("gpsimd", lambda b: b.memset(k_st[0][64:128, :], 0.0),
         waits=[("st", w.cnt["st"]), ("ve", w.cnt["ve"])], inc="po")
    w.op("gpsimd", lambda b: b.memset(k_st[1][64:128, :], 0.0), inc="po")
    po_kz = w.cnt["po"]

    tre_b = t_re[:, :].unsqueeze(1).broadcast_to((128, 4, 128))
    tim_b = t_im[:, :].unsqueeze(1).broadcast_to((128, 4, 128))

    def v3(ap):
        return ap.rearrange("p (c jf) -> p c jf", jf=128)

    def twiddle(dst_re, dst_im, src_re, src_im, conj, wait_pe, extra=()):
        sc1 = tw_s1[:, 0:512]
        sc2 = tw_s1[:, 512:1024]
        w.op("vector", lambda b: b.tensor_mul(v3(sc1), v3(src_re), tre_b),
             waits=[("pe", wait_pe)] + list(extra), inc="ve")
        w.op("vector", lambda b: b.tensor_mul(v3(sc2), v3(src_im), tim_b),
             waits=[("ve", w.cnt["ve"])], inc="ve")
        if conj:
            w.op("vector", lambda b: b.tensor_add(dst_re, sc1, sc2),
                 waits=[("ve", w.cnt["ve"])], inc="ve")
        else:
            w.op("vector", lambda b: b.tensor_sub(dst_re, sc1, sc2),
                 waits=[("ve", w.cnt["ve"])], inc="ve")
        w.op("vector", lambda b: b.tensor_mul(v3(sc1), v3(src_re), tim_b),
             waits=[("ve", w.cnt["ve"])], inc="ve")
        w.op("vector", lambda b: b.tensor_mul(v3(sc2), v3(src_im), tre_b),
             waits=[("ve", w.cnt["ve"])], inc="ve")
        if conj:
            w.op("vector", lambda b: b.tensor_sub(dst_im, sc2, sc1),
                 waits=[("ve", w.cnt["ve"])], inc="ve")
        else:
            w.op("vector", lambda b: b.tensor_add(dst_im, sc1, sc2),
                 waits=[("ve", w.cnt["ve"])], inc="ve")
        return w.cnt["ve"]

    def fft_group(g):
        c0 = 4 * g
        par = g % 2
        s1r, s1i = Ebk, Fbk
        sXr, sXi = G, H
        vst, kst = vx_st[par], k_st[par]
        dma_cast(vst[0:64, :].rearrange("js (c jf) -> js c jf", jf=128),
                 vx_loc[c0:c0 + 4, 0:LSEQ].rearrange("c (js jf) -> js c jf", jf=128),
                 waits=[("st", fft["phA_st"]), ("pe", fft[f"st_pe{par}"])])
        dma_cast(vst[64:128, :].rearrange("js (c jf) -> js c jf", jf=128),
                 vx_loc[c0:c0 + 4, LSEQ:NTOK].rearrange("c (js jf) -> js c jf", jf=128))
        dma_cast(kst[0:64, :].rearrange("js (c jf) -> js c jf", jf=128),
                 k_loc[c0:c0 + 4, :].rearrange("c (js jf) -> js c jf", jf=128),
                 waits=[("st", st_filter), ("po", po_kz)])
        cst_g = w.cnt["cst"]
        pe_s1 = None
        for c in range(4):
            pe_s1 = w.op("tensor", lambda b, c_=c, bb=s1r, v=vst:
                         b.matmul(bb[:, 128 * c_:128 * c_ + 128],
                                  v[:, 128 * c_:128 * c_ + 128], stk_re[:, :],
                                  start=True, stop=True),
                         waits=([("cst", cst_g), ("st", st_stk),
                                 ("ve", fft["s1_ve"]), ("se", fft["ent_se"])]
                                if c == 0 else []), inc="pe")
            pe_s1 = w.op("tensor", lambda b, c_=c, bb=s1i, v=vst:
                         b.matmul(bb[:, 128 * c_:128 * c_ + 128],
                                  v[:, 128 * c_:128 * c_ + 128], stk_im[:, :],
                                  start=True, stop=True), inc="pe")
        ve_tw1 = twiddle(ap_re[:, :], ap_im[:, :], s1r[:, 0:512], s1i[:, 0:512],
                         conj=False, wait_pe=pe_s1, extra=[("pe", fft["ap_pe"])])
        fft["s1_ve"] = ve_tw1
        w.op("tensor", lambda b, bb=sXr: b.matmul(bb[:, 0:512], f1re[:, :], ap_re[:, :],
                                                  start=True, stop=False),
             waits=[("ve", ve_tw1), ("ve", fft["x_ve"]),
                    ("se", fft["y_se"])], inc="pe")
        w.op("tensor", lambda b, bb=sXr: b.matmul(bb[:, 0:512], nf1im[:, :], ap_im[:, :],
                                                  start=False, stop=True), inc="pe")
        w.op("tensor", lambda b, bb=sXi: b.matmul(bb[:, 0:512], f1im[:, :], ap_re[:, :],
                                                  start=True, stop=False), inc="pe")
        pe_x = w.op("tensor", lambda b, bb=sXi: b.matmul(bb[:, 0:512], f1re[:, :],
                                                         ap_im[:, :],
                                                         start=False, stop=True), inc="pe")
        fft["ap_pe"] = w.cnt["pe"]
        pe_k1 = None
        for c in range(4):
            pe_k1 = w.op("tensor", lambda b, c_=c, bb=s1r, v=kst:
                         b.matmul(bb[:, 128 * c_:128 * c_ + 128],
                                  v[:, 128 * c_:128 * c_ + 128], stk_re[:, :],
                                  start=True, stop=True),
                         waits=([("ve", ve_tw1)] if c == 0 else []), inc="pe")
            pe_k1 = w.op("tensor", lambda b, c_=c, bb=s1i, v=kst:
                         b.matmul(bb[:, 128 * c_:128 * c_ + 128],
                                  v[:, 128 * c_:128 * c_ + 128], stk_im[:, :],
                                  start=True, stop=True), inc="pe")
        ve_tw2 = twiddle(ap_re[:, :], ap_im[:, :], s1r[:, 0:512], s1i[:, 0:512],
                         conj=False, wait_pe=pe_k1, extra=[("pe", fft["ap_pe"])])
        fft["s1_ve"] = ve_tw2
        w.op("tensor", lambda b, bb=s1r: b.matmul(bb[:, 0:512], f1re[:, :], ap_re[:, :],
                                                  start=True, stop=False),
             waits=[("ve", ve_tw2)], inc="pe")
        w.op("tensor", lambda b, bb=s1r: b.matmul(bb[:, 0:512], nf1im[:, :], ap_im[:, :],
                                                  start=False, stop=True), inc="pe")
        w.op("tensor", lambda b, bb=s1i: b.matmul(bb[:, 0:512], f1im[:, :], ap_re[:, :],
                                                  start=True, stop=False), inc="pe")
        pe_k2 = w.op("tensor", lambda b, bb=s1i: b.matmul(bb[:, 0:512], f1re[:, :],
                                                          ap_im[:, :],
                                                          start=False, stop=True),
                     inc="pe")
        fft["ap_pe"] = w.cnt["pe"]
        w.op("scalar", lambda b, bb=s1r: b.copy(xk_re[:, :], bb[:, 0:512]),
             waits=[("pe", pe_k2), ("ve", fft["xk_ve"])], inc="se")
        w.op("scalar", lambda b, bb=s1i: b.copy(xk_im[:, :], bb[:, 0:512]),
             waits=[("pe", pe_k2)], inc="se")
        se_xk = w.cnt["se"]
        sc1 = tw_s1[:, 0:512]
        sc2 = tw_s1[:, 512:1024]
        w.op("vector", lambda b, bb=sXr: b.tensor_mul(sc1, bb[:, 0:512], xk_re[:, :]),
             waits=[("pe", pe_x), ("se", se_xk), ("pe", fft["w_pe"])], inc="ve")
        w.op("vector", lambda b, bb=sXi: b.tensor_mul(sc2, bb[:, 0:512], xk_im[:, :]),
             waits=[("ve", w.cnt["ve"])], inc="ve")
        w.op("vector", lambda b: b.tensor_sub(w_re[:, :], sc1, sc2),
             waits=[("ve", w.cnt["ve"])], inc="ve")
        w.op("vector", lambda b, bb=sXr: b.tensor_mul(sc1, bb[:, 0:512], xk_im[:, :]),
             waits=[("ve", w.cnt["ve"])], inc="ve")
        w.op("vector", lambda b, bb=sXi: b.tensor_mul(sc2, bb[:, 0:512], xk_re[:, :]),
             waits=[("ve", w.cnt["ve"])], inc="ve")
        w.op("vector", lambda b: b.tensor_add(w_im[:, :], sc1, sc2),
             waits=[("ve", w.cnt["ve"])], inc="ve")
        ve_prod = w.cnt["ve"]
        fft["prod_ve"] = ve_prod
        fft["xk_ve"] = ve_prod
        fft["x_ve"] = ve_prod
        pe_i1 = None
        for c in range(4):
            pe_i1 = w.op("tensor", lambda b, c_=c, bb=s1r:
                         b.matmul(bb[:, 128 * c_:128 * c_ + 128],
                                  w_re[:, 128 * c_:128 * c_ + 128], f1re[:, :],
                                  start=True, stop=False),
                         waits=([("ve", ve_prod), ("se", se_xk)] if c == 0 else []),
                         inc="pe")
            pe_i1 = w.op("tensor", lambda b, c_=c, bb=s1r:
                         b.matmul(bb[:, 128 * c_:128 * c_ + 128],
                                  w_im[:, 128 * c_:128 * c_ + 128], f1im[:, :],
                                  start=False, stop=True), inc="pe")
            pe_i1 = w.op("tensor", lambda b, c_=c, bb=s1i:
                         b.matmul(bb[:, 128 * c_:128 * c_ + 128],
                                  w_re[:, 128 * c_:128 * c_ + 128], nf1im[:, :],
                                  start=True, stop=False), inc="pe")
            pe_i1 = w.op("tensor", lambda b, c_=c, bb=s1i:
                         b.matmul(bb[:, 128 * c_:128 * c_ + 128],
                                  w_im[:, 128 * c_:128 * c_ + 128], f1re[:, :],
                                  start=False, stop=True), inc="pe")
        fft["w_pe"] = w.cnt["pe"]
        ve_tw3 = twiddle(gp_re[:, :], gp_im[:, :], s1r[:, 0:512], s1i[:, 0:512],
                         conj=True, wait_pe=pe_i1, extra=[("pe", fft["gp_pe"])])
        fft["s1_ve"] = ve_tw3
        w.op("tensor", lambda b, bb=sXr: b.matmul(bb[0:64, 0:512], fi_re[:, :],
                                                  gp_re[:, :], start=True, stop=False),
             waits=[("ve", ve_tw3)], inc="pe")
        w.op("tensor", lambda b, bb=sXr: b.matmul(bb[0:64, 0:512], fi_ip[:, :],
                                                  gp_im[:, :], start=False, stop=True),
             inc="pe")
        w.op("tensor", lambda b, bb=sXi: b.matmul(bb[0:64, 0:512], fi_re[:, :],
                                                  gp_im[:, :], start=True, stop=False),
             inc="pe")
        pe_y = w.op("tensor", lambda b, bb=sXi: b.matmul(bb[0:64, 0:512], fi_in[:, :],
                                                         gp_re[:, :],
                                                         start=False, stop=True), inc="pe")
        fft["gp_pe"] = w.cnt["pe"]
        w.op("scalar", lambda b, bb=sXr: b.copy(y_sb[:, 0:512], bb[0:64, 0:512]),
             waits=[("pe", pe_y), ("st", fft["y_st"])], inc="se")
        w.op("scalar", lambda b, bb=sXi: b.copy(y_sb[:, 512:1024], bb[0:64, 0:512]),
             waits=[("pe", pe_y)], inc="se")
        se_y = w.cnt["se"]
        fft["y_se"] = se_y
        dma_st(y_loc[c0:c0 + 4, 0:LSEQ].rearrange("c (js jf) -> js c jf", jf=128),
               y_sb[:, 0:512].rearrange("js (c jf) -> js c jf", jf=128),
               waits=[("se", se_y)])
        dma_st(y_loc[c0:c0 + 4, LSEQ:NTOK].rearrange("c (js jf) -> js c jf", jf=128),
               y_sb[:, 512:1024].rearrange("js (c jf) -> js c jf", jf=128))
        fft["y_st"] = w.cnt["st"]
        fft[f"st_pe{par}"] = w.cnt["pe"]
    fft["phA_st"] = st_phA
    fillB = [(lambda gg: (lambda: fft_group(gg)))(g) for g in range(32)]
    run_expert(1, fillB, 1)
    run_expert(2, fillB, 1)
    flush(fillB)
    st_fft = w.cnt["st"]
    ctxB.close()

    ctxC = ExitStack()
    EC = ctxC.enter_context
    outw_sb = EC(nc.sbuf_tensor("outw_sb", [128, D], BF16))
    yc_sb = [EC(nc.sbuf_tensor(f"yc{i}", [128, 512], F32)) for i in range(2)]
    vxc_sb = [EC(nc.sbuf_tensor(f"vxc{i}", [128, 512], F32)) for i in range(2)]
    x0c_sb = [EC(nc.sbuf_tensor(f"x0c{i}", [128, 512], F32)) for i in range(2)]
    yv_sb = EC(nc.sbuf_tensor("yv_sb", [128, 512], BF16))
    pc_sb = [EC(nc.sbuf_tensor(f"pc{i}", [128, 512], F32)) for i in range(2)]
    at_sb = [EC(nc.sbuf_tensor(f"at_sb{i}", [128, 512], F32)) for i in range(2)]
    fo_sb = [EC(nc.sbuf_tensor(f"fo_sb{i}", [128, 512], F32)) for i in range(2)]

    dma_cast(outw_sb[:, :], outw_i, waits=[("st", w.cnt["st"])])
    cst_outw = w.cnt["cst"]
    pc = {"po_se0": w.cnt["se"], "po_se1": w.cnt["se"], "yv_pe": w.cnt["pe"],
          "v_ve": w.cnt["ve"]}
    def pc_tile(t):
        par = t % 2
        yc, vxc, x0c = yc_sb[par], vxc_sb[par], x0c_sb[par]
        dma_sync(yc[:, :], y_loc[:, 512 * t:512 * t + 512],
                 waits=[("st", pc["fft_st"])] + ([("ve", pc.get(f"ld_ve{par}", 0))]
                                           if t >= 2 else []))
        dma_sync(vxc[:, :], vx_loc[:, 512 * t:512 * t + 512])
        dma_sync(x0c[:, :], x0_loc[:, 512 * t:512 * t + 512])
        ld_c = w.cnt["ld"]
        w.op("vector", lambda b, a_=vxc, c_=yc: b.scalar_tensor_tensor(
            tw_s1[:, 0:512], a_[:, :], fb_sb[:, 0:1], c_[:, :], ALU.mult, ALU.add),
            waits=[("ld", ld_c), ("ve", pc["v_ve"])], inc="ve")
        w.op("vector", lambda b, a_=x0c: b.tensor_mul(yv_sb[:, :], tw_s1[:, 0:512],
                                                      a_[:, :]),
             waits=[("ve", w.cnt["ve"]), ("pe", pc["yv_pe"])], inc="ve")
        ve_yv = w.cnt["ve"]
        pc[f"ld_ve{par}"] = ve_yv
        pc["v_ve"] = ve_yv
        blk = t // 4
        coln = 512 * (t % 4)
        for oc in range(8):
            bk = Ebk if oc % 2 == 0 else Fbk
            pcs = pc_sb[oc % 2]
            pe_o = w.op("tensor", lambda b, o=oc, bb=bk:
                        b.matmul(bb[:, 0:512], outw_sb[:, 128 * o:128 * o + 128],
                                 yv_sb[:, :], start=True, stop=True),
                        waits=[("ve", ve_yv), ("se", pc[f"po_se{oc % 2}"]),
                               ("cst", cst_outw)], inc="pe")
            w.op("scalar", lambda b, bb=bk, p_=pcs: b.copy(p_[:, :], bb[:, 0:512]),
                 waits=[("pe", pe_o), ("pst", pc.get(f"p_st{oc % 2}", 0))], inc="se")
            pc[f"po_se{oc % 2}"] = w.cnt["se"]
            w.op("gpsimd", lambda b, bl=blk, o=oc, cl=coln, p_=pcs: b.dma_start(
                out=p_buf[1024 * bl + 128 * o:1024 * bl + 128 * o + 128, cl:cl + 512],
                in_=p_[:, :]),
                waits=[("se", pc[f"po_se{oc % 2}"])], inc="pst", amt=16)
            pc[f"p_st{oc % 2}"] = w.cnt["pst"]
        pc["yv_pe"] = w.cnt["pe"]
    pc["fft_st"] = st_fft
    fillC = [(lambda tt: (lambda: pc_tile(tt)))(t) for t in range(32)]
    run_expert(3, fillC, 1)
    run_expert(4, fillC, 1)
    flush(fillC)
    st_pc = w.cnt["pst"]

    w.op("gpsimd", lambda b: b.collective_compute(
        "ReduceScatter", ALU.add, replica_groups=[list(range(NCORES))],
        ins=[p_buf.opt()], outs=[attn_rs.opt()]),
        waits=[("pst", st_pc)], inc="rs")

    for e in range(5, NEXP):
        run_expert(e, [], 0)
    if "prev_pend" in ffn:
        emit_down(*ffn.pop("prev_pend"))

    fin = {"st0": 0, "st1": 0, "ve0": w.cnt["ve"], "ve1": w.cnt["ve"]}
    for i4 in range(32):
        hc, s4 = divmod(i4, 4)
        par = i4 % 2
        col = 512 * s4
        lda = dma_sync(at_sb[par][:, :],
                       attn_rs[128 * hc:128 * (hc + 1), col:col + 512],
                       waits=[("rs", 1), ("ve", fin[f"ve{par}"])])
        w.op("vector", lambda b, c_=hc, cl=col, p_=par: b.scalar_tensor_tensor(
            fo_sb[p_][:, :], at_sb[p_][:, :], outb_sb[:, c_:c_ + 1],
            acc[:, 2048 * c_ + cl:2048 * c_ + cl + 512], ALU.add, ALU.add),
            waits=[("ld", lda), ("st", fin[f"st{par}"])], inc="ve")
        fin[f"ve{par}"] = w.cnt["ve"]
        dma_st(out_fm[128 * hc:128 * (hc + 1), col:col + 512], fo_sb[par][:, :],
               waits=[("ve", fin[f"ve{par}"])])
        fin[f"st{par}"] = w.cnt["st"]
    if DEBUG_DUMPS:
        dbg_acc = nc.dram_tensor("dbg_acc", [128, 8 * TPC], F32, kind="ExternalOutput").ap()
        dma_st(dbg_acc, acc[:, :], waits=[("st", w.cnt["st"])])
        for nm, src_ap in (("dbg_vx", vx_loc), ("dbg_x0", x0_loc), ("dbg_y", y_loc),
                           ("dbg_k", k_loc), ("dbg_at", attn_rs)):
            dout = nc.dram_tensor(nm, list(src_ap.shape), F32, kind="ExternalOutput").ap()
            dma_st(dout, src_ap, waits=[("st", w.cnt["st"])])
    w.wait("sync", "st", w.cnt["st"])
    w.wait("gpsimd", "st", w.cnt["st"])
    ctxC.close()

    with nc.Block() as block:
        w.emit(block)
    ctx.close()
    return nc



def _prep_inputs(inp):
    f32 = np.float32
    x = np.asarray(inp["L0_attn_u"], f32).reshape(NTOK, D)
    in_w = np.asarray(inp["L0_attn_in_w"], f32)
    in_b = np.asarray(inp["L0_attn_in_b"], f32)
    short_w = np.asarray(inp["L0_attn_short_w"], f32)
    short_b = np.asarray(inp["L0_attn_short_b"], f32)
    mlp_w1 = np.asarray(inp["L0_attn_mlp_w1"], f32)
    mlp_b1 = np.asarray(inp["L0_attn_mlp_b1"], f32)
    mlp_w2 = np.asarray(inp["L0_attn_mlp_w2"], f32)
    mlp_b2 = np.asarray(inp["L0_attn_mlp_b2"], f32)
    mlp_w3 = np.asarray(inp["L0_attn_mlp_w3"], f32)
    mlp_b3 = np.asarray(inp["L0_attn_mlp_b3"], f32)
    mlp_wo = np.asarray(inp["L0_attn_mlp_wo"], f32)
    freq = np.asarray(inp["L0_attn_freq"], f32)
    filt_bias = np.asarray(inp["L0_attn_filt_bias"], f32)
    out_w = np.asarray(inp["L0_attn_out_w"], f32)
    out_b = np.asarray(inp["L0_attn_out_b"], f32)
    wg_s = np.asarray(inp["L0_ffn_Wg_s"], f32)
    wu_s = np.asarray(inp["L0_ffn_Wu_s"], f32)
    wd_s = np.asarray(inp["L0_ffn_Wd_s"], f32)
    wg = np.asarray(inp["L0_ffn_Wg"], f32)
    wu = np.asarray(inp["L0_ffn_Wu"], f32)
    wd = np.asarray(inp["L0_ffn_Wd"], f32)
    wr = np.asarray(inp["L0_ffn_Wr"], f32)
    rb = np.asarray(inp["L0_ffn_rb"], f32)

    A = np.ascontiguousarray
    w1p = np.zeros((4, FILT), f32)
    w1p[0:3] = mlp_w1
    sel = np.zeros((8, 1024), f32)
    for e in range(8):
        sel[e, 128 * e:128 * (e + 1)] = 1.0
    shared = {
        "w1": w1p, "b1": A(mlp_b1.reshape(FILT, 1)),
        "w2": A(mlp_w2), "b2": A(mlp_b2.reshape(FILT, 1)),
        "w3": A(mlp_w3), "b3": A(mlp_b3.reshape(FILT, 1)),
        "freq_c": A(freq.reshape(1, FILT).T),
        "outb": A(out_b.reshape(8, 128).T),
        "wr_t": A(wr.T), "rb": A(rb.reshape(8, 1)), "sel8": sel,
        "ident32": np.eye(128, dtype=f32),
    }
    g9 = np.concatenate([wg_s[None], wg], 0)
    u9 = np.concatenate([wu_s[None], wu], 0)
    d9 = np.concatenate([wd_s[None], wd], 0)
    gq = g9.reshape(9, 4, 4, 128, 8, 128).transpose(0, 1, 5, 4, 2, 3).reshape(9, 4, 128, 4096)
    uq = u9.reshape(9, 4, 4, 128, 8, 128).transpose(0, 1, 5, 4, 2, 3).reshape(9, 4, 128, 4096)
    dq = d9.reshape(9, 1024, 4, 4, 128).transpose(0, 2, 4, 3, 1).reshape(9, 4, 128, 4096)
    shared["wfq"] = A(np.concatenate([gq, uq, dq], axis=3))
    maps = []
    for p in range(NCORES):
        sl = slice(128 * p, 128 * (p + 1))
        cols = np.r_[np.arange(128 * p, 128 * (p + 1)),
                     1024 + np.arange(128 * p, 128 * (p + 1)),
                     2048 + np.arange(128 * p, 128 * (p + 1))]
        m = dict(shared)
        m["x_sl"] = A(x[TPC * p:TPC * (p + 1)])
        m["inw_sl"] = A(in_w[:, cols])
        m["inb_sl"] = A(in_b[cols].reshape(3, 128).T)
        m["shw_sl"] = A(short_w[cols].reshape(3, 128, 3).transpose(1, 0, 2).reshape(128, 9))
        m["shb_sl"] = A(short_b[cols].reshape(3, 128).T)
        m["wo_sl"] = A(mlp_wo[:, sl])
        m["fb_sl"] = A(filt_bias[sl].reshape(128, 1))
        m["outw_sl"] = A(out_w[sl, :])
        m["chan_row"] = A(np.arange(128 * p, 128 * (p + 1), dtype=f32).reshape(1, 128))
        maps.append(m)
    return maps


def kernel(**inputs):
    nc = build_program()
    in_maps = _prep_inputs(inputs)
    res = run_bass_kernel_spmd(nc, in_maps, core_ids=list(range(NCORES)))
    parts = [np.asarray(r["out_fm"], dtype=np.float32).T for r in res.results]
    out = np.concatenate(parts, axis=0)
    return np.ascontiguousarray(out.reshape(2, LSEQ, D))


# version 26
# speedup vs baseline: 1.0403x; 1.0191x over previous
import numpy as np
from contextlib import ExitStack

import concourse.bass as bass
import concourse.mybir as mybir
from concourse.bass_utils import run_bass_kernel_spmd

F32 = mybir.dt.float32
BF16 = mybir.dt.bfloat16
I32 = mybir.dt.int32
AF = mybir.ActivationFunctionType
ALU = mybir.AluOpType
AX = mybir.AxisListType.X

NCORES = 8
D = 1024
LSEQ = 8192
NTOK = 16384
TPC = 2048
CPC = 128
FILT = 64
NFFT = 16384
NEXP = 8
DEBUG_DUMPS = False

ENGINES = ("tensor", "vector", "scalar", "gpsimd", "sync")


class W:
    def __init__(self, nc, ctx):
        self.nc = nc
        self.tape = {e: [] for e in ENGINES}
        self.sems = {}
        self.cnt = {}
        self.ctx = ctx

    def sem(self, name):
        if name not in self.sems:
            self.sems[name] = self.ctx.enter_context(self.nc.semaphore(name))
            self.cnt[name] = 0
        return name

    def op(self, eng, fn, waits=(), inc=None, amt=1):
        if inc is not None:
            self.sem(inc)
            self.cnt[inc] += amt
        self.tape[eng].append((tuple(waits), fn, inc, amt))
        return self.cnt[inc] if inc is not None else None

    def wait(self, eng, sem, val):
        self.tape[eng].append((((sem, val),), None, None, 0))

    def emit(self, block):
        for eng in ENGINES:
            deco = getattr(block, eng)
            tape = self.tape[eng]
            sems = self.sems

            def make(tape_):
                def body(b):
                    for waits, fn, inc, amt in tape_:
                        for s, v in waits:
                            if v > 0:
                                b.wait_ge(sems[s], v)
                        if fn is not None:
                            inst = fn(b)
                            if inc is not None:
                                inst.then_inc(sems[inc], amt)
                return body

            deco(make(tape))


def build_program():
    nc = bass.Bass("TRN2", target_bir_lowering=False)

    def din(name, shape, dtype=F32):
        return nc.dram_tensor(name, shape, dtype, kind="ExternalInput").ap()

    x_sl = din("x_sl", [TPC, D])
    inw_sl = din("inw_sl", [D, 384])
    inb_sl = din("inb_sl", [128, 3])
    shw_sl = din("shw_sl", [128, 9])
    shb_sl = din("shb_sl", [128, 3])
    w1_i = din("w1", [4, FILT])
    b1_i = din("b1", [FILT, 1])
    w2_i = din("w2", [FILT, FILT])
    b2_i = din("b2", [FILT, 1])
    w3_i = din("w3", [FILT, FILT])
    b3_i = din("b3", [FILT, 1])
    freq_i = din("freq_c", [FILT, 1])
    wo_i = din("wo_sl", [FILT, 128])
    fb_i = din("fb_sl", [128, 1])
    outw_i = din("outw_sl", [128, D])
    outb_i = din("outb", [128, 8])
    wr_i = din("wr_t", [D, 8])
    rb_i = din("rb", [8, 1])
    sel_i = din("sel8", [8, 1024])
    wfq_i = din("wfq", [9, 4, 128, 12288])
    chan_i = din("chan_row", [1, 128])
    id_i = din("ident32", [128, 128])

    out_fm = nc.dram_tensor("out_fm", [D, TPC], F32, kind="ExternalOutput").ap()

    xn_bnc = nc.dram_tensor("xn_bnc", [D, TPC], BF16).ap()
    g_xn = nc.dram_tensor("g_xn", [NCORES * D, TPC], BF16).ap()
    vx_loc = nc.dram_tensor("vx_loc", [CPC, NTOK], F32).ap()
    x0_loc = nc.dram_tensor("x0_loc", [CPC, NTOK], F32).ap()
    y_loc = nc.dram_tensor("y_loc", [CPC, NTOK], F32).ap()
    k_loc = nc.dram_tensor("k_loc", [CPC, LSEQ], F32).ap()
    p_buf = nc.dram_tensor("p_buf", [NCORES * D, TPC], F32).ap()
    attn_rs = nc.dram_tensor("attn_rs", [D, TPC], F32).ap()

    ctx = ExitStack()
    E = ctx.enter_context
    w = W(nc, ctx)

    xn_fm = E(nc.sbuf_tensor("xn_fm", [128, 8 * TPC], BF16))
    acc = E(nc.sbuf_tensor("acc", [128, 8 * TPC], F32))
    wq = [E(nc.sbuf_tensor(f"wq{i}", [128, 3 * 4096], BF16)) for i in range(2)]
    consts = E(nc.sbuf_tensor("consts", [128, 24], F32))
    ones_f32 = E(nc.sbuf_tensor("ones_f32", [1, 128], F32))
    ident_f = E(nc.sbuf_tensor("ident_f", [128, 128], F32))
    inb_sb = E(nc.sbuf_tensor("inb_sb", [128, 3], F32))
    shw_sb = E(nc.sbuf_tensor("shw_sb", [128, 9], F32))
    shb_sb = E(nc.sbuf_tensor("shb_sb", [128, 3], F32))
    fb_sb = E(nc.sbuf_tensor("fb_sb", [128, 1], F32))
    outb_sb = E(nc.sbuf_tensor("outb_sb", [128, 8], F32))
    wr_sb = E(nc.sbuf_tensor("wr_sb", [128, 64], F32))
    rb_sb = E(nc.sbuf_tensor("rb_sb", [8, 1], F32))
    sel_sb = E(nc.sbuf_tensor("sel_sb", [8, 1024], BF16))
    w1_sb = E(nc.sbuf_tensor("w1_sb", [4, FILT], F32))
    w2_sb = E(nc.sbuf_tensor("w2_sb", [FILT, FILT], F32))
    w3_sb = E(nc.sbuf_tensor("w3_sb", [FILT, FILT], F32))
    b1_sb = E(nc.sbuf_tensor("b1_sb", [FILT, 1], F32))
    b2_sb = E(nc.sbuf_tensor("b2_sb", [FILT, 1], F32))
    b3_sb = E(nc.sbuf_tensor("b3_sb", [FILT, 1], F32))
    freq_sb = E(nc.sbuf_tensor("freq_sb", [FILT, 1], F32))
    fbias_sb = [E(nc.sbuf_tensor(f"fbias{i}", [FILT, 1], F32)) for i in range(3)]
    wo_sb = E(nc.sbuf_tensor("wo_sb", [FILT, 128], F32))
    chan_sb = E(nc.sbuf_tensor("chan_sb", [1, 128], F32))
    drow = E(nc.sbuf_tensor("drow", [1, 128], F32))
    cw_t = E(nc.sbuf_tensor("cw_t", [8, TPC], BF16))
    f1re = E(nc.sbuf_tensor("f1re", [128, 128], BF16))
    f1im = E(nc.sbuf_tensor("f1im", [128, 128], BF16))
    nf1im = E(nc.sbuf_tensor("nf1im", [128, 128], BF16))
    stk_re = E(nc.sbuf_tensor("stk_re", [128, 128], BF16))
    stk_im = E(nc.sbuf_tensor("stk_im", [128, 128], BF16))
    fi_re = E(nc.sbuf_tensor("fi_re", [128, 64], BF16))
    fi_ip = E(nc.sbuf_tensor("fi_ip", [128, 64], BF16))
    fi_in = E(nc.sbuf_tensor("fi_in", [128, 64], BF16))
    t_re = E(nc.sbuf_tensor("t_re", [128, 128], F32))
    t_im = E(nc.sbuf_tensor("t_im", [128, 128], F32))
    tw_s1 = E(nc.sbuf_tensor("tw_s1", [128, 2 * 512], F32))
    cwb_sb = E(nc.sbuf_tensor("cwb_sb", [128, 2048], F32))
    gs_sb = E(nc.sbuf_tensor("gs_sb", [128, 512], F32))
    us_sb = E(nc.sbuf_tensor("us_sb", [128, 512], F32))
    d_sb = [E(nc.sbuf_tensor(f"d_sb{i}", [128, 4 * 512], BF16)) for i in range(2)]

    bank = [E(nc.psum_tensor(f"bank{i}", [128, 512], F32)) for i in range(8)]
    A, Bk, C, Dk, Ebk, Fbk, G, H = bank

    CONSTS = {}

    def cdef(name, val):
        CONSTS[name] = (len(CONSTS), float(val))

    TWO_PI = 6.283185307179586
    C_HI = float(np.float32(TWO_PI))
    C_LO = float(np.float64(TWO_PI) - np.float64(np.float32(TWO_PI)))
    cdef("inv1024", 1.0 / 1024.0)
    cdef("eps", 1e-5)
    cdef("one", 1.0)
    cdef("neg1", -1.0)
    cdef("half", 0.5)
    cdef("neghalf", -0.5)
    cdef("inv2pi", 1.0 / TWO_PI)
    cdef("nchi", -C_HI)
    cdef("nclo", -C_LO)
    cdef("pihalf", float(np.pi / 2))
    cdef("inv128", 1.0 / 128.0)
    cdef("invN", 1.0 / NFFT)
    cdef("zero", 0.0)
    cdef("invL1", 1.0 / (LSEQ - 1.0))
    cdef("c512", 512.0)
    MIND = float(np.log(0.01) / 1.5)
    MAXD = float(np.log(0.01) / 0.3)
    cdef("mind", MIND)
    cdef("dslope", (MAXD - MIND) / (D - 1.0))
    POSC = TWO_PI * 1e-4 / LSEQ

    def cc(name, p=128):
        j = CONSTS[name][0]
        return consts[0:p, j:j + 1]

    for s in ("ld", "cst", "st", "wld", "pst", "ag", "rs", "pe", "se", "ve", "po"):
        w.sem(s)

    def dma_sync(dst, src, waits=()):
        return w.op("sync", lambda b, d=dst, s=src: b.dma_start(out=d, in_=s),
                    waits=waits, inc="ld", amt=16)

    def dma_st(dst, src, waits=()):
        return w.op("sync", lambda b, d=dst, s=src: b.dma_start(out=d, in_=s),
                    waits=waits, inc="st", amt=16)

    def dma_cast(dst, src, waits=()):
        return w.op("gpsimd", lambda b, d=dst, s=src: b.dma_start(out=d, in_=s),
                    waits=waits, inc="cst", amt=16)

    def dma_wt(dst, src, waits=()):
        return w.op("gpsimd", lambda b, d=dst, s=src: b.dma_start(out=d, in_=s),
                    waits=waits, inc="wld", amt=16)

    for name, (idx, val) in CONSTS.items():
        w.op("gpsimd", lambda b, i=idx, v=val: b.memset(consts[:, i:i + 1], v), inc="po")
    w.op("gpsimd", lambda b: b.memset(ones_f32[:, :], 1.0), inc="po")
    po_setup = w.cnt["po"]

    for dst, src in [
        (inb_sb[:, :], inb_sl), (shw_sb[:, :], shw_sl), (shb_sb[:, :], shb_sl),
        (fb_sb[:, :], fb_i), (outb_sb[:, :], outb_i), (rb_sb[:, :], rb_i),
        (w1_sb[:, :], w1_i), (w2_sb[:, :], w2_i), (w3_sb[:, :], w3_i),
        (b1_sb[:, :], b1_i), (b2_sb[:, :], b2_i), (b3_sb[:, :], b3_i),
        (freq_sb[:, :], freq_i), (wo_sb[:, :], wo_i), (chan_sb[:, :], chan_i),
        (ident_f[:, :], id_i),
        (wr_sb[:, :].rearrange("p (hc e) -> p hc e", e=8),
         wr_i.rearrange("(hc p) e -> p hc e", p=128)),
    ]:
        dma_sync(dst, src)
    ld_small = w.cnt["ld"]
    dma_cast(sel_sb[:, :], sel_i)
    cst_small = w.cnt["cst"]
    pref = {}
    for qq in range(2):
        dma_wt(wq[qq][:, :], wfq_i[0, qq])
        pref[(0, qq)] = w.cnt["wld"]

    ctxF = ExitStack()
    EF = ctxF.enter_context
    gscr = EF(nc.sbuf_tensor("gscr", [128, 128], F32))
    gscr2 = EF(nc.sbuf_tensor("gscr2", [128, 128], F32))
    gscri = EF(nc.sbuf_tensor("gscri", [128, 128], I32))

    w.op("gpsimd", lambda b: b.iota(gscri[0:1, 0:128], [[1, 128]], base=0, channel_multiplier=0), inc="po")
    w.op("vector", lambda b: b.tensor_copy(gscr[0:1, 0:128], gscri[0:1, 0:128]),
         waits=[("po", w.cnt["po"])], inc="ve")
    ve_iota = w.cnt["ve"]
    pe_jk = w.op("tensor", lambda b: b.matmul(A[:, 0:128], gscr[0:1, 0:128],
                                              gscr[0:1, 0:128], start=True, stop=True),
                 waits=[("ve", ve_iota), ("ld", ld_small)], inc="pe")

    def gen_trig(div_const, cos_dst, sin_dst, extra=None, ncols=128, neg_tile=None):
        w.op("vector", lambda b: b.tensor_single_scalar(gscr[:, 0:ncols], A[:, 0:ncols],
                                                        cc(div_const), ALU.mult),
             waits=[("pe", pe_jk), ("se", w.cnt["se"])], inc="ve")
        w.op("vector", lambda b: b.tensor_copy(gscri[:, 0:ncols], gscr[:, 0:ncols]),
             waits=[("ve", w.cnt["ve"])], inc="ve")
        w.op("vector", lambda b: b.tensor_copy(gscr2[:, 0:ncols], gscri[:, 0:ncols]),
             waits=[("ve", w.cnt["ve"])], inc="ve")
        w.op("vector", lambda b: b.tensor_sub(gscr2[:, 0:ncols], gscr[:, 0:ncols],
                                              gscr2[:, 0:ncols]),
             waits=[("ve", w.cnt["ve"])], inc="ve")
        w.op("vector", lambda b: b.tensor_single_scalar(gscr[:, 0:ncols], gscr2[:, 0:ncols],
                                                        cc("half"), ALU.is_gt),
             waits=[("ve", w.cnt["ve"])], inc="ve")
        w.op("vector", lambda b: b.tensor_sub(gscr2[:, 0:ncols], gscr2[:, 0:ncols],
                                              gscr[:, 0:ncols]),
             waits=[("ve", w.cnt["ve"])], inc="ve")
        w.op("vector", lambda b: b.tensor_single_scalar(gscr[:, 0:ncols], gscr2[:, 0:ncols],
                                                        cc("neghalf"), ALU.is_lt),
             waits=[("ve", w.cnt["ve"])], inc="ve")
        w.op("vector", lambda b: b.tensor_add(gscr2[:, 0:ncols], gscr2[:, 0:ncols],
                                              gscr[:, 0:ncols]),
             waits=[("ve", w.cnt["ve"])], inc="ve")
        vf = w.cnt["ve"]
        sin_tmp = gscr[:, 0:ncols]
        w.op("scalar", lambda b: b.activation(sin_tmp, gscr2[:, 0:ncols], AF.Sin,
                                              bias=cc("zero"), scale=-TWO_PI),
             waits=[("ve", vf)], inc="se")
        se_s = w.cnt["se"]
        if extra is None:
            w.op("vector", lambda b: b.tensor_copy(sin_dst, sin_tmp),
                 waits=[("se", se_s)], inc="ve")
        else:
            w.op("vector", lambda b: b.tensor_single_scalar(sin_dst, sin_tmp,
                                                            cc(extra), ALU.mult),
                 waits=[("se", se_s)], inc="ve")
        if neg_tile is not None:
            w.op("vector", lambda b: b.tensor_single_scalar(neg_tile, sin_dst,
                                                            cc("neg1"), ALU.mult),
                 waits=[("ve", w.cnt["ve"])], inc="ve")
        ve_sin = w.cnt["ve"]
        w.op("vector", lambda b: b.tensor_single_scalar(gscr[:, 0:ncols], gscr2[:, 0:ncols],
                                                        cc("neg1"), ALU.mult),
             waits=[("ve", ve_sin)], inc="ve")
        w.op("vector", lambda b: b.tensor_max(gscr[:, 0:ncols], gscr[:, 0:ncols],
                                              gscr2[:, 0:ncols]),
             waits=[("ve", w.cnt["ve"])], inc="ve")
        va = w.cnt["ve"]
        cos_tmp = gscr2[:, 0:ncols]
        w.op("scalar", lambda b: b.activation(cos_tmp, gscr[:, 0:ncols], AF.Sin,
                                              bias=cc("pihalf"), scale=-TWO_PI),
             waits=[("ve", va)], inc="se")
        se_c = w.cnt["se"]
        if extra is None:
            w.op("vector", lambda b: b.tensor_copy(cos_dst, cos_tmp),
                 waits=[("se", se_c)], inc="ve")
        else:
            w.op("vector", lambda b: b.tensor_single_scalar(cos_dst, cos_tmp,
                                                            cc(extra), ALU.mult),
                 waits=[("se", se_c)], inc="ve")

    gen_trig("inv128", f1re[:, :], f1im[:, :], neg_tile=nf1im[:, :])
    gen_trig("invN", t_re[:, :], t_im[:, :])
    gen_trig("inv128", fi_re[:, :], fi_ip[:, :], extra="invN", ncols=64,
             neg_tile=fi_in[:, :])
    ve_t4 = w.cnt["ve"]
    dma_st(stk_re[0:64, :], f1re[0:64, :], waits=[("ve", ve_t4)])
    dma_st(stk_re[64:128, :], nf1im[0:64, :])
    dma_st(stk_im[0:64, :], f1im[0:64, :])
    dma_st(stk_im[64:128, :], f1re[0:64, :])
    st_stk = w.cnt["st"]

    w.op("vector", lambda b: b.tensor_scalar(drow[0:1, :], chan_sb[0:1, :],
                                             cc("dslope", 1), cc("mind", 1),
                                             ALU.mult, ALU.add),
         waits=[("ld", ld_small), ("ve", ve_t4)], inc="ve")
    for bsb, fdst in ((b1_sb, fbias_sb[0]), (b2_sb, fbias_sb[1]), (b3_sb, fbias_sb[2])):
        w.op("vector", lambda b, s=bsb, d_=fdst: b.tensor_mul(d_[0:FILT, :],
                                                              freq_sb[0:FILT, :],
                                                              s[0:FILT, :]),
             waits=[("ld", ld_small)], inc="ve")
    ve_fb = w.cnt["ve"]
    ctxF.close()

    def rrsin(dst, src, scale_ap, bias_ap, waits):
        s0 = rrs[0:FILT, 0:512]
        s1 = rrs[0:FILT, 512:1024]
        s2 = rrs[0:FILT, 1024:1536]
        s3 = rrs[0:FILT, 1536:2048]
        i0 = rrs[0:FILT, :].bitcast(I32)[0:FILT, 1536:2048]
        w.op("vector", lambda b: b.tensor_scalar(s0, src, scale_ap, bias_ap,
                                                 ALU.mult, ALU.add), waits=waits, inc="ve")
        w.op("vector", lambda b: b.tensor_single_scalar(s1, s0, cc("inv2pi", FILT),
                                                        ALU.mult),
             waits=[("ve", w.cnt["ve"])], inc="ve")
        w.op("vector", lambda b: b.tensor_copy(i0, s1), waits=[("ve", w.cnt["ve"])], inc="ve")
        w.op("vector", lambda b: b.tensor_copy(s2, i0), waits=[("ve", w.cnt["ve"])], inc="ve")
        w.op("vector", lambda b: b.tensor_sub(s3, s1, s2), waits=[("ve", w.cnt["ve"])],
             inc="ve")
        w.op("vector", lambda b: b.tensor_single_scalar(s1, s3, cc("half", FILT), ALU.is_gt),
             waits=[("ve", w.cnt["ve"])], inc="ve")
        w.op("vector", lambda b: b.tensor_add(s2, s2, s1), waits=[("ve", w.cnt["ve"])],
             inc="ve")
        w.op("vector", lambda b: b.tensor_single_scalar(s1, s3, cc("neghalf", FILT),
                                                        ALU.is_lt),
             waits=[("ve", w.cnt["ve"])], inc="ve")
        w.op("vector", lambda b: b.tensor_sub(s2, s2, s1), waits=[("ve", w.cnt["ve"])],
             inc="ve")
        w.op("vector", lambda b: b.scalar_tensor_tensor(s1, s2, cc("nchi", FILT), s0,
                                                        ALU.mult, ALU.add),
             waits=[("ve", w.cnt["ve"])], inc="ve")
        w.op("vector", lambda b: b.scalar_tensor_tensor(s1, s2, cc("nclo", FILT), s1,
                                                        ALU.mult, ALU.add),
             waits=[("ve", w.cnt["ve"])], inc="ve")
        vm = w.cnt["ve"]
        w.op("scalar", lambda b: b.activation(dst, s1, AF.Sin, bias=cc("zero", FILT),
                                              scale=1.0),
             waits=[("ve", vm)], inc="se")
        return w.cnt["se"]

    flt = {}

    def filter_tile(t):
        base = 512 * t
        if t == 0:
            flt["st"] = w.cnt["st"]
            flt["pe"] = w.cnt["pe"]
            flt["se"] = w.cnt["se"]
            flt["init_ve"] = w.cnt["ve"]
            flt["tile_ve"] = w.cnt["ve"]
            flt["kt_ve"] = w.cnt["ve"]
            w.op("gpsimd", lambda b: b.iota(rri[0:1, :], [[1, 512]], base=0,
                                            channel_multiplier=0),
                 waits=[("ve", flt["tile_ve"]), ("se", flt["se"])], inc="po")
            po_i = w.cnt["po"]
            w.op("vector", lambda b: b.tensor_copy(prow[0:1, :], rri[0:1, :]),
                 waits=[("po", po_i), ("pe", flt["pe"]), ("st", flt["st"]),
                        ("se", flt["se"]), ("ve", flt["tile_ve"])], inc="ve")
        else:
            w.op("vector", lambda b: b.tensor_single_scalar(prow[0:1, :], prow[0:1, :],
                                                            cc("c512", 1), ALU.add),
                 waits=[("pe", flt["pe"]), ("st", flt["st"]), ("se", flt["se"]),
                        ("ve", flt["tile_ve"])], inc="ve")
        w.op("vector", lambda b: b.tensor_single_scalar(zt[0:1, :], prow[0:1, :],
                                                        cc("invL1", 1), ALU.mult),
             waits=[("ve", w.cnt["ve"])], inc="ve")
        ve_tr = w.cnt["ve"]
        w.op("scalar", lambda b: b.activation(zr[0][0:1, :], prow[0:1, :], AF.Sin,
                                              bias=cc("pihalf", 1), scale=POSC),
             waits=[("ve", ve_tr)], inc="se")
        w.op("scalar", lambda b: b.activation(zr2[0:1, :], prow[0:1, :], AF.Sin,
                                              bias=cc("zero", 1), scale=-POSC),
             waits=[("ve", ve_tr), ("se", w.cnt["se"])], inc="se")
        se_z = w.cnt["se"]
        dma_st(zt[1:2, :], zr[0][0:1, :], waits=[("se", se_z)])
        dma_st(zt[2:3, :], zr2[0:1, :], waits=[("se", se_z)])
        st_z = w.cnt["st"]
        pe1 = w.op("tensor", lambda b: b.matmul(Ebk[0:FILT, :], w1_sb[0:3, :], zt[0:3, :],
                                                start=True, stop=True),
                   waits=[("st", st_z), ("ld", ld_small), ("ve", w.cnt["ve"]),
                          ("se", w.cnt["se"])], inc="pe")
        se1 = rrsin(h_a[0:FILT, :], Ebk[0:FILT, 0:512], freq_sb[0:FILT, :],
                    fbias_sb[0][0:FILT, :], [("pe", pe1), ("ve", ve_fb)])
        pe2 = w.op("tensor", lambda b: b.matmul(Fbk[0:FILT, :], w2_sb[:, :], h_a[0:FILT, :],
                                                start=True, stop=True),
                   waits=[("se", se1)] + ([("ve", flt["init_ve"])] if t == 0 else []),
                   inc="pe")
        se2 = rrsin(h_b[0:FILT, :], Fbk[0:FILT, 0:512], freq_sb[0:FILT, :],
                    fbias_sb[1][0:FILT, :], [("pe", pe2)])
        pe3 = w.op("tensor", lambda b: b.matmul(Ebk[0:FILT, :], w3_sb[:, :], h_b[0:FILT, :],
                                                start=True, stop=True),
                   waits=[("se", se2)], inc="pe")
        se3 = rrsin(h_a[0:FILT, :], Ebk[0:FILT, 0:512], freq_sb[0:FILT, :],
                    fbias_sb[2][0:FILT, :], [("pe", pe3)])
        pe4 = w.op("tensor", lambda b: b.matmul(G[:, :], wo_sb[0:FILT, :], h_a[0:FILT, :],
                                                start=True, stop=True),
                   waits=[("se", se3)], inc="pe")
        pe5 = w.op("tensor", lambda b: b.matmul(H[:, :], drow[0:1, :], zt[0:1, :],
                                                start=True, stop=True),
                   waits=[("ve", ve_tr)] + ([("se", flt["se"])] if t == 0 else []),
                   inc="pe")
        w.op("scalar", lambda b: b.activation(kt_sb[:, :], H[:, :], AF.Exp,
                                              bias=cc("zero"), scale=1.0),
             waits=[("pe", pe5), ("ve", flt["kt_ve"]), ("st", flt["st"])],
             inc="se")
        se_dec = w.cnt["se"]
        w.op("vector", lambda b: b.tensor_mul(kt_sb[:, :], G[:, :], kt_sb[:, :]),
             waits=[("pe", pe4), ("se", se_dec)], inc="ve")
        flt["kt_ve"] = w.cnt["ve"]
        dma_st(k_loc[:, base:base + 512], kt_sb[:, :], waits=[("ve", flt["kt_ve"])])
        flt["st"] = w.cnt["st"]
        flt["pe"] = w.cnt["pe"]
        flt["se"] = w.cnt["se"]
        flt["tile_ve"] = w.cnt["ve"]

    ffn = {}

    def ffn_entry():
        for kk, ss in (("gu_se", "se"), ("u_ve", "ve"),
                       ("dn_ve0", "ve"), ("dn_ve1", "ve"), ("dn_pe", "pe"),
                       ("acc_se", "se"), ("cwb_ve", "ve"), ("cwb_se", "se")):
            ffn[kk] = max(ffn.get(kk, 0), w.cnt[ss])
        ffn.setdefault("d_po", ("po", 0))

    def emit_down(wt, tok_, po_d_, dp, wpar_, acc_w=None):
        for oc in range(8):
            bk = C if oc % 2 == 0 else Dk
            key = f"dn_ve{oc % 2}"
            pe_d = None
            for ic in range(4):
                pe_d = w.op("tensor", lambda b, i_=ic, o=oc, wt_=wt, bb=bk, d_=dp:
                            b.matmul(bb[:, 0:512],
                                     wt_[:, 8192 + 1024 * i_ + 128 * o:8192 + 1024 * i_ + 128 * o + 128],
                                     d_sb[d_][:, 512 * i_:512 * i_ + 512],
                                     start=(i_ == 0), stop=(i_ == 3)),
                            waits=([po_d_, ("ve", ffn[key])]
                                   + ([("se", ffn["cwb_se"])] if oc == 1 else [])
                                   if ic == 0 else []), inc="pe")
            asl = acc[:, 2048 * oc + tok_:2048 * oc + tok_ + 512]
            w.op("vector", lambda b, a_=asl, bb=bk: b.tensor_add(
                a_, bb[:, 0:512], a_),
                waits=[("pe", pe_d),
                       acc_w if acc_w else ("se", ffn["acc_se"])], inc="ve")
            ffn[key] = w.cnt["ve"]
        ffn["dn_pe"] = w.cnt["pe"]
        ffn[f"wq_pe{wpar_}"] = w.cnt["pe"]

    def emit_load(k):
        ei_, q_ = divmod(k, 4)
        par_ = k % 2
        prev = ffn.get(f"wq_pe{par_}", 0)
        dma_wt(wq[par_][:, :], wfq_i[ei_, q_],
               waits=[("pe", prev)] if prev else [])
        pref[(ei_, q_)] = w.cnt["wld"]

    def ffn_expert_units(e, gates=None):
        if gates is None:
            ffn_entry()
        else:
            ffn.setdefault("d_po", ("po", 0))
        eng_d, sem_d = (("vector", "ve") if e in () else ("gpsimd", "po"))
        if e >= 0:
            for t4 in range(4):
                pe_cb = w.op("tensor", lambda b, ee=e, tt=512 * t4:
                             b.matmul(Dk[:, 0:512],
                                      sel_sb[0:8, 128 * ee:128 * ee + 128],
                                      cw_t[0:8, tt:tt + 512], start=True, stop=True),
                             waits=[("ve", ffn["dn_ve1"]), ("se", ffn["cwb_se"])],
                             inc="pe")
                w.op("scalar", lambda b, tt=512 * t4: b.copy(cwb_sb[:, tt:tt + 512],
                                                             H[:, 0:512] if False else Dk[:, 0:512]),
                     waits=[("pe", pe_cb)], inc="se")
                ffn["cwb_se"] = w.cnt["se"]
        ei = e + 1
        for q in range(4):
            k = 4 * ei + q
            par = k % 2
            wqt = wq[par]
            if (ei, q) not in pref:
                emit_load(k)
            cst_w = pref.pop((ei, q))
            for t in range(4):
                tok = 512 * t
                gwts = [("wld", cst_w), ("se", ffn["gu_se"])]
                if gates is not None:
                    gwts.append(("ve", gates[0][4 * t + 3]))
                for ic in range(4):
                    pe_g = None
                    for hc in range(8):
                        pe_g = w.op("tensor", lambda b, i_=ic, h=hc, tt=tok, wt=wqt:
                                    b.matmul(A[:, 0:512],
                                             wt[:, 512 * h + 128 * i_:512 * h + 128 * i_ + 128],
                                             xn_fm[:, 2048 * h + tt:2048 * h + tt + 512],
                                             start=(h == 0), stop=(h == 7)),
                                    waits=(gwts if hc == 0 and ic == 0 else
                                           [("se", ffn["gu_se"])] if hc == 0 else []),
                                    inc="pe")
                    pe_u = None
                    for hc in range(8):
                        pe_u = w.op("tensor", lambda b, i_=ic, h=hc, tt=tok, wt=wqt:
                                    b.matmul(Bk[:, 0:512],
                                             wt[:, 4096 + 512 * h + 128 * i_:4096 + 512 * h + 128 * i_ + 128],
                                             xn_fm[:, 2048 * h + tt:2048 * h + tt + 512],
                                             start=(h == 0), stop=(h == 7)),
                                    waits=([("ve", ffn["u_ve"])] if hc == 0 else []),
                                    inc="pe")
                    w.op("scalar", lambda b: b.activation(gs_sb[:, :], A[:, 0:512],
                                                          AF.Silu, bias=cc("zero"),
                                                          scale=1.0),
                         waits=[("pe", pe_g), ffn["d_po"]], inc="se")
                    ffn["gu_se"] = w.cnt["se"]
                    if e >= 0:
                        w.op("vector", lambda b, tt=tok:
                             b.tensor_mul(us_sb[:, :], Bk[:, 0:512],
                                          cwb_sb[:, tt:tt + 512]),
                             waits=[("pe", pe_u), ("se", ffn["cwb_se"]),
                                    ffn["d_po"]], inc="ve")
                    else:
                        w.op("vector", lambda b: b.tensor_copy(us_sb[:, :], Bk[:, 0:512]),
                             waits=[("pe", pe_u), ffn["d_po"]], inc="ve")
                    ffn["u_ve"] = w.cnt["ve"]
                    w.op(eng_d, lambda b, i_=ic, dp=t % 2:
                         b.tensor_mul(d_sb[dp][:, 512 * i_:512 * i_ + 512], gs_sb[:, :],
                                      us_sb[:, :]),
                         waits=[("se", ffn["gu_se"]), ("ve", ffn["u_ve"]),
                                ("pe", ffn["dn_pe"])], inc=sem_d)
                    ffn["d_po"] = (sem_d, w.cnt[sem_d])
                po_d = ffn["d_po"]
                if "prev_pend" in ffn:
                    emit_down(*ffn.pop("prev_pend"))
                if t == 0 and k + 1 <= 35:
                    ei2, q2 = divmod(k + 1, 4)
                    if (ei2, q2) not in pref:
                        emit_load(k + 1)
                accw = ("se", gates[1][4 * t + 3]) if gates is not None else None
                ffn["prev_pend"] = (wqt, tok, po_d, t % 2, par, accw)
                ffn["cwb_ve"] = ffn["u_ve"]
                yield

    ffn_entry()
    ctx0 = ExitStack()
    E0 = ctx0.enter_context
    x_tm = [E0(nc.sbuf_tensor(f"x_tm{i}", [128, D], F32)) for i in range(2)]
    xn_tm = [E0(nc.sbuf_tensor(f"xn_tm{i}", [128, D], F32)) for i in range(2)]
    ln_s = [E0(nc.sbuf_tensor(f"ln_s{i}", [128, 8], F32)) for i in range(2)]
    xnf_t = E0(nc.sbuf_tensor("xnf_t", [128, 256], F32))
    top_sb = E0(nc.sbuf_tensor("top_sb", [128, 48], F32))
    pr_sb = E0(nc.sbuf_tensor("pr_sb", [8, 128], F32))

    ph0 = {"sig_se": w.cnt["se"], "cw_ve": w.cnt["ve"],
           "pe0": w.cnt["pe"], "pe1": w.cnt["pe"],
           "se0": w.cnt["se"], "se1": w.cnt["se"],
           "ve0": w.cnt["ve"], "ve1": w.cnt["ve"]}
    ph0_tile_ve = []
    ph0_tile_se = []

    def ph0_tile(i):
        bb2 = i % 2
        xt, xnt, lns = x_tm[bb2], xn_tm[bb2], ln_s[bb2]
        ldx = dma_sync(xt[:, :], x_sl[128 * i:128 * (i + 1), :],
                       waits=[("pe", ph0[f"pe{bb2}"]), ("se", ph0[f"se{bb2}"]),
                              ("ve", ph0[f"ve{bb2}"])])
        w.op("vector", lambda b, xt=xt, lns=lns:
             b.reduce_sum(lns[:, 0:1], xt[:, :], axis=AX),
             waits=[("ld", ldx)], inc="ve")
        w.op("scalar", lambda b, xt=xt, xnt=xnt: b.square(xnt[:, :], xt[:, :]),
             waits=[("ld", ldx), ("pe", ph0[f"pe{bb2}"]), ("ve", w.cnt["ve"])],
             inc="se")
        se_sq = w.cnt["se"]
        w.op("vector", lambda b, xnt=xnt, lns=lns:
             b.reduce_sum(lns[:, 1:2], xnt[:, :], axis=AX),
             waits=[("se", se_sq), ("ve", w.cnt["ve"])], inc="ve")
        w.op("vector", lambda b, lns=lns: b.tensor_single_scalar(
            lns[:, 0:1], lns[:, 0:1], cc("inv1024"), ALU.mult),
             waits=[("ve", w.cnt["ve"])], inc="ve")
        w.op("vector", lambda b, lns=lns: b.tensor_mul(lns[:, 2:3], lns[:, 0:1],
                                                       lns[:, 0:1]),
             waits=[("ve", w.cnt["ve"])], inc="ve")
        w.op("vector", lambda b, lns=lns: b.scalar_tensor_tensor(
            lns[:, 2:3], lns[:, 1:2], cc("inv1024"), lns[:, 2:3],
            ALU.mult, ALU.subtract),
             waits=[("ve", w.cnt["ve"])], inc="ve")
        w.op("scalar", lambda b, lns=lns: b.activation(lns[:, 3:4], lns[:, 2:3],
                                                       AF.Sqrt, bias=cc("eps"),
                                                       scale=1.0),
             waits=[("ve", w.cnt["ve"])], inc="se")
        w.op("vector", lambda b, lns=lns: b.reciprocal(lns[:, 4:5], lns[:, 3:4]),
             waits=[("se", w.cnt["se"])], inc="ve")
        w.op("vector", lambda b, lns=lns: b.tensor_mul(lns[:, 5:6], lns[:, 0:1],
                                                       lns[:, 4:5]),
             waits=[("ve", w.cnt["ve"])], inc="ve")
        w.op("vector", lambda b, xt=xt, xnt=xnt, lns=lns: b.tensor_scalar(
            xnt[:, :], xt[:, :], lns[:, 4:5], lns[:, 5:6], ALU.mult, ALU.subtract),
             waits=[("ve", w.cnt["ve"]), ("se", se_sq)], inc="ve")
        ve_xn = w.cnt["ve"]
        ph0[f"ve{bb2}"] = ve_xn
        for c8 in range(8):
            bx = G
            by = H
            xs = 128 * c8
            pe_t1 = w.op("tensor", lambda b, bb=bx, s=xs, xnt=xnt:
                         b.transpose(bb[:, 0:128], xnt[:, s:s + 128], ident_f[:, :]),
                         waits=[("ve", ve_xn), ("se", w.cnt["se"])], inc="pe")
            pe_t2 = w.op("tensor", lambda b, bb=by, s=xs, xt=xt:
                         b.transpose(bb[:, 0:128], xt[:, s:s + 128], ident_f[:, :]),
                         waits=[("se", w.cnt["se"])], inc="pe")
            o = (c8 % 2) * 128
            w.op("scalar", lambda b, bb=bx, oo=o: b.copy(xnf_t[:, oo:oo + 128],
                                                         bb[:, 0:128]),
                 waits=[("pe", pe_t1), ("ve", w.cnt["ve"])], inc="se")
            se_e1 = w.cnt["se"]
            w.op("scalar", lambda b, bb=by, c=c8, ti=i:
                 b.copy(acc[:, 2048 * c + 128 * ti:2048 * c + 128 * ti + 128],
                        bb[:, 0:128]),
                 waits=[("pe", pe_t2)], inc="se")
            w.op("vector", lambda b, c=c8, ti=i, oo=o:
                 b.tensor_copy(xn_fm[:, 2048 * c + 128 * ti:2048 * c + 128 * ti + 128],
                               xnf_t[:, oo:oo + 128]),
                 waits=[("se", se_e1)], inc="ve")
            w.op("tensor", lambda b, c=c8, oo=o:
                 b.matmul(Ebk[0:8, 0:128], wr_sb[:, 8 * c:8 * c + 8],
                          xnf_t[:, oo:oo + 128], start=(c == 0), stop=(c == 7)),
                 waits=([("se", se_e1), ("ve", ve_xn)] if c8 > 0 else
                        [("se", se_e1), ("ve", ve_xn), ("se", ph0["sig_se"])]),
                 inc="pe")
        pe_r = w.cnt["pe"]
        w.op("scalar", lambda b: b.activation(pr_sb[0:8, :], Ebk[0:8, 0:128], AF.Sigmoid,
                                              bias=rb_sb[0:8, :], scale=1.0),
             waits=[("pe", pe_r)], inc="se")
        ph0["sig_se"] = w.cnt["se"]
        pe_tp = w.op("tensor", lambda b: b.transpose(Fbk[0:128, 0:8], pr_sb[0:8, :],
                                                     ident_f[0:8, 0:8]),
                     waits=[("se", ph0["sig_se"]), ("ve", ph0["cw_ve"])], inc="pe")
        w.op("vector", lambda b: b.tensor_copy(top_sb[:, 0:8], Fbk[0:128, 0:8]),
             waits=[("pe", pe_tp)], inc="ve")
        w.op("vector", lambda b: b.reduce_max(top_sb[:, 8:9], top_sb[:, 0:8], axis=AX),
             waits=[("ve", w.cnt["ve"])], inc="ve")
        w.op("vector", lambda b: b.tensor_single_scalar(top_sb[:, 16:24], top_sb[:, 0:8],
                                                        top_sb[:, 8:9], ALU.is_ge),
             waits=[("ve", w.cnt["ve"])], inc="ve")
        w.op("vector", lambda b: b.tensor_scalar(top_sb[:, 24:32], top_sb[:, 16:24],
                                                 cc("neg1"), cc("one"), ALU.mult, ALU.add),
             waits=[("ve", w.cnt["ve"])], inc="ve")
        w.op("vector", lambda b: b.tensor_mul(top_sb[:, 24:32], top_sb[:, 0:8],
                                              top_sb[:, 24:32]),
             waits=[("ve", w.cnt["ve"])], inc="ve")
        w.op("vector", lambda b: b.reduce_max(top_sb[:, 9:10], top_sb[:, 24:32], axis=AX),
             waits=[("ve", w.cnt["ve"])], inc="ve")
        w.op("vector", lambda b: b.tensor_single_scalar(top_sb[:, 32:40], top_sb[:, 24:32],
                                                        top_sb[:, 9:10], ALU.is_ge),
             waits=[("ve", w.cnt["ve"])], inc="ve")
        w.op("vector", lambda b: b.tensor_add(top_sb[:, 16:24], top_sb[:, 16:24],
                                              top_sb[:, 32:40]),
             waits=[("ve", w.cnt["ve"])], inc="ve")
        w.op("vector", lambda b: b.tensor_mul(top_sb[:, 16:24], top_sb[:, 0:8],
                                              top_sb[:, 16:24]),
             waits=[("ve", w.cnt["ve"])], inc="ve")
        ve_cw = w.cnt["ve"]
        pe_tb = w.op("tensor", lambda b: b.transpose(Fbk[0:8, 0:128], top_sb[0:128, 16:24],
                                                     ident_f[:, :]),
                     waits=[("ve", ve_cw)], inc="pe")
        w.op("vector", lambda b, ti=i: b.tensor_copy(cw_t[0:8, 128 * ti:128 * ti + 128],
                                                     Fbk[0:8, 0:128]),
             waits=[("pe", pe_tb)], inc="ve")
        ph0["cw_ve"] = w.cnt["ve"]
        ph0[f"pe{bb2}"] = w.cnt["pe"]
        ph0[f"se{bb2}"] = w.cnt["se"]
        ph0_tile_ve.append(w.cnt["ve"])
        ph0_tile_se.append(w.cnt["se"])

    gen_sh = ffn_expert_units(-1, gates=(ph0_tile_ve, ph0_tile_se))
    for i in range(16):
        ph0_tile(i)
        if i % 4 == 3 and i < 15:
            next(gen_sh)
    ve_ph0 = w.cnt["ve"]

    st_xn = dma_st(xn_bnc.rearrange("(hc p) t -> p hc t", p=128),
                   xn_fm[:, :].rearrange("p (hc t) -> p hc t", t=TPC),
                   waits=[("ve", ve_ph0)])
    w.op("gpsimd", lambda b: b.collective_compute(
        "AllGather", ALU.bypass, replica_groups=[list(range(NCORES))],
        ins=[xn_bnc.opt()], outs=[g_xn.opt()]),
        waits=[("st", st_xn)], inc="ag")
    for _ in gen_sh:
        pass
    ctx0.close()

    ctxF2 = ExitStack()
    EF2 = ctxF2.enter_context
    prow = EF2(nc.sbuf_tensor("prow", [1, 512], F32))
    zr2 = EF2(nc.sbuf_tensor("zr2", [1, 512], F32))
    zt = EF2(nc.sbuf_tensor("zt", [4, 512], F32))
    zr = [EF2(nc.sbuf_tensor("zr0", [1, 512], F32))]
    h_a = EF2(nc.sbuf_tensor("h_a", [FILT, 512], F32))
    h_b = EF2(nc.sbuf_tensor("h_b", [FILT, 512], F32))
    rrs = EF2(nc.sbuf_tensor("rrs", [FILT, 4 * 512], F32))
    rri = EF2(nc.sbuf_tensor("rri", [1, 512], I32))
    kt_sb = EF2(nc.sbuf_tensor("kt_sb", [128, 512], F32))

    def run_expert(e, fillers, per_unit):
        for _ in ffn_expert_units(e):
            n = per_unit
            while n > 0 and fillers:
                fillers.pop(0)()
                n -= 1

    def flush(fillers):
        while fillers:
            fillers.pop(0)()

    fillF = [(lambda tt: (lambda: filter_tile(tt)))(t) for t in range(16)]
    run_expert(0, fillF, 1)
    flush(fillF)
    st_filter = w.cnt["st"]
    ctxF2.close()

    ctxA = ExitStack()
    EA = ctxA.enter_context
    inw_sb = EA(nc.sbuf_tensor("inw_sb", [128, 3072], BF16))
    xgh = [EA(nc.sbuf_tensor(f"xgh{i}", [128, 4 * 512], BF16)) for i in range(2)]
    up_sb = EA(nc.sbuf_tensor("up_sb", [128, 3 * 514], F32))
    uc2 = EA(nc.sbuf_tensor("uc2", [128, 2 * 512], F32))
    cscr = EA(nc.sbuf_tensor("cscr", [128, 512], F32))

    dma_cast(inw_sb[:, :].rearrange("p (hc i) -> p hc i", i=384),
             inw_sl.rearrange("(hc p) i -> p hc i", p=128),
             waits=[("st", st_filter), ("ag", 1)])
    cst_inw = w.cnt["cst"]
    pa = {"spill_st": w.cnt["st"],
          "xh_pe0": w.cnt["pe"], "xh_pe1": w.cnt["pe"]}
    for oc in range(3):
        pa[f"conv_{oc}"] = ("po" if oc == 1 else "ve", w.cnt["po" if oc == 1 else "ve"])
        pa[f"up_se{oc}"] = w.cnt["se"]
        w.op("gpsimd", lambda b, o=oc: b.memset(up_sb[:, 514 * o:514 * o + 2], 0.0),
             waits=[("st", st_filter), ("ag", 1)], inc="po")
    pa["po_hist"] = w.cnt["po"]

    def phaseA_tile(t):
        r = t // 4
        colc = 512 * (t % 4)
        ldh = []
        for half in range(2):
            ldh.append(w.op("sync", lambda b, hf=half, rr=r, cl=colc: b.dma_start(
                out=xgh[hf][:, :].rearrange("p (hc n) -> p hc n", n=512),
                in_=g_xn[1024 * rr + 512 * hf:1024 * rr + 512 * hf + 512,
                         cl:cl + 512].rearrange("(hc p) n -> p hc n", p=128)),
                waits=[("ag", 1), ("pe", pa[f"xh_pe{half}"])], inc="ld", amt=16))
        pe_last = None
        for hc in range(8):
            half = hc // 4
            xsl = 512 * (hc % 4)
            for oc in range(3):
                bk = (Ebk, Fbk, G)[oc]
                wts = []
                if hc == 0 and oc == 0:
                    wts = [("ld", ldh[0]), ("cst", cst_inw), ("po", pa["po_hist"]),
                           ("se", max(pa["up_se0"], pa["up_se1"], pa["up_se2"])),
                           pa["conv_0"], pa["conv_1"], pa["conv_2"]]
                elif hc == 4 and oc == 0:
                    wts = [("ld", ldh[1])]
                pe_last = w.op("tensor", lambda b, h=hc, o=oc, hf=half, xs=xsl, bb=bk:
                               b.matmul(bb[:, 0:512],
                                        inw_sb[:, 384 * h + 128 * o:384 * h + 128 * o + 128],
                                        xgh[hf][:, xs:xs + 512],
                                        start=(h == 0), stop=(h == 7)),
                               waits=wts, inc="pe")
            if hc == 3:
                pa["xh_pe0"] = w.cnt["pe"]
        pa["xh_pe1"] = w.cnt["pe"]
        if t == 16:
            pass
        for oc in range(3):
            bk = (Ebk, Fbk, G)[oc]
            if t == 16 and oc == 0:
                for o3 in range(3):
                    w.op("gpsimd", lambda b, o=o3:
                         b.memset(up_sb[:, 514 * o:514 * o + 2], 0.0),
                         waits=[pa[f"conv_{o3}"]], inc="po")
                pa["po_hist"] = w.cnt["po"]
            w.op("scalar", lambda b, o=oc, bb=bk:
                 b.activation(up_sb[:, 514 * o + 2:514 * o + 514], bb[:, 0:512],
                              AF.Identity, bias=inb_sb[:, o:o + 1], scale=1.0),
                 waits=[("pe", pe_last), ("po", pa["po_hist"]), pa[f"conv_{oc}"]],
                 inc="se")
            pa[f"up_se{oc}"] = w.cnt["se"]
            u0 = up_sb[:, 514 * oc + 0:514 * oc + 512]
            u1 = up_sb[:, 514 * oc + 1:514 * oc + 513]
            u2 = up_sb[:, 514 * oc + 2:514 * oc + 514]
            eng = "vector"
            sem = "ve"
            stage1 = cscr[:, :] if oc < 2 else tw_s1[:, 0:512]
            dst = uc2[:, 512 * oc:512 * oc + 512] if oc < 2 else cscr[:, :]
            w.op(eng, lambda b, a_=u0, o=oc, s_=stage1:
                 b.tensor_scalar(s_, a_, shw_sb[:, 3 * o:3 * o + 1], shb_sb[:, o:o + 1],
                                 ALU.mult, ALU.add),
                 waits=[("se", pa[f"up_se{oc}"]), ("st", pa["spill_st"])], inc=sem)
            w.op(eng, lambda b, a_=u1, o=oc, dd=dst, s_=stage1:
                 b.scalar_tensor_tensor(dd, a_, shw_sb[:, 3 * o + 1:3 * o + 2], s_,
                                        ALU.mult, ALU.add), inc=sem)
            w.op(eng, lambda b, a_=u2, o=oc, dd=dst:
                 b.scalar_tensor_tensor(dd, a_, shw_sb[:, 3 * o + 2:3 * o + 3], dd,
                                        ALU.mult, ALU.add), inc=sem)
            w.op(eng, lambda b, o=oc:
                 b.tensor_copy(up_sb[:, 514 * o:514 * o + 2],
                               up_sb[:, 514 * o + 512:514 * o + 514]), inc=sem)
            pa[f"conv_{oc}"] = (sem, w.cnt[sem])
        w.op("vector", lambda b: b.tensor_mul(uc2[:, 512:1024], cscr[:, :],
                                              uc2[:, 512:1024]),
             waits=[("ve", w.cnt["ve"]), pa["conv_1"]], inc="ve")
        ve_vx = w.cnt["ve"]
        dma_st(vx_loc[:, 512 * t:512 * t + 512], uc2[:, 512:1024], waits=[("ve", ve_vx)])
        dma_st(x0_loc[:, 512 * t:512 * t + 512], uc2[:, 0:512], waits=[("ve", ve_vx)])
        pa["spill_st"] = w.cnt["st"]
    fillA = [(lambda tt: (lambda: phaseA_tile(tt)))(t) for t in range(32)]
    run_expert(1, fillA, 2)
    flush(fillA)
    st_phA = w.cnt["st"]
    ctxA.close()

    ctxB = ExitStack()
    EB = ctxB.enter_context
    vx_st = [EB(nc.sbuf_tensor(f"vx_st{i}", [128, 512], BF16)) for i in range(2)]
    k_st = [EB(nc.sbuf_tensor(f"k_st{i}", [128, 512], BF16)) for i in range(2)]
    ap_re = EB(nc.sbuf_tensor("ap_re", [128, 512], BF16))
    ap_im = EB(nc.sbuf_tensor("ap_im", [128, 512], BF16))
    xk_re = EB(nc.sbuf_tensor("xk_re", [128, 512], F32))
    xk_im = EB(nc.sbuf_tensor("xk_im", [128, 512], F32))
    w_re = EB(nc.sbuf_tensor("w_re", [128, 512], BF16))
    w_im = EB(nc.sbuf_tensor("w_im", [128, 512], BF16))
    gp_re = EB(nc.sbuf_tensor("gp_re", [128, 512], BF16))
    gp_im = EB(nc.sbuf_tensor("gp_im", [128, 512], BF16))
    y_sb = EB(nc.sbuf_tensor("y_sb", [64, 2 * 512], F32))

    fft = {
        "s1_ve0": w.cnt["ve"], "s1_ve1": w.cnt["ve"],
        "x_ve0": w.cnt["ve"], "x_ve1": w.cnt["ve"], "s1_ve": w.cnt["ve"],
        "x_ve": w.cnt["ve"], "y_se": w.cnt["se"],
        "ap_pe": w.cnt["pe"], "gp_pe": w.cnt["pe"], "w_pe": w.cnt["pe"],
        "st_pe0": w.cnt["pe"], "st_pe1": w.cnt["pe"],
        "prod_ve": w.cnt["ve"], "y_st": w.cnt["st"], "xk_ve": w.cnt["ve"],
        "ent_se": w.cnt["se"], "y_se0": w.cnt["se"], "y_se1": w.cnt["se"],
    }
    w.op("gpsimd", lambda b: b.memset(k_st[0][64:128, :], 0.0),
         waits=[("st", w.cnt["st"]), ("ve", w.cnt["ve"])], inc="po")
    w.op("gpsimd", lambda b: b.memset(k_st[1][64:128, :], 0.0), inc="po")
    po_kz = w.cnt["po"]

    tre_b = t_re[:, :].unsqueeze(1).broadcast_to((128, 4, 128))
    tim_b = t_im[:, :].unsqueeze(1).broadcast_to((128, 4, 128))

    def v3(ap):
        return ap.rearrange("p (c jf) -> p c jf", jf=128)

    def twiddle(dst_re, dst_im, src_re, src_im, conj, wait_pe, extra=()):
        sc1 = tw_s1[:, 0:512]
        sc2 = tw_s1[:, 512:1024]
        w.op("vector", lambda b: b.tensor_mul(v3(sc1), v3(src_re), tre_b),
             waits=[("pe", wait_pe)] + list(extra), inc="ve")
        w.op("vector", lambda b: b.tensor_mul(v3(sc2), v3(src_im), tim_b),
             waits=[("ve", w.cnt["ve"])], inc="ve")
        if conj:
            w.op("vector", lambda b: b.tensor_add(dst_re, sc1, sc2),
                 waits=[("ve", w.cnt["ve"])], inc="ve")
        else:
            w.op("vector", lambda b: b.tensor_sub(dst_re, sc1, sc2),
                 waits=[("ve", w.cnt["ve"])], inc="ve")
        w.op("vector", lambda b: b.tensor_mul(v3(sc1), v3(src_re), tim_b),
             waits=[("ve", w.cnt["ve"])], inc="ve")
        w.op("vector", lambda b: b.tensor_mul(v3(sc2), v3(src_im), tre_b),
             waits=[("ve", w.cnt["ve"])], inc="ve")
        if conj:
            w.op("vector", lambda b: b.tensor_sub(dst_im, sc2, sc1),
                 waits=[("ve", w.cnt["ve"])], inc="ve")
        else:
            w.op("vector", lambda b: b.tensor_add(dst_im, sc1, sc2),
                 waits=[("ve", w.cnt["ve"])], inc="ve")
        return w.cnt["ve"]

    def fft_group(g):
        c0 = 4 * g
        par = g % 2
        s1r, s1i = Ebk, Fbk
        sXr, sXi = G, H
        vst, kst = vx_st[par], k_st[par]
        dma_cast(vst[0:64, :].rearrange("js (c jf) -> js c jf", jf=128),
                 vx_loc[c0:c0 + 4, 0:LSEQ].rearrange("c (js jf) -> js c jf", jf=128),
                 waits=[("st", fft["phA_st"]), ("pe", fft[f"st_pe{par}"])])
        dma_cast(vst[64:128, :].rearrange("js (c jf) -> js c jf", jf=128),
                 vx_loc[c0:c0 + 4, LSEQ:NTOK].rearrange("c (js jf) -> js c jf", jf=128))
        dma_cast(kst[0:64, :].rearrange("js (c jf) -> js c jf", jf=128),
                 k_loc[c0:c0 + 4, :].rearrange("c (js jf) -> js c jf", jf=128),
                 waits=[("st", st_filter), ("po", po_kz)])
        cst_g = w.cnt["cst"]
        pe_s1 = None
        for c in range(4):
            pe_s1 = w.op("tensor", lambda b, c_=c, bb=s1r, v=vst:
                         b.matmul(bb[:, 128 * c_:128 * c_ + 128],
                                  v[:, 128 * c_:128 * c_ + 128], stk_re[:, :],
                                  start=True, stop=True),
                         waits=([("cst", cst_g), ("st", st_stk),
                                 ("ve", fft["s1_ve"]), ("se", fft["ent_se"])]
                                if c == 0 else []), inc="pe")
            pe_s1 = w.op("tensor", lambda b, c_=c, bb=s1i, v=vst:
                         b.matmul(bb[:, 128 * c_:128 * c_ + 128],
                                  v[:, 128 * c_:128 * c_ + 128], stk_im[:, :],
                                  start=True, stop=True), inc="pe")
        ve_tw1 = twiddle(ap_re[:, :], ap_im[:, :], s1r[:, 0:512], s1i[:, 0:512],
                         conj=False, wait_pe=pe_s1, extra=[("pe", fft["ap_pe"])])
        fft["s1_ve"] = ve_tw1
        w.op("tensor", lambda b, bb=sXr: b.matmul(bb[:, 0:512], f1re[:, :], ap_re[:, :],
                                                  start=True, stop=False),
             waits=[("ve", ve_tw1), ("ve", fft["x_ve"]),
                    ("se", fft["y_se"])], inc="pe")
        w.op("tensor", lambda b, bb=sXr: b.matmul(bb[:, 0:512], nf1im[:, :], ap_im[:, :],
                                                  start=False, stop=True), inc="pe")
        w.op("tensor", lambda b, bb=sXi: b.matmul(bb[:, 0:512], f1im[:, :], ap_re[:, :],
                                                  start=True, stop=False), inc="pe")
        pe_x = w.op("tensor", lambda b, bb=sXi: b.matmul(bb[:, 0:512], f1re[:, :],
                                                         ap_im[:, :],
                                                         start=False, stop=True), inc="pe")
        fft["ap_pe"] = w.cnt["pe"]
        pe_k1 = None
        for c in range(4):
            pe_k1 = w.op("tensor", lambda b, c_=c, bb=s1r, v=kst:
                         b.matmul(bb[:, 128 * c_:128 * c_ + 128],
                                  v[:, 128 * c_:128 * c_ + 128], stk_re[:, :],
                                  start=True, stop=True),
                         waits=([("ve", ve_tw1)] if c == 0 else []), inc="pe")
            pe_k1 = w.op("tensor", lambda b, c_=c, bb=s1i, v=kst:
                         b.matmul(bb[:, 128 * c_:128 * c_ + 128],
                                  v[:, 128 * c_:128 * c_ + 128], stk_im[:, :],
                                  start=True, stop=True), inc="pe")
        ve_tw2 = twiddle(ap_re[:, :], ap_im[:, :], s1r[:, 0:512], s1i[:, 0:512],
                         conj=False, wait_pe=pe_k1, extra=[("pe", fft["ap_pe"])])
        fft["s1_ve"] = ve_tw2
        w.op("tensor", lambda b, bb=s1r: b.matmul(bb[:, 0:512], f1re[:, :], ap_re[:, :],
                                                  start=True, stop=False),
             waits=[("ve", ve_tw2)], inc="pe")
        w.op("tensor", lambda b, bb=s1r: b.matmul(bb[:, 0:512], nf1im[:, :], ap_im[:, :],
                                                  start=False, stop=True), inc="pe")
        w.op("tensor", lambda b, bb=s1i: b.matmul(bb[:, 0:512], f1im[:, :], ap_re[:, :],
                                                  start=True, stop=False), inc="pe")
        pe_k2 = w.op("tensor", lambda b, bb=s1i: b.matmul(bb[:, 0:512], f1re[:, :],
                                                          ap_im[:, :],
                                                          start=False, stop=True),
                     inc="pe")
        fft["ap_pe"] = w.cnt["pe"]
        w.op("scalar", lambda b, bb=s1r: b.copy(xk_re[:, :], bb[:, 0:512]),
             waits=[("pe", pe_k2), ("ve", fft["xk_ve"])], inc="se")
        w.op("scalar", lambda b, bb=s1i: b.copy(xk_im[:, :], bb[:, 0:512]),
             waits=[("pe", pe_k2)], inc="se")
        se_xk = w.cnt["se"]
        sc1 = tw_s1[:, 0:512]
        sc2 = tw_s1[:, 512:1024]
        w.op("vector", lambda b, bb=sXr: b.tensor_mul(sc1, bb[:, 0:512], xk_re[:, :]),
             waits=[("pe", pe_x), ("se", se_xk), ("pe", fft["w_pe"])], inc="ve")
        w.op("vector", lambda b, bb=sXi: b.tensor_mul(sc2, bb[:, 0:512], xk_im[:, :]),
             waits=[("ve", w.cnt["ve"])], inc="ve")
        w.op("vector", lambda b: b.tensor_sub(w_re[:, :], sc1, sc2),
             waits=[("ve", w.cnt["ve"])], inc="ve")
        w.op("vector", lambda b, bb=sXr: b.tensor_mul(sc1, bb[:, 0:512], xk_im[:, :]),
             waits=[("ve", w.cnt["ve"])], inc="ve")
        w.op("vector", lambda b, bb=sXi: b.tensor_mul(sc2, bb[:, 0:512], xk_re[:, :]),
             waits=[("ve", w.cnt["ve"])], inc="ve")
        w.op("vector", lambda b: b.tensor_add(w_im[:, :], sc1, sc2),
             waits=[("ve", w.cnt["ve"])], inc="ve")
        ve_prod = w.cnt["ve"]
        fft["prod_ve"] = ve_prod
        fft["xk_ve"] = ve_prod
        fft["x_ve"] = ve_prod
        pe_i1 = None
        for c in range(4):
            pe_i1 = w.op("tensor", lambda b, c_=c, bb=s1r:
                         b.matmul(bb[:, 128 * c_:128 * c_ + 128],
                                  w_re[:, 128 * c_:128 * c_ + 128], f1re[:, :],
                                  start=True, stop=False),
                         waits=([("ve", ve_prod), ("se", se_xk)] if c == 0 else []),
                         inc="pe")
            pe_i1 = w.op("tensor", lambda b, c_=c, bb=s1r:
                         b.matmul(bb[:, 128 * c_:128 * c_ + 128],
                                  w_im[:, 128 * c_:128 * c_ + 128], f1im[:, :],
                                  start=False, stop=True), inc="pe")
            pe_i1 = w.op("tensor", lambda b, c_=c, bb=s1i:
                         b.matmul(bb[:, 128 * c_:128 * c_ + 128],
                                  w_re[:, 128 * c_:128 * c_ + 128], nf1im[:, :],
                                  start=True, stop=False), inc="pe")
            pe_i1 = w.op("tensor", lambda b, c_=c, bb=s1i:
                         b.matmul(bb[:, 128 * c_:128 * c_ + 128],
                                  w_im[:, 128 * c_:128 * c_ + 128], f1re[:, :],
                                  start=False, stop=True), inc="pe")
        fft["w_pe"] = w.cnt["pe"]
        ve_tw3 = twiddle(gp_re[:, :], gp_im[:, :], s1r[:, 0:512], s1i[:, 0:512],
                         conj=True, wait_pe=pe_i1, extra=[("pe", fft["gp_pe"])])
        fft["s1_ve"] = ve_tw3
        w.op("tensor", lambda b, bb=sXr: b.matmul(bb[0:64, 0:512], fi_re[:, :],
                                                  gp_re[:, :], start=True, stop=False),
             waits=[("ve", ve_tw3)], inc="pe")
        w.op("tensor", lambda b, bb=sXr: b.matmul(bb[0:64, 0:512], fi_ip[:, :],
                                                  gp_im[:, :], start=False, stop=True),
             inc="pe")
        w.op("tensor", lambda b, bb=sXi: b.matmul(bb[0:64, 0:512], fi_re[:, :],
                                                  gp_im[:, :], start=True, stop=False),
             inc="pe")
        pe_y = w.op("tensor", lambda b, bb=sXi: b.matmul(bb[0:64, 0:512], fi_in[:, :],
                                                         gp_re[:, :],
                                                         start=False, stop=True), inc="pe")
        fft["gp_pe"] = w.cnt["pe"]
        w.op("scalar", lambda b, bb=sXr: b.copy(y_sb[:, 0:512], bb[0:64, 0:512]),
             waits=[("pe", pe_y), ("st", fft["y_st"])], inc="se")
        w.op("scalar", lambda b, bb=sXi: b.copy(y_sb[:, 512:1024], bb[0:64, 0:512]),
             waits=[("pe", pe_y)], inc="se")
        se_y = w.cnt["se"]
        fft["y_se"] = se_y
        dma_st(y_loc[c0:c0 + 4, 0:LSEQ].rearrange("c (js jf) -> js c jf", jf=128),
               y_sb[:, 0:512].rearrange("js (c jf) -> js c jf", jf=128),
               waits=[("se", se_y)])
        dma_st(y_loc[c0:c0 + 4, LSEQ:NTOK].rearrange("c (js jf) -> js c jf", jf=128),
               y_sb[:, 512:1024].rearrange("js (c jf) -> js c jf", jf=128))
        fft["y_st"] = w.cnt["st"]
        fft[f"st_pe{par}"] = w.cnt["pe"]
    fft["phA_st"] = st_phA
    fillB = [(lambda gg: (lambda: fft_group(gg)))(g) for g in range(32)]
    run_expert(2, fillB, 1)
    run_expert(3, fillB, 1)
    flush(fillB)
    st_fft = w.cnt["st"]
    ctxB.close()

    ctxC = ExitStack()
    EC = ctxC.enter_context
    outw_sb = EC(nc.sbuf_tensor("outw_sb", [128, D], BF16))
    yc_sb = [EC(nc.sbuf_tensor(f"yc{i}", [128, 512], F32)) for i in range(2)]
    vxc_sb = [EC(nc.sbuf_tensor(f"vxc{i}", [128, 512], F32)) for i in range(2)]
    x0c_sb = [EC(nc.sbuf_tensor(f"x0c{i}", [128, 512], F32)) for i in range(2)]
    yv_sb = EC(nc.sbuf_tensor("yv_sb", [128, 512], BF16))
    pc_sb = [EC(nc.sbuf_tensor(f"pc{i}", [128, 512], F32)) for i in range(2)]
    at_sb = [EC(nc.sbuf_tensor(f"at_sb{i}", [128, 512], F32)) for i in range(2)]
    fo_sb = [EC(nc.sbuf_tensor(f"fo_sb{i}", [128, 512], F32)) for i in range(2)]

    dma_cast(outw_sb[:, :], outw_i, waits=[("st", w.cnt["st"])])
    cst_outw = w.cnt["cst"]
    pc = {"po_se0": w.cnt["se"], "po_se1": w.cnt["se"], "yv_pe": w.cnt["pe"],
          "v_ve": w.cnt["ve"]}
    def pc_tile(t):
        par = t % 2
        yc, vxc, x0c = yc_sb[par], vxc_sb[par], x0c_sb[par]
        dma_sync(yc[:, :], y_loc[:, 512 * t:512 * t + 512],
                 waits=[("st", pc["fft_st"])] + ([("ve", pc.get(f"ld_ve{par}", 0))]
                                           if t >= 2 else []))
        dma_sync(vxc[:, :], vx_loc[:, 512 * t:512 * t + 512])
        dma_sync(x0c[:, :], x0_loc[:, 512 * t:512 * t + 512])
        ld_c = w.cnt["ld"]
        w.op("vector", lambda b, a_=vxc, c_=yc: b.scalar_tensor_tensor(
            tw_s1[:, 0:512], a_[:, :], fb_sb[:, 0:1], c_[:, :], ALU.mult, ALU.add),
            waits=[("ld", ld_c), ("ve", pc["v_ve"])], inc="ve")
        w.op("vector", lambda b, a_=x0c: b.tensor_mul(yv_sb[:, :], tw_s1[:, 0:512],
                                                      a_[:, :]),
             waits=[("ve", w.cnt["ve"]), ("pe", pc["yv_pe"])], inc="ve")
        ve_yv = w.cnt["ve"]
        pc[f"ld_ve{par}"] = ve_yv
        pc["v_ve"] = ve_yv
        blk = t // 4
        coln = 512 * (t % 4)
        for oc in range(8):
            bk = Ebk if oc % 2 == 0 else Fbk
            pcs = pc_sb[oc % 2]
            pe_o = w.op("tensor", lambda b, o=oc, bb=bk:
                        b.matmul(bb[:, 0:512], outw_sb[:, 128 * o:128 * o + 128],
                                 yv_sb[:, :], start=True, stop=True),
                        waits=[("ve", ve_yv), ("se", pc[f"po_se{oc % 2}"]),
                               ("cst", cst_outw)], inc="pe")
            w.op("scalar", lambda b, bb=bk, p_=pcs: b.copy(p_[:, :], bb[:, 0:512]),
                 waits=[("pe", pe_o), ("pst", pc.get(f"p_st{oc % 2}", 0))], inc="se")
            pc[f"po_se{oc % 2}"] = w.cnt["se"]
            w.op("gpsimd", lambda b, bl=blk, o=oc, cl=coln, p_=pcs: b.dma_start(
                out=p_buf[1024 * bl + 128 * o:1024 * bl + 128 * o + 128, cl:cl + 512],
                in_=p_[:, :]),
                waits=[("se", pc[f"po_se{oc % 2}"])], inc="pst", amt=16)
            pc[f"p_st{oc % 2}"] = w.cnt["pst"]
        pc["yv_pe"] = w.cnt["pe"]
    pc["fft_st"] = st_fft
    fillC = [(lambda tt: (lambda: pc_tile(tt)))(t) for t in range(32)]
    run_expert(4, fillC, 1)
    run_expert(5, fillC, 1)
    flush(fillC)
    st_pc = w.cnt["pst"]

    w.op("gpsimd", lambda b: b.collective_compute(
        "ReduceScatter", ALU.add, replica_groups=[list(range(NCORES))],
        ins=[p_buf.opt()], outs=[attn_rs.opt()]),
        waits=[("pst", st_pc)], inc="rs")

    for e in range(6, NEXP):
        run_expert(e, [], 0)
    if "prev_pend" in ffn:
        emit_down(*ffn.pop("prev_pend"))

    fin = {"st0": 0, "st1": 0, "ve0": w.cnt["ve"], "ve1": w.cnt["ve"]}
    for i4 in range(32):
        hc, s4 = divmod(i4, 4)
        par = i4 % 2
        col = 512 * s4
        lda = dma_sync(at_sb[par][:, :],
                       attn_rs[128 * hc:128 * (hc + 1), col:col + 512],
                       waits=[("rs", 1), ("ve", fin[f"ve{par}"])])
        w.op("vector", lambda b, c_=hc, cl=col, p_=par: b.scalar_tensor_tensor(
            fo_sb[p_][:, :], at_sb[p_][:, :], outb_sb[:, c_:c_ + 1],
            acc[:, 2048 * c_ + cl:2048 * c_ + cl + 512], ALU.add, ALU.add),
            waits=[("ld", lda), ("st", fin[f"st{par}"])], inc="ve")
        fin[f"ve{par}"] = w.cnt["ve"]
        dma_st(out_fm[128 * hc:128 * (hc + 1), col:col + 512], fo_sb[par][:, :],
               waits=[("ve", fin[f"ve{par}"])])
        fin[f"st{par}"] = w.cnt["st"]
    if DEBUG_DUMPS:
        dbg_acc = nc.dram_tensor("dbg_acc", [128, 8 * TPC], F32, kind="ExternalOutput").ap()
        dma_st(dbg_acc, acc[:, :], waits=[("st", w.cnt["st"])])
        for nm, src_ap in (("dbg_vx", vx_loc), ("dbg_x0", x0_loc), ("dbg_y", y_loc),
                           ("dbg_k", k_loc), ("dbg_at", attn_rs)):
            dout = nc.dram_tensor(nm, list(src_ap.shape), F32, kind="ExternalOutput").ap()
            dma_st(dout, src_ap, waits=[("st", w.cnt["st"])])
    w.wait("sync", "st", w.cnt["st"])
    w.wait("gpsimd", "st", w.cnt["st"])
    ctxC.close()

    with nc.Block() as block:
        w.emit(block)
    ctx.close()
    return nc



def _prep_inputs(inp):
    f32 = np.float32
    x = np.asarray(inp["L0_attn_u"], f32).reshape(NTOK, D)
    in_w = np.asarray(inp["L0_attn_in_w"], f32)
    in_b = np.asarray(inp["L0_attn_in_b"], f32)
    short_w = np.asarray(inp["L0_attn_short_w"], f32)
    short_b = np.asarray(inp["L0_attn_short_b"], f32)
    mlp_w1 = np.asarray(inp["L0_attn_mlp_w1"], f32)
    mlp_b1 = np.asarray(inp["L0_attn_mlp_b1"], f32)
    mlp_w2 = np.asarray(inp["L0_attn_mlp_w2"], f32)
    mlp_b2 = np.asarray(inp["L0_attn_mlp_b2"], f32)
    mlp_w3 = np.asarray(inp["L0_attn_mlp_w3"], f32)
    mlp_b3 = np.asarray(inp["L0_attn_mlp_b3"], f32)
    mlp_wo = np.asarray(inp["L0_attn_mlp_wo"], f32)
    freq = np.asarray(inp["L0_attn_freq"], f32)
    filt_bias = np.asarray(inp["L0_attn_filt_bias"], f32)
    out_w = np.asarray(inp["L0_attn_out_w"], f32)
    out_b = np.asarray(inp["L0_attn_out_b"], f32)
    wg_s = np.asarray(inp["L0_ffn_Wg_s"], f32)
    wu_s = np.asarray(inp["L0_ffn_Wu_s"], f32)
    wd_s = np.asarray(inp["L0_ffn_Wd_s"], f32)
    wg = np.asarray(inp["L0_ffn_Wg"], f32)
    wu = np.asarray(inp["L0_ffn_Wu"], f32)
    wd = np.asarray(inp["L0_ffn_Wd"], f32)
    wr = np.asarray(inp["L0_ffn_Wr"], f32)
    rb = np.asarray(inp["L0_ffn_rb"], f32)

    A = np.ascontiguousarray
    w1p = np.zeros((4, FILT), f32)
    w1p[0:3] = mlp_w1
    sel = np.zeros((8, 1024), f32)
    for e in range(8):
        sel[e, 128 * e:128 * (e + 1)] = 1.0
    shared = {
        "w1": w1p, "b1": A(mlp_b1.reshape(FILT, 1)),
        "w2": A(mlp_w2), "b2": A(mlp_b2.reshape(FILT, 1)),
        "w3": A(mlp_w3), "b3": A(mlp_b3.reshape(FILT, 1)),
        "freq_c": A(freq.reshape(1, FILT).T),
        "outb": A(out_b.reshape(8, 128).T),
        "wr_t": A(wr.T), "rb": A(rb.reshape(8, 1)), "sel8": sel,
        "ident32": np.eye(128, dtype=f32),
    }
    g9 = np.concatenate([wg_s[None], wg], 0)
    u9 = np.concatenate([wu_s[None], wu], 0)
    d9 = np.concatenate([wd_s[None], wd], 0)
    gq = g9.reshape(9, 4, 4, 128, 8, 128).transpose(0, 1, 5, 4, 2, 3).reshape(9, 4, 128, 4096)
    uq = u9.reshape(9, 4, 4, 128, 8, 128).transpose(0, 1, 5, 4, 2, 3).reshape(9, 4, 128, 4096)
    dq = d9.reshape(9, 1024, 4, 4, 128).transpose(0, 2, 4, 3, 1).reshape(9, 4, 128, 4096)
    shared["wfq"] = A(np.concatenate([gq, uq, dq], axis=3))
    maps = []
    for p in range(NCORES):
        sl = slice(128 * p, 128 * (p + 1))
        cols = np.r_[np.arange(128 * p, 128 * (p + 1)),
                     1024 + np.arange(128 * p, 128 * (p + 1)),
                     2048 + np.arange(128 * p, 128 * (p + 1))]
        m = dict(shared)
        m["x_sl"] = A(x[TPC * p:TPC * (p + 1)])
        m["inw_sl"] = A(in_w[:, cols])
        m["inb_sl"] = A(in_b[cols].reshape(3, 128).T)
        m["shw_sl"] = A(short_w[cols].reshape(3, 128, 3).transpose(1, 0, 2).reshape(128, 9))
        m["shb_sl"] = A(short_b[cols].reshape(3, 128).T)
        m["wo_sl"] = A(mlp_wo[:, sl])
        m["fb_sl"] = A(filt_bias[sl].reshape(128, 1))
        m["outw_sl"] = A(out_w[sl, :])
        m["chan_row"] = A(np.arange(128 * p, 128 * (p + 1), dtype=f32).reshape(1, 128))
        maps.append(m)
    return maps


def kernel(**inputs):
    nc = build_program()
    in_maps = _prep_inputs(inputs)
    res = run_bass_kernel_spmd(nc, in_maps, core_ids=list(range(NCORES)))
    parts = [np.asarray(r["out_fm"], dtype=np.float32).T for r in res.results]
    out = np.concatenate(parts, axis=0)
    return np.ascontiguousarray(out.reshape(2, LSEQ, D))
